# version 01
# baseline (speedup 1.0000x reference)
import os
import numpy as np
import concourse.bass as bass
import concourse.mybir as mybir
from concourse.bass_utils import run_bass_kernel_spmd

F32 = mybir.dt.float32
BF16 = mybir.dt.bfloat16
ALU = mybir.AluOpType
AF = mybir.ActivationFunctionType
AX = mybir.AxisListType

H, DK, DV, GG, NG = 16, 128, 128, 16, 8
D = 2048
SCALE = DK ** -0.5
C = 128
TLOC = 2048
HALO = 128
NCH = TLOC // C
NBLK = 4
BT = 512
FS, FR, NE = 2816, 1408, 8
BIG = 1e30
EHALF = 512

ENG = ("sync", "gpsimd", "tensor", "vector", "scalar")


class S:
    __slots__ = ("h", "cnt", "name")

    def __init__(self, h, name):
        self.h, self.cnt, self.name = h, 0, name


class Bld:
    def __init__(self, nc, stack):
        self.nc, self.stack = nc, stack
        self.q = {e: [] for e in ENG}
        self.pe = self.sem("pe")
        self.ve = self.sem("ve")
        self.ac = self.sem("ac")
        self.cc = self.sem("cc")
        self.Q = {}

    def sem(self, name):
        return S(self.stack.enter_context(self.nc.semaphore(name)), name)

    def P(self, s):
        return (s, s.cnt)

    def emit(self, eng, make, waits=(), inc=None, d=1):
        waits = tuple((s, v) for (s, v) in waits if s is not None and v > 0)
        for s, v in waits:
            assert v <= s.cnt, f"wait on future {s.name} {v}>{s.cnt}"
        tok = None
        if inc is not None:
            inc.cnt += d
            tok = (inc, inc.cnt)

        def f(e, waits=waits, inc=inc, d=d, make=make):
            for s, v in waits:
                e.wait_ge(s.h, v)
            r = make(e)
            if inc is not None:
                r.then_inc(inc.h, d)
        self.q[eng].append(f)
        return tok

    def dma(self, eng, out, in_, stream="ld", waits=()):
        if stream not in self.Q:
            self.Q[stream] = self.sem("q_" + stream)
        s = self.Q[stream]
        return self.emit(eng, lambda e, o=out, i=in_: e.dma_start(out=o, in_=i),
                         waits=waits, inc=s, d=16)

    def pe_mm(self, out, lhsT, rhs, start, stop, waits=(), inc=False):
        return self.emit(
            "tensor",
            lambda e, o=out, l=lhsT, r=rhs, st=start, sp=stop: e.matmul(
                out=o, lhsT=l, rhs=r, start=st, stop=sp),
            waits=waits, inc=self.pe if inc else None)

    def pe_tr(self, out, in_, ident, waits=()):
        return self.emit(
            "tensor",
            lambda e, o=out, i=in_, d=ident: e.transpose(out=o, in_=i, identity=d),
            waits=waits, inc=self.pe)

    def mm_group(self, parts, waits=()):
        n = len(parts)
        tok = None
        for i, (o, l, r, st, sp) in enumerate(parts):
            w = waits if i == 0 else ()
            tok = self.pe_mm(o, l, r, st, sp, waits=w, inc=(i == n - 1))
        return tok

    def v(self, make, waits=()):
        tok = self.emit("vector", make, waits=waits, inc=self.ve)
        self.q["vector"].append(lambda e: e.drain())
        return tok

    def a(self, make, waits=()):
        tok = self.emit("scalar", make, waits=waits, inc=self.ac)
        self.q["scalar"].append(lambda e: e.drain())
        return tok

    def split(self, src, hi, lo, waits=()):
        self.v(lambda e, s=src, h=hi: e.tensor_copy(out=h, in_=s), waits=waits)
        return self.v(lambda e, s=src, h=hi, l=lo: e.tensor_sub(out=l, in0=s, in1=h))


def host_constants():
    cs = {}
    cs["cident"] = np.eye(128, dtype=np.float32)
    cs["cuincl"] = np.triu(np.ones((C, C), np.float32))
    e1 = np.zeros((NG, DK), np.float32)
    for n in range(NG):
        e1[n, n * GG:(n + 1) * GG] = 1
    cs["ce1"] = e1
    e2 = np.zeros((H, DK), np.float32)
    for h in range(H):
        e2[h, h * NG:(h + 1) * NG] = 1
    cs["ce2"] = e2
    cs["conesr"] = np.ones((1, 128), np.float32)
    cs["conesc"] = np.ones((128, 1), np.float32)
    ii = np.arange(C)[:, None]
    jj = np.arange(C)[None, :]
    mA = np.where(ii > jj, 0.0, BIG).astype(np.float32)
    mM = np.where(jj >= ii, 0.0, -BIG).astype(np.float32)
    cs["cmaskA"] = np.tile(mA, (1, NG))
    cs["cmaskM"] = np.tile(mM, (1, NG))
    return cs


def build_program(dbg=False, upto="E", heads=None, blocks=None):
    import contextlib
    nc = bass.Bass("TRN2", target_bir_lowering=False)
    stack = contextlib.ExitStack()
    b = Bld(nc, stack)
    PH = ("A", "B", "C", "D", "E")
    run = {p: PH.index(p) <= PH.index(upto) for p in PH}

    def dt(name, shape, kind="ExternalInput", dtype=F32):
        return nc.dram_tensor(name, shape, dtype, kind=kind).ap()

    xs = dt("xs", [HALO + TLOC, D])
    wq, wk, wv, wg, wo = (dt(n, [D, D]) for n in ("wq", "wk", "wv", "wg", "wo"))
    wf1 = dt("wf1", [D, 128])
    wf2 = dt("wf2", [128, 128])
    wb = dt("wb", [D, 16])
    convw = dt("convw", [128, 192])
    afm = dt("afm", [16, 128])
    dtb = dt("dtb", [128, 128])
    nwbc = dt("nwbc", [128, 128])
    bgc = dt("bgc", [128, 16])
    gatew = dt("gatew", [D, 8])
    sw1 = dt("sw1", [D, FS])
    sw2 = dt("sw2", [FS, D])
    sw3 = dt("sw3", [D, FS])
    rw1 = dt("rw1", [NE * D, FR])
    rw2 = dt("rw2", [NE * FR, D])
    rw3 = dt("rw3", [NE * D, FR])
    cident = dt("cident", [128, 128])
    cuincl = dt("cuincl", [128, 128])
    ce1 = dt("ce1", [8, 128])
    ce2 = dt("ce2", [16, 128])
    conesr = dt("conesr", [1, 128])
    conesc = dt("conesc", [128, 1])
    cmaskA = dt("cmaskA", [128, NG * C])
    cmaskM = dt("cmaskM", [128, NG * C])
    cisev = dt("cisev", [128, 1])
    cisod = dt("cisod", [128, 1])
    cactf = dt("cactf", [128, 8])
    spk = "ExternalOutput" if dbg else "Internal"
    sp_mst = dt("sp_mst", [H * NCH * 128, 128], spk)
    sp_ns = dt("sp_ns", [H * NCH * 128, 128], spk)
    sp_q2 = dt("sp_q2", [H * NCH * 128, 128], spk)
    sp_yl = dt("sp_yl", [H * NCH * 128, 128], spk)
    sp_sg = dt("sp_sg", [H * 128, TLOC], spk)
    sp_x1 = dt("sp_x1", [TLOC, D], spk)
    cc_in = nc.dram_tensor("cc_in", [H * 128, 128], F32)
    cc_out = nc.dram_tensor("cc_out", [H * 128, 128], F32)
    dbg_y = dt("dbg_y", [H * 128, TLOC], "ExternalOutput", BF16) if dbg else None
    dbg_send = dt("dbg_send", [H * 128, 128], "ExternalOutput") if dbg else None
    dbg_qkv = (dt("dbg_qkv", [3 * H * 128, TLOC], "ExternalOutput")
               if dbg and os.environ.get("K_DBG_QKV") else None)
    dbg_ck = (dt("dbg_ck", [H * NCH * 128, 4352], "ExternalOutput")
              if dbg and os.environ.get("K_DBG_CHUNK") else None)
    dbg_pre = (dt("dbg_pre", [3 * H * 128, 515 * NBLK], "ExternalOutput")
               if dbg and os.environ.get("K_DBG_PRE") else None)
    out = dt("out", [TLOC, D], kind="ExternalOutput")

    E = stack.enter_context
    import contextlib as _ctl
    stk0 = _ctl.ExitStack()
    stkA = _ctl.ExitStack()
    stkCD = _ctl.ExitStack()
    stkE = _ctl.ExitStack()
    E0 = stk0.enter_context
    EA = stkA.enter_context
    EC = stkCD.enter_context
    EE = stkE.enter_context

    pj = E(nc.psum_tensor("pj", [128, 512], F32))
    pj2 = E(nc.psum_tensor("pj2", [128, 512], F32))
    pg = E(nc.psum_tensor("pg", [128, 512], F32))
    pr = E(nc.psum_tensor("pr", [128, 1024], F32))
    pt = E(nc.psum_tensor("pt", [128, 512], F32))
    px = E(nc.psum_tensor("px", [128, 512], F32))
    ps = E(nc.psum_tensor("ps", [128, 512], F32))

    cs_id = E(nc.sbuf_tensor("cs_id", [128, 128], F32))
    cs_ui = E(nc.sbuf_tensor("cs_ui", [128, 128], F32))
    cs_e1 = E(nc.sbuf_tensor("cs_e1", [8, 128], F32))
    cs_e2 = E(nc.sbuf_tensor("cs_e2", [16, 128], F32))
    cs_or = E(nc.sbuf_tensor("cs_or", [1, 128], F32))
    cs_oc = E(nc.sbuf_tensor("cs_oc", [128, 1], F32))
    cs_nw = E(nc.sbuf_tensor("cs_nw", [128, 128], F32))
    cs_dtb = E(nc.sbuf_tensor("cs_dtb", [128, 128], F32))
    cs_bg = E(nc.sbuf_tensor("cs_bg", [128, 16], F32))
    cs_cw = E(nc.sbuf_tensor("cs_cw", [128, 192], F32))
    cs_afm = E(nc.sbuf_tensor("cs_afm", [16, 128], F32))
    cs_ascl = E(nc.sbuf_tensor("cs_ascl", [128, 128], F32))
    cs_iev = E(nc.sbuf_tensor("cs_iev", [128, 1], F32))
    cs_iod = E(nc.sbuf_tensor("cs_iod", [128, 1], F32))
    cs_af = E(nc.sbuf_tensor("cs_af", [128, 8], F32))
    wbh = E(nc.sbuf_tensor("wbh", [128, 256], BF16))
    wbl = E(nc.sbuf_tensor("wbl", [128, 256], BF16))
    wf1h = E(nc.sbuf_tensor("wf1h", [128, D], BF16))
    wf1l = E(nc.sbuf_tensor("wf1l", [128, D], BF16))
    cs_wf2 = E(nc.sbuf_tensor("cs_wf2", [128, 128], F32))
    gwh = E(nc.sbuf_tensor("gwh", [128, 128], BF16))
    gwl = E(nc.sbuf_tensor("gwl", [128, 128], BF16))
    c_t = E(nc.sbuf_tensor("c_t", [128, 32], F32))

    tmpw = E0(nc.sbuf_tensor("tmpw", [128, D + 256], F32))
    for dst, src in ((cs_id, cident), (cs_ui, cuincl), (cs_e1, ce1),
                     (cs_e2, ce2), (cs_or, conesr), (cs_oc, conesc),
                     (cs_nw, nwbc), (cs_dtb, dtb), (cs_bg, bgc),
                     (cs_cw, convw), (cs_afm, afm), (cs_iev, cisev),
                     (cs_iod, cisod), (cs_wf2, wf2), (cs_af, cactf)):
        b.dma("sync", dst[:, :], src[:, :])
    t_ld0 = b.P(b.Q["ld"])
    for k in range(16):
        b.dma("sync", tmpw[:, k * 16:(k + 1) * 16], wb[k * 128:(k + 1) * 128, :],
              stream="ldb")
    t_wb = b.P(b.Q["ldb"])
    b.split(tmpw[:, 0:256], wbh[:, :], wbl[:, :], waits=[t_wb])
    for k in range(16):
        b.dma("sync", tmpw[:, 256 + k * 128:256 + (k + 1) * 128],
              wf1[k * 128:(k + 1) * 128, :], stream="ldf")
    t_wf1 = b.P(b.Q["ldf"])
    t_splits = b.split(tmpw[:, 256:256 + D], wf1h[:, :], wf1l[:, :], waits=[t_wf1])
    gwtmp = E0(nc.sbuf_tensor("gwtmp", [128, 128], F32))
    for k in range(16):
        b.dma("sync", gwtmp[:, k * 8:(k + 1) * 8],
              gatew[k * 128:(k + 1) * 128, :], stream="ldg")
    t_gw = b.P(b.Q["ldg"])
    t_gws = b.split(gwtmp[:, :], gwh[:, :], gwl[:, :], waits=[t_gw])
    t_e = b.a(lambda e: e.activation(out=cs_afm[:, :], in_=cs_afm[:, :],
                                     func=AF.Exp), waits=[t_ld0])
    t_n = b.v(lambda e: e.tensor_scalar(out=cs_afm[:, :], in0=cs_afm[:, :],
                                        scalar1=-1.0, scalar2=None, op0=ALU.mult),
              waits=[t_e])
    t_m = b.pe_mm(ps[:, 0:128], cs_afm[:, :], cs_e2[:, :], True, True,
                  waits=[t_n, t_ld0], inc=True)
    t_consts = b.v(lambda e: e.tensor_copy(out=cs_ascl[:, :], in_=ps[:, 0:128]),
                   waits=[t_m])

    stk0.close()

    csbA = EA(nc.sbuf_tensor("csbA", [128, NG * C], F32))
    csbM = EA(nc.sbuf_tensor("csbM", [128, NG * C], F32))
    b.dma("sync", csbA[:, :], cmaskA[:, :], stream="ldm",
          waits=[t_splits, t_gws, t_consts])
    t_mask = b.dma("sync", csbM[:, :], cmaskM[:, :], stream="ldm")

    xlt_h = EA(nc.sbuf_tensor("xlt_h", [128, 16 * 640], BF16))
    xlt_l = EA(nc.sbuf_tensor("xlt_l", [128, 16 * 640], BF16))
    a_xt = EA(nc.sbuf_tensor("a_xt", [128, D], F32))
    a_tmp = EA(nc.sbuf_tensor("a_tmp", [128, D], F32))
    a_pre = EA(nc.sbuf_tensor("a_pre", [128, 515], F32))
    a_qf = EA(nc.sbuf_tensor("a_qf", [128, BT], F32))
    a_kf = EA(nc.sbuf_tensor("a_kf", [128, BT], F32))
    a_vf = EA(nc.sbuf_tensor("a_vf", [128, BT], F32))
    a_sq = EA(nc.sbuf_tensor("a_sq", [128, BT], F32))
    a_srow = EA(nc.sbuf_tensor("a_srow", [1, BT], F32))
    a_sg = EA(nc.sbuf_tensor("a_sg", [128, BT], F32))
    a_ghat = EA(nc.sbuf_tensor("a_ghat", [128, NBLK * 128], F32))
    a_beta = EA(nc.sbuf_tensor("a_beta", [128, NBLK * 16], F32))
    a_t1 = EA(nc.sbuf_tensor("a_t1", [128, BT], F32))
    a_wl = [EA(nc.sbuf_tensor(f"a_wl{i}", [128, D], F32)) for i in range(2)]
    a_wh = [EA(nc.sbuf_tensor(f"a_wh{i}", [128, D], BF16)) for i in range(2)]
    a_wlo = [EA(nc.sbuf_tensor(f"a_wlo{i}", [128, D], BF16)) for i in range(2)]
    a_kg = EA(nc.sbuf_tensor("a_kg", [16, NG * C], F32))
    a_qg = EA(nc.sbuf_tensor("a_qg", [16, NG * C], F32))
    c_gt = EA(nc.sbuf_tensor("c_gt", [128, 8], F32))
    c_gf = EA(nc.sbuf_tensor("c_gf", [8, 128], F32))
    c_gfl = EA(nc.sbuf_tensor("c_gfl", [1, NG * C], F32))
    c_ra = EA(nc.sbuf_tensor("c_ra", [128, NG * C], F32))
    c_rm = EA(nc.sbuf_tensor("c_rm", [128, NG * C], F32))
    c_A = EA(nc.sbuf_tensor("c_A", [128, 128], F32))
    c_At = EA(nc.sbuf_tensor("c_At", [128, 128], F32))
    c_Ap = EA(nc.sbuf_tensor("c_Ap", [128, 128], F32))
    c_ApT = EA(nc.sbuf_tensor("c_ApT", [128, 128], F32))
    c_Mt = EA(nc.sbuf_tensor("c_Mt", [128, 128], F32))
    c_eg = EA(nc.sbuf_tensor("c_eg", [128, 128], F32))
    c_kt = EA(nc.sbuf_tensor("c_kt", [128, 128], F32))
    c_qt = EA(nc.sbuf_tensor("c_qt", [128, 128], F32))
    c_kh = EA(nc.sbuf_tensor("c_kh", [128, 128], F32))
    c_khT = EA(nc.sbuf_tensor("c_khT", [128, 128], F32))
    c_X = EA(nc.sbuf_tensor("c_X", [128, 256], F32))
    c_r2 = EA(nc.sbuf_tensor("c_r2", [8, 128], F32))
    c_ms = EA(nc.sbuf_tensor("c_ms", [128, 128], F32))
    c_ns = EA(nc.sbuf_tensor("c_ns", [128, 128], F32))
    c_q2 = EA(nc.sbuf_tensor("c_q2", [128, 128], F32))
    c_yl = EA(nc.sbuf_tensor("c_yl", [128, 128], F32))
    c_tt = EA(nc.sbuf_tensor("c_tt", [128, 128], F32))

    def ln_transpose(src_rows, xh, xl, colw, dst_col0, deps):
        tok_x = b.dma("sync", a_xt[:, :], src_rows, stream="xt", waits=deps)
        t1 = b.v(lambda e: e.tensor_reduce(out=c_t[:, 0:1], in_=a_xt[:, :],
                                           axis=AX.X, op=ALU.add), waits=[tok_x])
        t2 = b.a(lambda e: e.square(out=a_tmp[:, :], in_=a_xt[:, :]),
                 waits=[tok_x, b.P(b.ve)])
        t3 = b.v(lambda e: e.tensor_reduce(out=c_t[:, 1:2], in_=a_tmp[:, :],
                                           axis=AX.X, op=ALU.add), waits=[t2])
        t4 = b.v(lambda e: e.tensor_scalar(out=c_t[:, 2:3], in0=c_t[:, 0:1],
                                           scalar1=1.0 / D, scalar2=None,
                                           op0=ALU.mult), waits=[t1])
        t5 = b.v(lambda e: e.tensor_tensor(out=c_t[:, 3:4], in0=c_t[:, 2:3],
                                           in1=c_t[:, 2:3], op=ALU.mult))
        t6 = b.v(lambda e: e.scalar_tensor_tensor(out=c_t[:, 4:5], in0=c_t[:, 1:2],
                                                  scalar=1.0 / D, in1=c_t[:, 3:4],
                                                  op0=ALU.mult, op1=ALU.subtract),
                 waits=[t3])
        t7a = b.a(lambda e: e.activation(out=c_t[:, 31:32], in_=c_t[:, 4:5],
                                         func=AF.Sqrt, bias=cs_af[:, 0:1]),
                  waits=[t6])
        t7 = b.v(lambda e: e.reciprocal(out=c_t[:, 5:6], in_=c_t[:, 31:32]),
                 waits=[t7a])
        t8 = b.v(lambda e: e.tensor_scalar(out=a_tmp[:, :], in0=a_xt[:, :],
                                           scalar1=c_t[:, 2:3], scalar2=c_t[:, 5:6],
                                           op0=ALU.subtract, op1=ALU.mult),
                 waits=[t7])
        tv = t8
        for k in range(16):
            q = k % 4
            tp = b.pe_tr(pt[:, q * 128:(q + 1) * 128],
                         a_tmp[:, k * 128:(k + 1) * 128], cs_id[:, :],
                         waits=[tv])
            col = k * colw + dst_col0
            tv = b.split(pt[:, q * 128:(q + 1) * 128], xh[:, col:col + 128],
                         xl[:, col:col + 128], waits=[tp])
        return tv

    def mm3pass(out_ap, out_tail, wh_s, wl_s, xcol0, ncols, waits,
                xh=None, xl=None, colw=640):
        xh = xh if xh is not None else xlt_h
        xl = xl if xl is not None else xlt_l
        parts = []
        n0 = min(ncols, 512)
        for k in range(16):
            xhi = xh[:, k * colw + xcol0: k * colw + xcol0 + n0]
            xlo = xl[:, k * colw + xcol0: k * colw + xcol0 + n0]
            first = (k == 0)
            for (wop, xop) in ((wh_s(k), xhi), (wh_s(k), xlo), (wl_s(k), xhi)):
                parts.append((out_ap, wop, xop, first, False))
                first = False
        parts[-1] = parts[-1][:4] + (True,)
        if ncols > 512:
            for k in range(16):
                xhi2 = xh[:, k * colw + xcol0 + 512: k * colw + xcol0 + ncols]
                xlo2 = xl[:, k * colw + xcol0 + 512: k * colw + xcol0 + ncols]
                f2 = (k == 0)
                for (wop, xop) in ((wh_s(k), xhi2), (wh_s(k), xlo2),
                                   (wl_s(k), xhi2)):
                    parts.append((out_tail, wop, xop, f2, False))
                    f2 = False
            parts[-1] = parts[-1][:4] + (True,)
        return b.mm_group(parts, waits=waits)

    CSTG = int(os.environ.get("K_CHUNK_STAGE", "9"))

    def chunk_math(h, blk, c_loc, t_first):
        cg = (blk * NBLK + c_loc) + h * NCH
        if CSTG == 0:
            return
        tyl_prev_guard = (None, 0)
        if dbg_ck is not None and "dbgc" in b.Q:
            t_first = list(t_first) + [b.P(b.Q["dbgc"])]
        col = c_loc * C
        gsl = a_ghat[:, c_loc * 128 + h * NG: c_loc * 128 + h * NG + NG]
        bcol = a_beta[:, c_loc * 16 + h: c_loc * 16 + h + 1]
        for n in range(NG):
            b.dma("gpsimd", a_kg[:, n * C:(n + 1) * C],
                  a_kf[16 * n:16 * (n + 1), col:col + C], stream="rg",
                  waits=t_first if n == 0 else ())
        for n in range(NG):
            b.dma("gpsimd", a_qg[:, n * C:(n + 1) * C],
                  a_qf[16 * n:16 * (n + 1), col:col + C], stream="rg")
        t_rg = b.P(b.Q["rg"])
        t1 = b.pe_mm(px[:, 0:8], cs_ui[:, :], gsl, True, True,
                     waits=t_first, inc=True)
        t2 = b.v(lambda e: e.tensor_copy(out=c_gt[:, :], in_=px[:, 0:8]), waits=[t1])
        t3 = b.pe_tr(pt[0:8, 0:128], c_gt[:, :], cs_id[:, :], waits=[t2])
        t4 = b.v(lambda e: e.tensor_copy(out=c_gf[:, :], in_=pt[0:8, 0:128]),
                 waits=[t3])
        if CSTG == 1:
            return
        t5 = b.dma("gpsimd", c_gfl[:, :], c_gf[:, :], stream="fl", waits=[t4])
        b.pe_mm(pr[:, 0:512], cs_or[0:1, :], c_gfl[0:1, 0:512], True, True,
                waits=[t5])
        t7 = b.pe_mm(pr[:, 512:1024], cs_or[0:1, :], c_gfl[0:1, 512:1024],
                     True, True, inc=True)
        t8 = b.v(lambda e: e.tensor_tensor(out=c_ra[:, :], in0=pr[:, :],
                                           in1=csbA[:, :], op=ALU.add), waits=[t7])
        t9 = b.v(lambda e: e.tensor_tensor(out=c_rm[:, :], in0=pr[:, :],
                                           in1=csbM[:, :], op=ALU.add))
        for n in range(NG):
            s2 = slice(n * C, (n + 1) * C)
            gcol = c_gt[:, n:n + 1]
            b.v(lambda e, s2=s2, g=gcol: e.tensor_scalar(
                out=c_ra[:, s2], in0=c_ra[:, s2], scalar1=g, scalar2=-1.0,
                op0=ALU.subtract, op1=ALU.mult))
            b.v(lambda e, s2=s2, g=gcol: e.tensor_scalar(
                out=c_rm[:, s2], in0=c_rm[:, s2], scalar1=g, scalar2=None,
                op0=ALU.subtract))
        tv = b.P(b.ve)
        b.a(lambda e: e.activation(out=c_ra[:, 0:512], in_=c_ra[:, 0:512],
                                   func=AF.Exp), waits=[tv])
        ea2 = b.a(lambda e: e.activation(out=c_ra[:, 512:1024], in_=c_ra[:, 512:1024],
                                         func=AF.Exp))
        b.a(lambda e: e.activation(out=c_rm[:, 0:512], in_=c_rm[:, 0:512],
                                   func=AF.Exp))
        em2 = b.a(lambda e: e.activation(out=c_rm[:, 512:1024], in_=c_rm[:, 512:1024],
                                         func=AF.Exp))
        if CSTG == 2:
            return
        tA = tM = None
        for wv in range(4):
            mwaits = [ea2, em2, b.P(b.ve)]
            toks = []
            for i, n in enumerate((wv * 2, wv * 2 + 1)):
                ksl = a_kg[:, n * C:(n + 1) * C]
                qsl = a_qg[:, n * C:(n + 1) * C]
                toks.append(b.pe_mm(pg[:, i * 256:i * 256 + 128], ksl, ksl,
                                    True, True,
                                    waits=(mwaits + [t_rg]) if i == 0 else (),
                                    inc=True))
                toks.append(b.pe_mm(pg[:, i * 256 + 128:i * 256 + 256], ksl, qsl,
                                    True, True, inc=True))
            for i, n in enumerate((wv * 2, wv * 2 + 1)):
                s2 = slice(n * C, (n + 1) * C)
                pK = pg[:, i * 256:i * 256 + 128]
                pQ = pg[:, i * 256 + 128:i * 256 + 256]
                wv_done = [toks[-1]]
                if n == 0:
                    tA = b.v(lambda e, s2=s2, p=pK: e.tensor_tensor(
                        out=c_A[:, :], in0=p, in1=c_ra[:, s2], op=ALU.mult),
                        waits=wv_done)
                    tM = b.v(lambda e, s2=s2, p=pQ: e.tensor_tensor(
                        out=c_Mt[:, :], in0=p, in1=c_rm[:, s2], op=ALU.mult))
                else:
                    b.v(lambda e, s2=s2, p=pK: e.tensor_tensor(
                        out=c_tt[:, :], in0=p, in1=c_ra[:, s2], op=ALU.mult),
                        waits=wv_done)
                    tA = b.v(lambda e: e.tensor_tensor(
                        out=c_A[:, :], in0=c_A[:, :], in1=c_tt[:, :], op=ALU.add))
                    b.v(lambda e, s2=s2, p=pQ: e.tensor_tensor(
                        out=c_tt[:, :], in0=p, in1=c_rm[:, s2], op=ALU.mult))
                    tM = b.v(lambda e: e.tensor_tensor(
                        out=c_Mt[:, :], in0=c_Mt[:, :], in1=c_tt[:, :], op=ALU.add))
        tA = b.v(lambda e, g=bcol: e.tensor_scalar(out=c_A[:, :], in0=c_A[:, :],
                                                   scalar1=g, scalar2=None,
                                                   op0=ALU.mult), waits=[tA])
        if CSTG == 3:
            return
        t10 = b.pe_mm(px[:, 128:256], cs_e1[:, :], c_gf[:, :], True, True,
                      waits=[t4, tA], inc=True)
        t11 = b.a(lambda e: e.activation(out=c_eg[:, :], in_=px[:, 128:256],
                                         func=AF.Exp), waits=[t10])
        t12 = b.v(lambda e: e.tensor_scalar(out=c_r2[:, :], in0=c_gf[:, :],
                                            scalar1=c_gf[:, 127:128], scalar2=-1.0,
                                            op0=ALU.subtract, op1=ALU.mult),
                  waits=[t4])
        t13 = b.pe_mm(px[:, 256:384], cs_e1[:, :], c_r2[:, :], True, True,
                      waits=[t12, t11], inc=True)
        t14 = b.a(lambda e: e.activation(out=c_kh[:, :], in_=px[:, 256:384],
                                         func=AF.Exp), waits=[t13])
        kslf = a_kf[:, col:col + C]
        qslf = a_qf[:, col:col + C]
        vslf = a_vf[:, col:col + C]
        t15 = b.v(lambda e: e.tensor_tensor(out=c_kt[:, :], in0=kslf, in1=c_eg[:, :],
                                            op=ALU.mult), waits=[t11])
        t16 = b.v(lambda e: e.tensor_tensor(out=c_qt[:, :], in0=qslf, in1=c_eg[:, :],
                                            op=ALU.mult))
        t17 = b.v(lambda e: e.tensor_tensor(out=c_kh[:, :], in0=kslf, in1=c_kh[:, :],
                                            op=ALU.mult), waits=[t14])
        if CSTG == 4:
            return
        t18 = b.pe_tr(pt[:, 0:128], c_kt[:, :], cs_id[:, :], waits=[t15])
        t19 = b.v(lambda e, g=bcol: e.tensor_scalar(out=c_X[:, 0:128],
                                                    in0=pt[:, 0:128], scalar1=g,
                                                    scalar2=None, op0=ALU.mult),
                  waits=[t18])
        t20 = b.pe_tr(pt[:, 128:256], vslf, cs_id[:, :], waits=[t19])
        t21 = b.v(lambda e, g=bcol: e.tensor_scalar(out=c_X[:, 128:256],
                                                    in0=pt[:, 128:256], scalar1=g,
                                                    scalar2=None, op0=ALU.mult),
                  waits=[t20])
        t22 = b.pe_tr(pt[:, 256:384], c_kh[:, :], cs_id[:, :], waits=[t17, t21])
        t23 = b.v(lambda e: e.tensor_copy(out=c_khT[:, :], in_=pt[:, 256:384]),
                  waits=[t22])
        t24 = b.pe_tr(pt[:, 384:512], c_A[:, :], cs_id[:, :], waits=[tA, t23])
        t25 = b.v(lambda e: e.tensor_copy(out=c_At[:, :], in_=pt[:, 384:512]),
                  waits=[t24])
        if CSTG == 5:
            return
        t26 = b.pe_mm(ps[:, 0:256], c_At[:, :], c_X[:, :], True, True,
                      waits=[t25], inc=True)
        tX = b.v(lambda e: e.tensor_sub(out=c_X[:, :], in0=c_X[:, :],
                                        in1=ps[:, 0:256]), waits=[t26])
        cur, curT = c_A, c_At
        nxt, nxtT = c_Ap, c_ApT
        lvl = 2
        while lvl < C:
            tq = b.pe_mm(ps[:, 256:384], curT[:, :], cur[:, :], True, True,
                         waits=[tX], inc=True)
            te = b.v(lambda e, nx=nxt: e.tensor_copy(out=nx[:, :], in_=ps[:, 256:384]),
                     waits=[tq])
            tt_ = b.pe_tr(pt[:, 384:512], nxt[:, :], cs_id[:, :], waits=[te])
            te2 = b.v(lambda e, nx=nxtT: e.tensor_copy(out=nx[:, :],
                                                       in_=pt[:, 384:512]),
                      waits=[tt_])
            ta = b.pe_mm(ps[:, 0:256], nxtT[:, :], c_X[:, :], True, True,
                         waits=[te2], inc=True)
            tX = b.v(lambda e: e.tensor_add(out=c_X[:, :], in0=c_X[:, :],
                                            in1=ps[:, 0:256]), waits=[ta])
            cur, curT, nxt, nxtT = nxt, nxtT, cur, curT
            lvl *= 2
        if CSTG == 6:
            return
        lamc = c_eg[:, 127:128]
        m1 = b.pe_mm(pg[:, 0:128], c_X[:, 0:128], c_khT[:, :], True, True,
                     waits=[tX], inc=True)
        tms0 = b.v(lambda e, l=lamc: e.tensor_scalar(out=c_tt[:, :], in0=cs_id[:, :],
                                                     scalar1=l, scalar2=None,
                                                     op0=ALU.mult), waits=[t11])
        tms = b.v(lambda e: e.tensor_sub(out=c_ms[:, :], in0=c_tt[:, :],
                                         in1=pg[:, 0:128]),
                  waits=[m1] + ([b.P(b.Q["sp"])] if "sp" in b.Q else []))
        m2 = b.pe_mm(pg[:, 128:256], c_khT[:, :], c_X[:, 128:256], True, True,
                     waits=[tms], inc=True)
        tns = b.v(lambda e: e.tensor_copy(out=c_ns[:, :], in_=pg[:, 128:256]),
                  waits=[m2])
        m3 = b.pe_mm(pg[:, 256:384], c_X[:, 0:128], c_Mt[:, :], True, True,
                     waits=[tns], inc=True)
        tq2 = b.v(lambda e: e.tensor_sub(out=c_q2[:, :], in0=c_qt[:, :],
                                         in1=pg[:, 256:384]), waits=[m3, t16])
        m4 = b.pe_mm(pg[:, 384:512], c_Mt[:, :], c_X[:, 128:256], True, True,
                     waits=[tq2], inc=True)
        tyl = b.v(lambda e: e.tensor_copy(out=c_yl[:, :], in_=pg[:, 384:512]),
                  waits=[m4])
        r0 = cg * 128
        if dbg_ck is not None:
            for col, (src, tok, w_) in enumerate((
                    (c_gt[:, 0:8], t2, 8), (c_eg[:, :], t11, 128),
                    (c_A[:, :], tA, 128), (c_Mt[:, :], tM, 128),
                    (c_X[:, :], tX, 256), (c_kt[:, :], t15, 128),
                    (c_kh[:, :], t17, 128), (c_khT[:, :], t23, 128),
                    (c_qt[:, :], t16, 128), (c_ra[:, :], ea2, 1024),
                    (c_rm[:, :], em2, 1024), (a_kg[:, :], t_rg, 1024))):
                off = [0, 8, 136, 264, 392, 648, 776, 904, 1032, 1160,
                       2184, 3208][col]
                b.dma("gpsimd", dbg_ck[r0:r0 + (128 if col != 11 else 16),
                                       off:off + w_],
                      src, stream="dbgc", waits=[tX, tyl_prev_guard])
        b.dma("gpsimd", sp_mst[r0:r0 + 128, :], c_ms[:, :], stream="sp", waits=[tms])
        b.dma("gpsimd", sp_ns[r0:r0 + 128, :], c_ns[:, :], stream="sp", waits=[tns])
        b.dma("gpsimd", sp_q2[r0:r0 + 128, :], c_q2[:, :], stream="sp", waits=[tq2])
        b.dma("gpsimd", sp_yl[r0:r0 + 128, :], c_yl[:, :], stream="sp", waits=[tyl])

    HSEL = list(range(H)) if heads is None else heads
    BSEL = list(range(NBLK)) if blocks is None else blocks
    if run["A"]:
        for blk in BSEL:
            deps = [t_consts, t_splits, t_gws, t_mask, b.P(b.pe), b.P(b.ve),
                    b.P(b.ac)]
            if "sp" in b.Q:
                deps.append(b.P(b.Q["sp"]))
            t_x = None
            for i in range(5):
                row0 = blk * BT + i * 128
                t_x = ln_transpose(xs[row0:row0 + 128, :], xlt_h, xlt_l, 640,
                                   i * 128, deps if i == 0 else [t_x])
            tgh = mm3pass(pj[:, 0:512], None,
                          lambda k: wf1h[:, k * 128:(k + 1) * 128],
                          lambda k: wf1l[:, k * 128:(k + 1) * 128], 128, 512,
                          waits=[t_x, b.P(b.ac)])
            tev = b.v(lambda e: e.tensor_copy(out=a_t1[:, :], in_=pj[:, 0:512]),
                      waits=[tgh])
            tv1 = None
            for cl in range(4):
                tm = b.pe_mm(pg[:, cl * 128:(cl + 1) * 128],
                             a_t1[:, cl * 128:(cl + 1) * 128], cs_wf2[:, :],
                             True, True, waits=[tev] if cl == 0 else [tv1],
                             inc=True)
                tv1 = b.v(lambda e, cl=cl: e.tensor_tensor(
                    out=a_ghat[:, cl * 128:(cl + 1) * 128],
                    in0=pg[:, cl * 128:(cl + 1) * 128], in1=cs_dtb[:, :],
                    op=ALU.add), waits=[tm])
                tv2a = b.a(lambda e, cl=cl: e.activation(
                    out=a_ghat[:, cl * 128:(cl + 1) * 128],
                    in_=a_ghat[:, cl * 128:(cl + 1) * 128], func=AF.Exp),
                    waits=[tv1])
                tv2 = b.a(lambda e, cl=cl: e.activation(
                    out=a_ghat[:, cl * 128:(cl + 1) * 128],
                    in_=a_ghat[:, cl * 128:(cl + 1) * 128], func=AF.Ln,
                    bias=cs_af[:, 4:5]), waits=[tv2a])
                b.v(lambda e, cl=cl: e.tensor_tensor(
                    out=a_ghat[:, cl * 128:(cl + 1) * 128],
                    in0=a_ghat[:, cl * 128:(cl + 1) * 128], in1=cs_ascl[:, :],
                    op=ALU.mult), waits=[tv2])
            tb_prev = None
            for cl in range(4):
                xc = 128 + cl * 128
                parts = []
                for k in range(16):
                    xhi = xlt_h[:, k * 640 + xc: k * 640 + xc + 128]
                    xlo = xlt_l[:, k * 640 + xc: k * 640 + xc + 128]
                    whi = wbh[:, k * 16:(k + 1) * 16]
                    wlo = wbl[:, k * 16:(k + 1) * 16]
                    for (xo, wop) in ((xhi, whi), (xlo, whi), (xhi, wlo)):
                        parts.append((px[:, 384 + cl * 16:400 + cl * 16],
                                      xo, wop, False, False))
                parts[0] = parts[0][:3] + (True, False)
                parts[-1] = parts[-1][:4] + (True,)
                tb = b.mm_group(parts, waits=[t_x, b.P(b.ac)] if cl == 0
                                else [tb_prev])
                tb_prev = b.a(lambda e, cl=cl: e.activation(
                    out=a_beta[:, cl * 16:(cl + 1) * 16],
                    in_=px[:, 384 + cl * 16:400 + cl * 16], func=AF.Sigmoid),
                    waits=[tb])
            wpar = 0
            land_free = [None, None]
            hilo_free = [None, None]

            def load_w(srcfn):
                nonlocal wpar
                par = wpar
                wpar = 1 - wpar
                w0 = [land_free[par]] if land_free[par] else []
                for k in range(16):
                    b.dma("sync", a_wl[par][:, k * 128:(k + 1) * 128], srcfn(k),
                          stream=f"wl{par}", waits=w0 if k == 0 else ())
                t_w = b.P(b.Q[f"wl{par}"])
                ws = [t_w] + ([hilo_free[par]] if hilo_free[par] else [])
                t_ws = b.split(a_wl[par][:, :], a_wh[par][:, :], a_wlo[par][:, :],
                               waits=ws)
                land_free[par] = t_ws
                return par, t_ws

            LHS = int(os.environ.get("K_LAST_HEAD_STAGE", "9"))
            for h in HSEL:
                is_last = (h == HSEL[-1])
                if is_last and LHS == 0:
                    continue
                conv_done = []
                for ti, (wt, cw_off) in enumerate(((wq, 0), (wk, 64), (wv, 128))):
                    par, t_ws = load_w(lambda k, wt=wt: wt[
                        k * 128:(k + 1) * 128, h * 128:(h + 1) * 128])
                    if is_last and LHS == 1:
                        continue
                    tp = mm3pass(pj[:, 0:512], px[:, 0:3],
                                 lambda k, p=par: a_wh[p][:, k * 128:(k + 1) * 128],
                                 lambda k, p=par: a_wlo[p][:, k * 128:(k + 1) * 128],
                                 125, 515, waits=[t_ws, t_x, b.P(b.ve), b.P(b.ac)])
                    hilo_free[par] = b.P(b.pe)
                    te1 = b.v(lambda e: e.tensor_copy(out=a_pre[:, 0:512],
                                                      in_=pj[:, 0:512]),
                              waits=[tp] + ([b.P(b.Q["dbgp"])]
                                            if "dbgp" in b.Q else []))
                    te2 = b.v(lambda e: e.tensor_copy(out=a_pre[:, 512:515],
                                                      in_=px[:, 0:3]))
                    dst = (a_qf, a_kf, a_vf)[ti]
                    if dbg_pre is not None:
                        b.dma("gpsimd",
                              dbg_pre[(ti * H + h) * 128:(ti * H + h + 1) * 128,
                                      blk * 515:(blk + 1) * 515],
                              a_pre[:, :], stream="dbgp", waits=[te1, te2])
                    if is_last and LHS == 2:
                        continue
                    t_cv = b.v(lambda e, d=dst, co=cw_off, hh=h: e.tensor_scalar(
                        out=d[:, :], in0=a_pre[:, 0:512],
                        scalar1=cs_cw[:, co + hh * 4: co + hh * 4 + 1], scalar2=None,
                        op0=ALU.mult), waits=[te1, te2,
                              b.P(b.Q["rg"]) if "rg" in b.Q else (None, 0),
                              b.P(b.Q["dbgq"]) if "dbgq" in b.Q else (None, 0)])
                    for j in range(1, 4):
                        t_cv = b.v(lambda e, d=dst, co=cw_off, j=j, hh=h:
                                   e.scalar_tensor_tensor(
                                       out=d[:, :], in0=a_pre[:, j:j + 512],
                                       scalar=cs_cw[:, co + hh * 4 + j:
                                                    co + hh * 4 + j + 1],
                                       in1=d[:, :], op0=ALU.mult, op1=ALU.add))
                    t_si = b.a(lambda e, d=dst: e.activation(out=d[:, :], in_=d[:, :],
                                                             func=AF.Silu),
                               waits=[t_cv])
                    conv_done.append(t_si)
                if is_last and LHS <= 3:
                    continue
                for dst, t_pr, bcol_, scol_ in (() if os.environ.get("K_SKIP_L2")
                                                else ((a_qf, conv_done[0], 1, 3),
                                                      (a_kf, conv_done[1], 2, 4))):
                    tsq = b.a(lambda e, d=dst: e.square(out=a_sq[:, :], in_=d[:, :]),
                              waits=[t_pr])
                    tmm = b.pe_mm(ps[0:1, 0:512], cs_oc[:, 0:1], a_sq[:, :],
                                  True, True, waits=[tsq], inc=True)
                    trsa = b.a(lambda e, bc=bcol_, sc=scol_: e.activation(
                        out=a_srow[0:1, :], in_=ps[0:1, 0:512], func=AF.Sqrt,
                        bias=cs_af[0:1, bc:bc + 1], scale=cs_af[0:1, sc:sc + 1]),
                        waits=[tmm])
                    trs = b.v(lambda e: e.reciprocal(out=a_srow[0:1, :],
                                                     in_=a_srow[0:1, :]),
                              waits=[trsa])
                    tbc = b.pe_mm(pg[:, 0:512], cs_or[0:1, :], a_srow[0:1, :],
                                  True, True, waits=[trs, b.P(b.ve)], inc=True)
                    b.v(lambda e, d=dst: e.tensor_tensor(out=d[:, :], in0=d[:, :],
                                                         in1=pg[:, 0:512],
                                                         op=ALU.mult), waits=[tbc])
                t_qk = b.P(b.ve)
                if dbg_qkv is not None:
                    for ti2, src2 in enumerate((a_qf, a_kf, a_vf)):
                        b.dma("gpsimd",
                              dbg_qkv[(ti2 * H + h) * 128:(ti2 * H + h + 1) * 128,
                                      blk * BT:(blk + 1) * BT],
                              src2[:, :], stream="dbgq",
                              waits=[t_qk, conv_done[2]])
                par, t_ws = load_w(lambda k: wg[k * 128:(k + 1) * 128,
                                                h * 128:(h + 1) * 128])
                tg = mm3pass(pj2[:, 0:512], None,
                             lambda k, p=par: a_wh[p][:, k * 128:(k + 1) * 128],
                             lambda k, p=par: a_wlo[p][:, k * 128:(k + 1) * 128],
                             128, 512, waits=[t_ws, b.P(b.ac)])
                hilo_free[par] = b.P(b.pe)
                tsg = b.a(lambda e, hh=h: e.activation(out=a_sg[:, :],
                                                 in_=pj2[:, 0:512],
                                                 func=AF.Silu,
                                                 bias=cs_bg[:, hh:hh + 1]),
                          waits=[tg, b.P(b.Q["sg"]) if "sg" in b.Q else (None, 0)])
                b.dma("gpsimd", sp_sg[h * 128:(h + 1) * 128,
                                      blk * BT:(blk + 1) * BT],
                      a_sg[:, :], stream="sg", waits=[tsg])
                t_in = [t_qk, conv_done[2], b.P(b.ac)]
                for cl in range(4):
                    chunk_math(h, blk, cl, t_in if cl == 0 else
                               ([b.P(b.Q["sp"])] if "sp" in b.Q else t_in))

    stkA.close()

    b_s = [E(nc.sbuf_tensor(f"b_s{i}", [128, 128], F32)) for i in range(2)]
    b_m = [E(nc.sbuf_tensor(f"b_m{i}", [128, 128], F32)) for i in range(2)]
    b_n = [E(nc.sbuf_tensor(f"b_n{i}", [128, 128], F32)) for i in range(2)]
    if run["B"]:
        for h in range(H):
            tz = b.v(lambda e: e.memset(b_s[0][:, :], 0.0),
                     waits=[b.P(b.Q["sp"]), b.P(b.Q["sg"]), b.P(b.pe)]
                     if h == 0 else
                     ([b.P(b.Q["ccin"])] if "ccin" in b.Q else []))
            t_pv = tz
            for ci in range(NCH):
                cg = h * NCH + ci
                par = ci % 2
                r0 = cg * 128
                tm1 = b.dma("sync", b_m[par][:, :], sp_mst[r0:r0 + 128, :],
                            stream=f"pb{par}", waits=[t_pv])
                tm2 = b.dma("sync", b_n[par][:, :], sp_ns[r0:r0 + 128, :],
                            stream=f"pb{par}")
                sprev = b_s[ci % 2]
                snext = b_s[1 - ci % 2]
                tmm = b.pe_mm(px[:, 0:128], b_m[par][:, :], sprev[:, :],
                              True, True, waits=[tm1, t_pv], inc=True)
                t_pv = b.v(lambda e, sn=snext, pn=b_n[par]: e.tensor_add(
                    out=sn[:, :], in0=pn[:, :], in1=px[:, 0:128]),
                    waits=[tmm, tm2])
            send = b_s[0]
            tsc = b.v(lambda e, s=send: e.tensor_scalar(
                out=s[:, :], in0=s[:, :], scalar1=cs_iev[:, 0:1], scalar2=None,
                op0=ALU.mult), waits=[t_pv])
            b.dma("gpsimd", cc_in.ap()[h * 128:(h + 1) * 128, :], send[:, :],
                  stream="ccin", waits=[tsc])
            if dbg:
                b.dma("gpsimd", dbg_send[h * 128:(h + 1) * 128, :], send[:, :],
                      stream="dbg")
        b.emit("gpsimd", lambda e: e.collective_compute(
            "AllReduce", ALU.add,
            replica_groups=[[0, 1], [2, 3], [4, 5], [6, 7]],
            ins=[cc_in.ap().opt()], outs=[cc_out.ap().opt()]),
            waits=[b.P(b.Q["ccin"])], inc=b.cc, d=1)

    yslab_h = EC(nc.sbuf_tensor("yslab_h", [128, H * TLOC], BF16))
    yslab_l = EC(nc.sbuf_tensor("yslab_l", [128, H * TLOC], BF16))
    c_rl = [[EC(nc.sbuf_tensor(f"c_rl{i}_{j}", [128, 128], F32)) for j in range(4)]
            for i in range(2)]
    c_sgC = [EC(nc.sbuf_tensor(f"c_sgC{i}", [128, BT], F32)) for i in range(2)]
    c_O = EC(nc.sbuf_tensor("c_O", [128, 128], F32))
    c_y = EC(nc.sbuf_tensor("c_y", [128, 128], F32))
    c_y2 = EC(nc.sbuf_tensor("c_y2", [128, 128], F32))
    if run["C"]:
        for h in range(H):
            tl = b.dma("sync", b_s[0][:, :], cc_out.ap()[h * 128:(h + 1) * 128, :],
                       stream="si", waits=[(b.cc, 1), b.P(b.ve)])
            t_pv = b.v(lambda e: e.tensor_scalar(out=b_s[0][:, :], in0=b_s[0][:, :],
                                                 scalar1=cs_iod[:, 0:1],
                                                 scalar2=None, op0=ALU.mult),
                       waits=[tl])
            tsg = None
            for ci in range(NCH):
                cg = h * NCH + ci
                par = ci % 2
                r0 = cg * 128
                rl = c_rl[par]
                tr1 = b.dma("sync", rl[0][:, :], sp_mst[r0:r0 + 128, :],
                            stream=f"pc{par}", waits=[t_pv])
                tr2 = b.dma("sync", rl[1][:, :], sp_ns[r0:r0 + 128, :],
                            stream=f"pc{par}")
                tr3 = b.dma("sync", rl[2][:, :], sp_q2[r0:r0 + 128, :],
                            stream=f"pc{par}")
                tr4 = b.dma("sync", rl[3][:, :], sp_yl[r0:r0 + 128, :],
                            stream=f"pc{par}")
                if ci % 4 == 0:
                    tsg = b.dma("sync", c_sgC[(ci // 4) % 2][:, :],
                                sp_sg[h * 128:(h + 1) * 128,
                                      (ci // 4) * BT:(ci // 4 + 1) * BT],
                                stream="sgc", waits=[b.P(b.pe)])
                sgb = c_sgC[(ci // 4) % 2]
                sprev = b_s[ci % 2]
                snext = b_s[1 - ci % 2]
                to1 = b.pe_mm(pg[:, 0:128], rl[2][:, :], sprev[:, :], True, True,
                              waits=[tr3, t_pv], inc=True)
                to2 = b.v(lambda e, r=rl[3]: e.tensor_add(out=c_O[:, :],
                                                          in0=pg[:, 0:128],
                                                          in1=r[:, :]),
                          waits=[to1, tr4])
                tc1 = b.pe_mm(pg[:, 128:256], rl[0][:, :], sprev[:, :], True, True,
                              waits=[tr1, to2], inc=True)
                t_pv = b.v(lambda e, sn=snext, r=rl[1]: e.tensor_add(
                    out=sn[:, :], in0=r[:, :], in1=pg[:, 128:256]),
                    waits=[tc1, tr2])
                tt1 = b.pe_tr(pt[:, 0:128],
                              sgb[:, (ci % 4) * 128:(ci % 4 + 1) * 128],
                              cs_id[:, :],
                              waits=[tsg, to2] if ci % 4 == 0 else [to2])
                ty1 = b.v(lambda e: e.tensor_tensor(out=c_y[:, :], in0=c_O[:, :],
                                                    in1=pt[:, 0:128], op=ALU.mult),
                          waits=[tt1])
                ty2 = b.v(lambda e: e.tensor_tensor(out=c_y2[:, :], in0=c_y[:, :],
                                                    in1=c_y[:, :], op=ALU.mult))
                ty3 = b.v(lambda e: e.tensor_reduce(out=c_t[:, 6:7], in_=c_y2[:, :],
                                                    axis=AX.X, op=ALU.add),
                          waits=[ty2])
                ty4a = b.a(lambda e: e.activation(out=c_t[:, 30:31],
                                                  in_=c_t[:, 6:7],
                                                  func=AF.Sqrt,
                                                  bias=cs_af[:, 0:1],
                                                  scale=cs_af[:, 5:6]),
                           waits=[ty3])
                ty4 = b.v(lambda e: e.reciprocal(out=c_t[:, 7:8],
                                                 in_=c_t[:, 30:31]), waits=[ty4a])
                ty5 = b.v(lambda e: e.scalar_tensor_tensor(
                    out=c_y2[:, :], in0=c_y[:, :], scalar=c_t[:, 7:8],
                    in1=cs_nw[:, :], op0=ALU.mult, op1=ALU.mult), waits=[ty4])
                tt2 = b.pe_tr(pt[:, 128:256], c_y2[:, :], cs_id[:, :], waits=[ty5])
                colh = h * TLOC + ci * 128
                b.split(pt[:, 128:256], yslab_h[:, colh:colh + 128],
                        yslab_l[:, colh:colh + 128], waits=[tt2])
        if dbg:
            tA = b.P(b.ve)
            for h in range(H):
                b.dma("gpsimd", dbg_y[h * 128:(h + 1) * 128, :],
                      yslab_h[:, h * TLOC:(h + 1) * TLOC], stream="dbg",
                      waits=[tA] if h == 0 else ())

    d_wl = EC(nc.sbuf_tensor("d_wl", [128, 16 * 256], F32))
    d_wh = EC(nc.sbuf_tensor("d_wh", [128, 16 * 256], BF16))
    d_wlo = EC(nc.sbuf_tensor("d_wlo", [128, 16 * 256], BF16))
    d_xr = [EC(nc.sbuf_tensor(f"d_xr{i}", [128, 256], F32)) for i in range(2)]
    d_x1 = [EC(nc.sbuf_tensor(f"d_x1{i}", [128, 256], F32)) for i in range(2)]
    if run["D"]:
        t_y = b.P(b.ve)
        for dc in range(8):
            for k in range(16):
                b.dma("sync", d_wl[:, k * 256:(k + 1) * 256],
                      wo[k * 128:(k + 1) * 128, dc * 256:(dc + 1) * 256],
                      stream="wo", waits=[b.P(b.ve)] if k == 0 else ())
            t_w = b.P(b.Q["wo"])
            t_ws = b.split(d_wl[:, :], d_wh[:, :], d_wlo[:, :],
                           waits=[t_w, b.P(b.pe)])
            for tt in range(16):
                par = tt % 2
                txr = b.dma("sync", d_xr[par][:, :],
                            xs[HALO + tt * 128:HALO + (tt + 1) * 128,
                               dc * 256:(dc + 1) * 256],
                            stream=f"xr{par}", waits=[b.P(b.ve)])
                parts = []
                for k in range(16):
                    yh = yslab_h[:, k * TLOC + tt * 128: k * TLOC + (tt + 1) * 128]
                    yl2 = yslab_l[:, k * TLOC + tt * 128: k * TLOC + (tt + 1) * 128]
                    wh2 = d_wh[:, k * 256:(k + 1) * 256]
                    wl2 = d_wlo[:, k * 256:(k + 1) * 256]
                    for (yo, wop) in ((yh, wh2), (yh, wl2), (yl2, wh2)):
                        parts.append((pj[:, 0:256], yo, wop, False, False))
                parts[0] = parts[0][:3] + (True, False)
                parts[-1] = parts[-1][:4] + (True,)
                tm = b.mm_group(parts, waits=[t_ws, t_y, b.P(b.ve)])
                tev = b.v(lambda e, p=par: e.tensor_add(out=d_x1[p][:, :],
                                                        in0=pj[:, 0:256],
                                                        in1=d_xr[p][:, :]),
                          waits=[tm, txr,
                                 b.P(b.Q["x1"]) if "x1" in b.Q else (None, 0)])
                b.dma("gpsimd", sp_x1[tt * 128:(tt + 1) * 128,
                                      dc * 256:(dc + 1) * 256],
                      d_x1[par][:, :], stream="x1", waits=[tev])

    stkCD.close()

    e_acc = EE(nc.sbuf_tensor("e_acc", [128, (EHALF // 128) * D], F32))
    e_x2h = EE(nc.sbuf_tensor("e_x2h", [128, 16 * EHALF], BF16))
    e_x2l = EE(nc.sbuf_tensor("e_x2l", [128, 16 * EHALF], BF16))
    e_t = EE(nc.sbuf_tensor("e_t", [128, D], F32))
    e_h2h = EE(nc.sbuf_tensor("e_h2h", [128, 2 * EHALF], BF16))
    e_h2l = EE(nc.sbuf_tensor("e_h2l", [128, 2 * EHALF], BF16))
    e_w1l = [EE(nc.sbuf_tensor(f"e_w1l{i}", [128, D], F32)) for i in range(2)]
    e_w1h = [EE(nc.sbuf_tensor(f"e_w1h{i}", [128, D], BF16)) for i in range(2)]
    e_w1o = [EE(nc.sbuf_tensor(f"e_w1o{i}", [128, D], BF16)) for i in range(2)]
    e_w3l = [EE(nc.sbuf_tensor(f"e_w3l{i}", [128, D], F32)) for i in range(2)]
    e_w3h = [EE(nc.sbuf_tensor(f"e_w3h{i}", [128, D], BF16)) for i in range(2)]
    e_w3o = [EE(nc.sbuf_tensor(f"e_w3o{i}", [128, D], BF16)) for i in range(2)]
    e_w2l = EE(nc.sbuf_tensor("e_w2l", [128, 2 * D], F32))
    e_w2h = EE(nc.sbuf_tensor("e_w2h", [128, 2 * D], BF16))
    e_w2o = EE(nc.sbuf_tensor("e_w2o", [128, 2 * D], BF16))
    e_cw = EE(nc.sbuf_tensor("e_cw", [128, 64], F32))
    e_sc = EE(nc.sbuf_tensor("e_sc", [128, 8], F32))
    if run["E"]:
        for hf in range(TLOC // EHALF):
            t_pd = [b.P(b.Q["x1"]) if "x1" in b.Q else (None, 0),
                    b.P(b.pe), b.P(b.ve)]
            if "out" in b.Q:
                t_pd.append(b.P(b.Q["out"]))
            for tt in range(EHALF // 128):
                row0 = hf * EHALF + tt * 128
                tx = b.dma("sync", e_acc[:, tt * D:(tt + 1) * D],
                           sp_x1[row0:row0 + 128, :], stream="ex",
                           waits=t_pd if tt == 0 else [b.P(b.ve)])
                accs = e_acc[:, tt * D:(tt + 1) * D]
                t1 = b.v(lambda e, a2=accs: e.tensor_reduce(
                    out=c_t[:, 0:1], in_=a2, axis=AX.X, op=ALU.add), waits=[tx])
                t2 = b.a(lambda e, a2=accs: e.square(out=e_t[:, :], in_=a2),
                         waits=[tx, b.P(b.ve)])
                t3 = b.v(lambda e: e.tensor_reduce(out=c_t[:, 1:2], in_=e_t[:, :],
                                                   axis=AX.X, op=ALU.add),
                         waits=[t2])
                t4 = b.v(lambda e: e.tensor_scalar(out=c_t[:, 2:3], in0=c_t[:, 0:1],
                                                   scalar1=1.0 / D, scalar2=None,
                                                   op0=ALU.mult), waits=[t1])
                t5 = b.v(lambda e: e.tensor_tensor(out=c_t[:, 3:4], in0=c_t[:, 2:3],
                                                   in1=c_t[:, 2:3], op=ALU.mult))
                t6 = b.v(lambda e: e.scalar_tensor_tensor(
                    out=c_t[:, 4:5], in0=c_t[:, 1:2], scalar=1.0 / D,
                    in1=c_t[:, 3:4], op0=ALU.mult, op1=ALU.subtract), waits=[t3])
                t7a = b.a(lambda e: e.activation(out=c_t[:, 31:32],
                                                 in_=c_t[:, 4:5],
                                                 func=AF.Sqrt,
                                                 bias=cs_af[:, 0:1]), waits=[t6])
                t7 = b.v(lambda e: e.reciprocal(out=c_t[:, 5:6],
                                                in_=c_t[:, 31:32]), waits=[t7a])
                t8 = b.v(lambda e, a2=accs: e.tensor_scalar(
                    out=e_t[:, :], in0=a2, scalar1=c_t[:, 2:3],
                    scalar2=c_t[:, 5:6], op0=ALU.subtract, op1=ALU.mult),
                    waits=[t7])
                tv = t8
                for k in range(16):
                    q = k % 4
                    tp = b.pe_tr(pt[:, q * 128:(q + 1) * 128],
                                 e_t[:, k * 128:(k + 1) * 128], cs_id[:, :],
                                 waits=[tv])
                    col = k * EHALF + tt * 128
                    tv = b.split(pt[:, q * 128:(q + 1) * 128],
                                 e_x2h[:, col:col + 128], e_x2l[:, col:col + 128],
                                 waits=[tp])
                parts = []
                for k in range(16):
                    xh2 = e_x2h[:, k * EHALF + tt * 128: k * EHALF + (tt + 1) * 128]
                    xl2 = e_x2l[:, k * EHALF + tt * 128: k * EHALF + (tt + 1) * 128]
                    gh2 = gwh[:, k * 8:(k + 1) * 8]
                    gl2 = gwl[:, k * 8:(k + 1) * 8]
                    for (xo, go) in ((xh2, gh2), (xl2, gh2), (xh2, gl2)):
                        parts.append((px[:, 0:8], xo, go, False, False))
                parts[0] = parts[0][:3] + (True, False)
                parts[-1] = parts[-1][:4] + (True,)
                tsc_ = b.mm_group(parts, waits=[tv, t_gws, b.P(b.ve), b.P(b.ac)])
                r1 = b.v(lambda e: e.tensor_reduce(out=c_t[:, 8:9], in_=px[:, 0:8],
                                                   axis=AX.X, op=ALU.max),
                         waits=[tsc_])
                r1b = b.v(lambda e: e.tensor_scalar(out=c_t[:, 8:9],
                                                    in0=c_t[:, 8:9], scalar1=-1.0,
                                                    scalar2=None, op0=ALU.mult))
                r2 = b.a(lambda e: e.activation(out=e_sc[:, :], in_=px[:, 0:8],
                                                func=AF.Exp, bias=c_t[:, 8:9]),
                         waits=[r1b])
                r3 = b.v(lambda e: e.tensor_reduce(out=c_t[:, 9:10], in_=e_sc[:, :],
                                                   axis=AX.X, op=ALU.add),
                         waits=[r2])
                r4 = b.v(lambda e: e.reciprocal(out=c_t[:, 10:11],
                                                in_=c_t[:, 9:10]), waits=[r3])
                r5 = b.v(lambda e: e.tensor_scalar(out=e_sc[:, :], in0=e_sc[:, :],
                                                   scalar1=c_t[:, 10:11],
                                                   scalar2=None, op0=ALU.mult),
                         waits=[r4])
                r6 = b.v(lambda e: e.tensor_reduce(out=c_t[:, 11:12],
                                                   in_=e_sc[:, :], axis=AX.X,
                                                   op=ALU.max), waits=[r5])
                r7 = b.v(lambda e: e.tensor_scalar(out=c_t[:, 16:24], in0=e_sc[:, :],
                                                   scalar1=c_t[:, 11:12],
                                                   scalar2=None, op0=ALU.is_ge),
                         waits=[r6])
                r8 = b.v(lambda e: e.scalar_tensor_tensor(
                    out=c_t[:, 16:24], in0=c_t[:, 16:24], scalar=-BIG,
                    in1=e_sc[:, :], op0=ALU.mult, op1=ALU.add), waits=[r7])
                r9 = b.v(lambda e: e.tensor_reduce(out=c_t[:, 12:13],
                                                   in_=c_t[:, 16:24], axis=AX.X,
                                                   op=ALU.max), waits=[r8])
                r10 = b.v(lambda e: e.tensor_scalar(out=c_t[:, 16:24],
                                                    in0=e_sc[:, :],
                                                    scalar1=c_t[:, 12:13],
                                                    scalar2=None, op0=ALU.is_ge),
                          waits=[r9])
                b.v(lambda e, tt=tt: e.tensor_tensor(
                    out=e_cw[:, tt * 8:(tt + 1) * 8], in0=e_sc[:, :],
                    in1=c_t[:, 16:24], op=ALU.mult), waits=[r10])
            t_router = b.P(b.ve)
            for ex in range(10):
                if ex < 2:
                    w1src = lambda k, ft, ex=ex: sw1[
                        k * 128:(k + 1) * 128,
                        ex * FR + ft * 128: ex * FR + (ft + 1) * 128]
                    w3src = lambda k, ft, ex=ex: sw3[
                        k * 128:(k + 1) * 128,
                        ex * FR + ft * 128: ex * FR + (ft + 1) * 128]
                    w2src = lambda ft, ex=ex: sw2[
                        ex * FR + ft * 128: ex * FR + (ft + 1) * 128, :]
                    cwcol = None
                else:
                    ei = ex - 2
                    w1src = lambda k, ft, ei=ei: rw1[
                        ei * D + k * 128: ei * D + (k + 1) * 128,
                        ft * 128:(ft + 1) * 128]
                    w3src = lambda k, ft, ei=ei: rw3[
                        ei * D + k * 128: ei * D + (k + 1) * 128,
                        ft * 128:(ft + 1) * 128]
                    w2src = lambda ft, ei=ei: rw2[
                        ei * FR + ft * 128: ei * FR + (ft + 1) * 128, :]
                    cwcol = ex - 2
                nft = FR // 128
                for fg0 in range(0, nft, 2):
                    gsz = min(2, nft - fg0)
                    for sl_ in range(gsz):
                        ft = fg0 + sl_
                        par = ft % 2
                        for k in range(16):
                            b.dma("sync", e_w1l[par][:, k * 128:(k + 1) * 128],
                                  w1src(k, ft), stream=f"ew{par}",
                                  waits=[b.P(b.ve)] if k == 0 else ())
                        for k in range(16):
                            b.dma("sync", e_w3l[par][:, k * 128:(k + 1) * 128],
                                  w3src(k, ft), stream=f"ew{par}")
                        t_w = b.P(b.Q[f"ew{par}"])
                        ts1 = b.split(e_w1l[par][:, :], e_w1h[par][:, :],
                                      e_w1o[par][:, :], waits=[t_w, b.P(b.pe)])
                        ts3 = b.split(e_w3l[par][:, :], e_w3h[par][:, :],
                                      e_w3o[par][:, :])
                        for s_ in range(EHALF // 512):
                            xc = s_ * 512
                            p1b = pj if s_ == 0 else pj2
                            p3b = pg if s_ == 0 else pr
                            parts = []
                            for k in range(16):
                                xh2 = e_x2h[:, k * EHALF + xc: k * EHALF + xc + 512]
                                xl2 = e_x2l[:, k * EHALF + xc: k * EHALF + xc + 512]
                                wch = e_w1h[par][:, k * 128:(k + 1) * 128]
                                wcl = e_w1o[par][:, k * 128:(k + 1) * 128]
                                for (wop, xop) in ((wch, xh2), (wch, xl2),
                                                   (wcl, xh2)):
                                    parts.append((p1b[:, 0:512], wop, xop,
                                                  False, False))
                            parts[0] = parts[0][:3] + (True, False)
                            parts[-1] = parts[-1][:4] + (True,)
                            tm1 = b.mm_group(parts, waits=[ts1, t_router,
                                                           b.P(b.ve), b.P(b.ac)])
                            parts = []
                            for k in range(16):
                                xh2 = e_x2h[:, k * EHALF + xc: k * EHALF + xc + 512]
                                xl2 = e_x2l[:, k * EHALF + xc: k * EHALF + xc + 512]
                                wch = e_w3h[par][:, k * 128:(k + 1) * 128]
                                wcl = e_w3o[par][:, k * 128:(k + 1) * 128]
                                for (wop, xop) in ((wch, xh2), (wch, xl2),
                                                   (wcl, xh2)):
                                    parts.append((p3b[:, 0:512], wop, xop,
                                                  False, False))
                            parts[0] = parts[0][:3] + (True, False)
                            parts[-1] = parts[-1][:4] + (True,)
                            tm3 = b.mm_group(parts, waits=[ts3])
                            tsl = b.a(lambda e, p=p1b: e.activation(
                                out=e_t[:, 0:512], in_=p[:, 0:512], func=AF.Silu),
                                waits=[tm1, b.P(b.ve)])
                            th2 = b.v(lambda e, p=p3b: e.tensor_tensor(
                                out=e_t[:, 512:1024], in0=e_t[:, 0:512],
                                in1=p[:, 0:512], op=ALU.mult), waits=[tsl, tm3])
                            col = sl_ * EHALF + xc
                            b.split(e_t[:, 512:1024], e_h2h[:, col:col + 512],
                                    e_h2l[:, col:col + 512], waits=[th2])
                    t_h2 = b.P(b.ve)
                    for sl_ in range(gsz):
                        b.dma("sync", e_w2l[:, sl_ * D:(sl_ + 1) * D],
                              w2src(fg0 + sl_), stream="ew2",
                              waits=[b.P(b.ve)] if sl_ == 0 else ())
                    t_w2 = b.P(b.Q["ew2"])
                    ts2 = b.split(e_w2l[:, 0:gsz * D], e_w2h[:, 0:gsz * D],
                                  e_w2o[:, 0:gsz * D], waits=[t_w2, b.P(b.pe)])
                    for tt in range(EHALF // 128):
                        for dc in range(4):
                            pb = px if (tt * 4 + dc) % 2 == 0 else ps
                            parts = []
                            for sl_ in range(gsz):
                                hh = e_h2h[:, sl_ * EHALF + tt * 128:
                                           sl_ * EHALF + (tt + 1) * 128]
                                hl = e_h2l[:, sl_ * EHALF + tt * 128:
                                           sl_ * EHALF + (tt + 1) * 128]
                                wh2 = e_w2h[:, sl_ * D + dc * 512:
                                            sl_ * D + (dc + 1) * 512]
                                wl2 = e_w2o[:, sl_ * D + dc * 512:
                                            sl_ * D + (dc + 1) * 512]
                                for (ho, wop) in ((hh, wh2), (hh, wl2),
                                                  (hl, wh2)):
                                    parts.append((pb[:, 0:512], ho, wop,
                                                  False, False))
                            parts[0] = parts[0][:3] + (True, False)
                            parts[-1] = parts[-1][:4] + (True,)
                            tm2 = b.mm_group(parts, waits=[ts2, t_h2, b.P(b.ve)])
                            acc2 = e_acc[:, tt * D + dc * 512:
                                         tt * D + (dc + 1) * 512]
                            if cwcol is None:
                                b.v(lambda e, p=pb, a2=acc2: e.tensor_tensor(
                                    out=a2, in0=p[:, 0:512], in1=a2, op=ALU.add),
                                    waits=[tm2])
                            else:
                                cwc = e_cw[:, tt * 8 + cwcol: tt * 8 + cwcol + 1]
                                b.v(lambda e, p=pb, a2=acc2, cw=cwc:
                                    e.scalar_tensor_tensor(
                                        out=a2, in0=p[:, 0:512], scalar=cw,
                                        in1=a2, op0=ALU.mult, op1=ALU.add),
                                    waits=[tm2])
            tfin = b.P(b.ve)
            for tt in range(EHALF // 128):
                b.dma("gpsimd", out[hf * EHALF + tt * 128:
                                    hf * EHALF + (tt + 1) * 128, :],
                      e_acc[:, tt * D:(tt + 1) * D], stream="out",
                      waits=[tfin] if tt == 0 else ())

    stkE.close()

    endw = [b.P(b.pe), b.P(b.ve), b.P(b.ac)] + [b.P(q) for q in b.Q.values()]

    def fin(e, ws=tuple(endw)):
        for s, v in ws:
            if v > 0:
                e.wait_ge(s.h, v)
    b.q["gpsimd"].append(fin)

    with nc.Block() as block:
        @block.sync
        def _(e):
            for f in b.q["sync"]:
                f(e)

        @block.gpsimd
        def _(e):
            for f in b.q["gpsimd"]:
                f(e)

        @block.tensor
        def _(e):
            for f in b.q["tensor"]:
                f(e)

        @block.vector
        def _(e):
            for f in b.q["vector"]:
                f(e)

        @block.scalar
        def _(e):
            for f in b.q["scalar"]:
                f(e)
    stack.close()
    return nc


def make_in_maps(inputs):
    cs = host_constants()
    x = np.asarray(inputs["L0_attn_hidden_states"])
    g = lambda n: np.ascontiguousarray(np.asarray(inputs[n], np.float32))
    conv = np.zeros((128, 192), np.float32)
    for ti, n in enumerate(("conv_q", "conv_k", "conv_v")):
        w = g(f"L0_attn_{n}").reshape(16, 128, 4)
        conv[:, ti * 64:(ti + 1) * 64] = w.transpose(1, 0, 2).reshape(128, 64)
    common = {
        "wq": g("L0_attn_Wq"), "wk": g("L0_attn_Wk"), "wv": g("L0_attn_Wv"),
        "wg": g("L0_attn_Wg"), "wo": g("L0_attn_Wo"), "wf1": g("L0_attn_Wf1"),
        "wf2": g("L0_attn_Wf2"), "wb": g("L0_attn_Wb"), "convw": conv,
        "afm": np.ascontiguousarray(np.broadcast_to(
            g("L0_attn_A_log")[:, None], (16, 128))),
        "dtb": np.ascontiguousarray(np.broadcast_to(
            g("L0_attn_dt_bias")[None, :], (128, 128))),
        "nwbc": np.ascontiguousarray(np.broadcast_to(
            g("L0_attn_norm_w")[None, :], (128, 128))),
        "bgc": np.ascontiguousarray(g("L0_attn_bg").reshape(16, 128).T),
        "gatew": g("L0_ffn_gate_w"),
        "sw1": g("L0_ffn_sw1"), "sw2": g("L0_ffn_sw2"), "sw3": g("L0_ffn_sw3"),
        "rw1": g("L0_ffn_rw1").reshape(NE * D, FR),
        "rw2": g("L0_ffn_rw2").reshape(NE * FR, D),
        "rw3": g("L0_ffn_rw3").reshape(NE * D, FR),
        "cident": cs["cident"], "cuincl": cs["cuincl"], "ce1": cs["ce1"],
        "ce2": cs["ce2"], "conesr": cs["conesr"], "conesc": cs["conesc"],
        "cmaskA": cs["cmaskA"], "cmaskM": cs["cmaskM"],
        "cactf": np.ascontiguousarray(np.broadcast_to(np.array(
            [1e-5, 1e-6 * 128.0, 1e-6, 128.0, 1.0, 1.0 / 128.0, 0.0, 0.0],
            np.float32)[None, :], (128, 8))),
    }
    in_maps = []
    for cid in range(8):
        bi, half = cid // 2, cid % 2
        shard = np.zeros((HALO + TLOC, D), np.float32)
        lo = half * TLOC
        if lo >= HALO:
            shard[:HALO] = x[bi, lo - HALO:lo]
        shard[HALO:] = x[bi, lo:lo + TLOC]
        m = dict(common)
        m["xs"] = shard
        m["cisev"] = np.full((128, 1), 1.0 if half == 0 else 0.0, np.float32)
        m["cisod"] = np.full((128, 1), 1.0 if half == 1 else 0.0, np.float32)
        in_maps.append(m)
    return in_maps


def kernel(**inputs):
    nc = build_program()
    in_maps = make_in_maps(inputs)
    res = run_bass_kernel_spmd(nc, in_maps, core_ids=list(range(8)))
    out = np.empty((4, 4096, D), np.float32)
    for cid in range(8):
        bi, half = cid // 2, cid % 2
        out[bi, half * TLOC:(half + 1) * TLOC] = res.results[cid]["out"]
    return out


# version 03
# speedup vs baseline: 1.3771x; 1.3771x over previous
import os
import numpy as np
import concourse.bass as bass
import concourse.mybir as mybir
from concourse.bass_utils import run_bass_kernel_spmd

F32 = mybir.dt.float32
BF16 = mybir.dt.bfloat16
ALU = mybir.AluOpType
AF = mybir.ActivationFunctionType
AX = mybir.AxisListType

H, DK, DV, GG, NG = 16, 128, 128, 16, 8
D = 2048
SCALE = DK ** -0.5
C = 128
TLOC = 2048
HALO = 128
NCH = TLOC // C
NBLK = 4
BT = 512
FS, FR, NE = 2816, 1408, 8
BIG = 1e30
EHALF = 512

ENG = ("sync", "gpsimd", "tensor", "vector", "scalar")


class S:
    __slots__ = ("h", "cnt", "name")

    def __init__(self, h, name):
        self.h, self.cnt, self.name = h, 0, name


class Bld:
    def __init__(self, nc, stack):
        self.nc, self.stack = nc, stack
        self.q = {e: [] for e in ENG}
        self.pe = self.sem("pe")
        self.ve = self.sem("ve")
        self.ac = self.sem("ac")
        self.cc = self.sem("cc")
        self.gp = self.sem("gp")
        self.Q = {}

    def sem(self, name):
        return S(self.stack.enter_context(self.nc.semaphore(name)), name)

    def P(self, s):
        return (s, s.cnt)

    def emit(self, eng, make, waits=(), inc=None, d=1):
        waits = tuple((s, v) for (s, v) in waits if s is not None and v > 0)
        for s, v in waits:
            assert v <= s.cnt, f"wait on future {s.name} {v}>{s.cnt}"
        tok = None
        if inc is not None:
            inc.cnt += d
            tok = (inc, inc.cnt)

        def f(e, waits=waits, inc=inc, d=d, make=make):
            for s, v in waits:
                e.wait_ge(s.h, v)
            r = make(e)
            if inc is not None:
                r.then_inc(inc.h, d)
        self.q[eng].append(f)
        return tok

    def dma(self, eng, out, in_, stream="ld", waits=()):
        if stream not in self.Q:
            self.Q[stream] = self.sem("q_" + stream)
        s = self.Q[stream]
        return self.emit(eng, lambda e, o=out, i=in_: e.dma_start(out=o, in_=i),
                         waits=waits, inc=s, d=16)

    def pe_mm(self, out, lhsT, rhs, start, stop, waits=(), inc=False):
        return self.emit(
            "tensor",
            lambda e, o=out, l=lhsT, r=rhs, st=start, sp=stop: e.matmul(
                out=o, lhsT=l, rhs=r, start=st, stop=sp),
            waits=waits, inc=self.pe if inc else None)

    def pe_tr(self, out, in_, ident, waits=()):
        return self.emit(
            "tensor",
            lambda e, o=out, i=in_, d=ident: e.transpose(out=o, in_=i, identity=d),
            waits=waits, inc=self.pe)

    def mm_group(self, parts, waits=()):
        n = len(parts)
        tok = None
        for i, (o, l, r, st, sp) in enumerate(parts):
            w = waits if i == 0 else ()
            tok = self.pe_mm(o, l, r, st, sp, waits=w, inc=(i == n - 1))
        return tok

    drains = False
    vdrains = False
    adrains = False

    def v(self, make, waits=(), dr=False):
        tok = self.emit("vector", make, waits=waits, inc=self.ve)
        if dr or self.drains or self.vdrains:
            self.q["vector"].append(lambda e: e.drain())
        return tok

    def a(self, make, waits=()):
        tok = self.emit("scalar", make, waits=waits, inc=self.ac)
        if self.drains or self.adrains:
            self.q["scalar"].append(lambda e: e.drain())
        return tok

    def g(self, make, waits=()):
        tok = self.emit("gpsimd", make, waits=waits, inc=self.gp)
        return tok

    def split_sg(self, src, hi, lo, waits=()):
        if os.environ.get("K_VSPLIT"):
            return self.split(src, hi, lo, waits=waits)
        t1 = self.a(lambda e, s=src, h=hi: e.copy(out=h, in_=s), waits=waits)
        t2 = self.g(lambda e, s=src, h=hi, l=lo: e.tensor_sub(out=l, in0=s, in1=h),
                    waits=[t1])
        return t2

    def split(self, src, hi, lo, waits=()):
        self.v(lambda e, s=src, h=hi: e.tensor_copy(out=h, in_=s), waits=waits,
               dr=True)
        return self.v(lambda e, s=src, h=hi, l=lo: e.tensor_sub(out=l, in0=s,
                                                                in1=h))

    def split_av(self, src, hi, lo, waits=()):
        t1 = self.a(lambda e, s=src, h=hi: e.copy(out=h, in_=s), waits=waits)
        return self.v(lambda e, s=src, h=hi, l=lo: e.tensor_sub(out=l, in0=s,
                                                                in1=h),
                      waits=[t1])


def host_constants():
    cs = {}
    cs["cident"] = np.eye(128, dtype=np.float32)
    cs["cuincl"] = np.triu(np.ones((C, C), np.float32))
    e1 = np.zeros((NG, DK), np.float32)
    for n in range(NG):
        e1[n, n * GG:(n + 1) * GG] = 1
    cs["ce1"] = e1
    e2 = np.zeros((H, DK), np.float32)
    for h in range(H):
        e2[h, h * NG:(h + 1) * NG] = 1
    cs["ce2"] = e2
    cs["conesr"] = np.ones((1, 128), np.float32)
    cs["conesc"] = np.ones((128, 1), np.float32)
    ii = np.arange(C)[:, None]
    jj = np.arange(C)[None, :]
    mA = np.where(ii > jj, 0.0, BIG).astype(np.float32)
    mM = np.where(jj >= ii, 0.0, -BIG).astype(np.float32)
    cs["cmaskA"] = np.tile(mA, (1, NG))
    cs["cmaskM"] = np.tile(mM, (1, NG))
    return cs


def build_program(dbg=False, upto="E", heads=None, blocks=None):
    import contextlib
    check = bool(os.environ.get("K_CHECK"))
    nc = bass.Bass("TRN2", target_bir_lowering=False,
                   detect_race_conditions=check)
    stack = contextlib.ExitStack()
    b = Bld(nc, stack)
    b.drains = check
    b.vdrains = bool(os.environ.get('K_VDRAIN'))
    b.adrains = bool(os.environ.get('K_ADRAIN'))
    PH = ("A", "B", "C", "D", "E")
    run = {p: PH.index(p) <= PH.index(upto) for p in PH}

    def dt(name, shape, kind="ExternalInput", dtype=F32):
        return nc.dram_tensor(name, shape, dtype, kind=kind).ap()

    xs = dt("xs", [HALO + TLOC, D])
    wq, wk, wv, wg, wo = (dt(n, [D, D]) for n in ("wq", "wk", "wv", "wg", "wo"))
    wf1 = dt("wf1", [D, 128])
    wf2 = dt("wf2", [128, 128])
    wb = dt("wb", [D, 16])
    convw = dt("convw", [128, 192])
    afm = dt("afm", [16, 128])
    dtb = dt("dtb", [128, 128])
    nwbc = dt("nwbc", [128, 128])
    bgc = dt("bgc", [128, 16])
    gatew = dt("gatew", [D, 8])
    sw1 = dt("sw1", [D, FS])
    sw2 = dt("sw2", [FS, D])
    sw3 = dt("sw3", [D, FS])
    rw1 = dt("rw1", [NE * D, FR])
    rw2 = dt("rw2", [NE * FR, D])
    rw3 = dt("rw3", [NE * D, FR])
    cident = dt("cident", [128, 128])
    cuincl = dt("cuincl", [128, 128])
    ce1 = dt("ce1", [8, 128])
    ce2 = dt("ce2", [16, 128])
    conesr = dt("conesr", [1, 128])
    conesc = dt("conesc", [128, 1])
    cmaskA = dt("cmaskA", [128, NG * C])
    cmaskM = dt("cmaskM", [128, NG * C])
    cisev = dt("cisev", [128, 1])
    cisod = dt("cisod", [128, 1])
    cactf = dt("cactf", [128, 8])
    spk = "ExternalOutput" if dbg else "Internal"
    sp_mst = dt("sp_mst", [H * NCH * 128, 128], spk)
    sp_ns = dt("sp_ns", [H * NCH * 128, 128], spk)
    sp_q2 = dt("sp_q2", [H * NCH * 128, 128], spk)
    sp_yl = dt("sp_yl", [H * NCH * 128, 128], spk)
    sp_sg = dt("sp_sg", [H * 128, TLOC], spk)
    sp_x1 = dt("sp_x1", [TLOC, D], spk)
    cc_in = nc.dram_tensor("cc_in", [H * 128, 128], F32)
    cc_out = nc.dram_tensor("cc_out", [H * 128, 128], F32)
    dbg_y = dt("dbg_y", [H * 128, TLOC], "ExternalOutput", BF16) if dbg else None
    dbg_send = dt("dbg_send", [H * 128, 128], "ExternalOutput") if dbg else None
    dbg_qkv = (dt("dbg_qkv", [3 * H * 128, TLOC], "ExternalOutput")
               if dbg and os.environ.get("K_DBG_QKV") else None)
    dbg_ck = (dt("dbg_ck", [H * NCH * 128, 4352], "ExternalOutput")
              if dbg and os.environ.get("K_DBG_CHUNK") else None)
    dbg_pre = (dt("dbg_pre", [3 * H * 128, 515 * NBLK], "ExternalOutput")
               if dbg and os.environ.get("K_DBG_PRE") else None)
    out = dt("out", [TLOC, D], kind="ExternalOutput")

    E = stack.enter_context
    import contextlib as _ctl
    stk0 = _ctl.ExitStack()
    stkA = _ctl.ExitStack()
    stkCD = _ctl.ExitStack()
    stkE = _ctl.ExitStack()
    E0 = stk0.enter_context
    EA = stkA.enter_context
    EC = stkCD.enter_context
    EE = stkE.enter_context

    pj = E(nc.psum_tensor("pj", [128, 512], F32))
    pj2 = E(nc.psum_tensor("pj2", [128, 512], F32))
    pg = E(nc.psum_tensor("pg", [128, 512], F32))
    pr = E(nc.psum_tensor("pr", [128, 1024], F32))
    pt = E(nc.psum_tensor("pt", [128, 512], F32))
    px = E(nc.psum_tensor("px", [128, 512], F32))
    ps = E(nc.psum_tensor("ps", [128, 512], F32))

    cs_id = E(nc.sbuf_tensor("cs_id", [128, 128], F32))
    cs_ui = E(nc.sbuf_tensor("cs_ui", [128, 128], F32))
    cs_e1 = E(nc.sbuf_tensor("cs_e1", [8, 128], F32))
    cs_e2 = E(nc.sbuf_tensor("cs_e2", [16, 128], F32))
    cs_or = E(nc.sbuf_tensor("cs_or", [1, 128], F32))
    cs_oc = E(nc.sbuf_tensor("cs_oc", [128, 1], F32))
    cs_nw = E(nc.sbuf_tensor("cs_nw", [128, 128], F32))
    cs_dtb = E(nc.sbuf_tensor("cs_dtb", [128, 128], F32))
    cs_bg = E(nc.sbuf_tensor("cs_bg", [128, 16], F32))
    cs_cw = E(nc.sbuf_tensor("cs_cw", [128, 192], F32))
    cs_afm = E(nc.sbuf_tensor("cs_afm", [16, 128], F32))
    cs_ascl = E(nc.sbuf_tensor("cs_ascl", [128, 128], F32))
    cs_iev = E(nc.sbuf_tensor("cs_iev", [128, 1], F32))
    cs_iod = E(nc.sbuf_tensor("cs_iod", [128, 1], F32))
    cs_af = E(nc.sbuf_tensor("cs_af", [128, 8], F32))
    wbh = E(nc.sbuf_tensor("wbh", [128, 256], BF16))
    wbl = E(nc.sbuf_tensor("wbl", [128, 256], BF16))
    wf1h = E(nc.sbuf_tensor("wf1h", [128, D], BF16))
    wf1l = E(nc.sbuf_tensor("wf1l", [128, D], BF16))
    cs_wf2 = E(nc.sbuf_tensor("cs_wf2", [128, 128], F32))
    gwh = E(nc.sbuf_tensor("gwh", [128, 128], BF16))
    gwl = E(nc.sbuf_tensor("gwl", [128, 128], BF16))
    c_t = E(nc.sbuf_tensor("c_t", [128, 32], F32))

    tmpw = E0(nc.sbuf_tensor("tmpw", [128, D + 256], F32))
    for dst, src in ((cs_id, cident), (cs_ui, cuincl), (cs_e1, ce1),
                     (cs_e2, ce2), (cs_or, conesr), (cs_oc, conesc),
                     (cs_nw, nwbc), (cs_dtb, dtb), (cs_bg, bgc),
                     (cs_cw, convw), (cs_afm, afm), (cs_iev, cisev),
                     (cs_iod, cisod), (cs_wf2, wf2), (cs_af, cactf)):
        b.dma("sync", dst[:, :], src[:, :])
    t_ld0 = b.P(b.Q["ld"])
    for k in range(16):
        b.dma("sync", tmpw[:, k * 16:(k + 1) * 16], wb[k * 128:(k + 1) * 128, :],
              stream="ldb")
    t_wb = b.P(b.Q["ldb"])
    b.split(tmpw[:, 0:256], wbh[:, :], wbl[:, :], waits=[t_wb])
    for k in range(16):
        b.dma("sync", tmpw[:, 256 + k * 128:256 + (k + 1) * 128],
              wf1[k * 128:(k + 1) * 128, :], stream="ldf")
    t_wf1 = b.P(b.Q["ldf"])
    t_splits = b.split(tmpw[:, 256:256 + D], wf1h[:, :], wf1l[:, :], waits=[t_wf1])
    gwtmp = E0(nc.sbuf_tensor("gwtmp", [128, 128], F32))
    for k in range(16):
        b.dma("sync", gwtmp[:, k * 8:(k + 1) * 8],
              gatew[k * 128:(k + 1) * 128, :], stream="ldg")
    t_gw = b.P(b.Q["ldg"])
    t_gws = b.split(gwtmp[:, :], gwh[:, :], gwl[:, :], waits=[t_gw])
    t_e = b.a(lambda e: e.activation(out=cs_afm[:, :], in_=cs_afm[:, :],
                                     func=AF.Exp), waits=[t_ld0])
    t_n = b.v(lambda e: e.tensor_scalar(out=cs_afm[:, :], in0=cs_afm[:, :],
                                        scalar1=-1.0, scalar2=None, op0=ALU.mult),
              waits=[t_e])
    t_m = b.pe_mm(ps[:, 0:128], cs_afm[:, :], cs_e2[:, :], True, True,
                  waits=[t_n, t_ld0], inc=True)
    t_consts = b.v(lambda e: e.tensor_copy(out=cs_ascl[:, :], in_=ps[:, 0:128]),
                   waits=[t_m])

    stk0.close()

    csbA = EA(nc.sbuf_tensor("csbA", [128, NG * C], F32))
    csbM = EA(nc.sbuf_tensor("csbM", [128, NG * C], F32))
    b.dma("sync", csbA[:, :], cmaskA[:, :], stream="ldm",
          waits=[t_splits, t_gws, t_consts])
    t_mask = b.dma("sync", csbM[:, :], cmaskM[:, :], stream="ldm")

    xlt_h = EA(nc.sbuf_tensor("xlt_h", [128, 16 * 640], BF16))
    xlt_l = EA(nc.sbuf_tensor("xlt_l", [128, 16 * 640], BF16))
    a_xt = EA(nc.sbuf_tensor("a_xt", [128, D], F32))
    a_tmp = EA(nc.sbuf_tensor("a_tmp", [128, D], F32))
    a_pre = EA(nc.sbuf_tensor("a_pre", [128, 515], F32))
    a_qf = EA(nc.sbuf_tensor("a_qf", [128, BT], F32))
    a_kf = EA(nc.sbuf_tensor("a_kf", [128, BT], F32))
    a_vf = EA(nc.sbuf_tensor("a_vf", [128, BT], F32))
    a_sq = EA(nc.sbuf_tensor("a_sq", [128, BT], F32))
    a_srow = EA(nc.sbuf_tensor("a_srow", [1, BT], F32))
    a_sg = EA(nc.sbuf_tensor("a_sg", [128, BT], F32))
    a_ghat = EA(nc.sbuf_tensor("a_ghat", [128, NBLK * 128], F32))
    a_beta = EA(nc.sbuf_tensor("a_beta", [128, NBLK * 16], F32))
    a_t1 = EA(nc.sbuf_tensor("a_t1", [128, BT], F32))
    a_wl = [EA(nc.sbuf_tensor(f"a_wl{i}", [128, D], F32)) for i in range(2)]
    a_wh = [EA(nc.sbuf_tensor(f"a_wh{i}", [128, D], BF16)) for i in range(2)]
    a_wlo = [EA(nc.sbuf_tensor(f"a_wlo{i}", [128, D], BF16)) for i in range(2)]
    a_kg = EA(nc.sbuf_tensor("a_kg", [16, NG * C], F32))
    a_qg = EA(nc.sbuf_tensor("a_qg", [16, NG * C], F32))
    c_gt = EA(nc.sbuf_tensor("c_gt", [128, 8], F32))
    c_gf = EA(nc.sbuf_tensor("c_gf", [8, 128], F32))
    c_gfl = EA(nc.sbuf_tensor("c_gfl", [1, NG * C], F32))
    c_ra = EA(nc.sbuf_tensor("c_ra", [128, NG * C], F32))
    c_rm = EA(nc.sbuf_tensor("c_rm", [128, NG * C], F32))
    c_A = EA(nc.sbuf_tensor("c_A", [128, 128], F32))
    c_At = EA(nc.sbuf_tensor("c_At", [128, 128], F32))
    c_Ap = EA(nc.sbuf_tensor("c_Ap", [128, 128], F32))
    c_ApT = EA(nc.sbuf_tensor("c_ApT", [128, 128], F32))
    c_Mt = EA(nc.sbuf_tensor("c_Mt", [128, 128], F32))
    c_eg = EA(nc.sbuf_tensor("c_eg", [128, 128], F32))
    c_kt = EA(nc.sbuf_tensor("c_kt", [128, 128], F32))
    c_qt = EA(nc.sbuf_tensor("c_qt", [128, 128], F32))
    c_kh = EA(nc.sbuf_tensor("c_kh", [128, 128], F32))
    c_khT = EA(nc.sbuf_tensor("c_khT", [128, 128], F32))
    c_X = EA(nc.sbuf_tensor("c_X", [128, 256], F32))
    c_r2 = EA(nc.sbuf_tensor("c_r2", [8, 128], F32))
    c_ms = EA(nc.sbuf_tensor("c_ms", [128, 128], F32))
    c_ns = EA(nc.sbuf_tensor("c_ns", [128, 128], F32))
    c_q2 = EA(nc.sbuf_tensor("c_q2", [128, 128], F32))
    c_yl = EA(nc.sbuf_tensor("c_yl", [128, 128], F32))
    c_tt = EA(nc.sbuf_tensor("c_tt", [128, 128], F32))
    c_tt2 = EA(nc.sbuf_tensor("c_tt2", [128, 128], F32))

    def ln_transpose(src_rows, xh, xl, colw, dst_col0, deps):
        tok_x = b.dma("sync", a_xt[:, :], src_rows, stream="xt", waits=deps)
        t1 = b.v(lambda e: e.tensor_reduce(out=c_t[:, 0:1], in_=a_xt[:, :],
                                           axis=AX.X, op=ALU.add), waits=[tok_x])
        t2 = b.a(lambda e: e.square(out=a_tmp[:, :], in_=a_xt[:, :]),
                 waits=[tok_x, b.P(b.ve)])
        t3 = b.v(lambda e: e.tensor_reduce(out=c_t[:, 1:2], in_=a_tmp[:, :],
                                           axis=AX.X, op=ALU.add), waits=[t2])
        t4 = b.v(lambda e: e.tensor_scalar(out=c_t[:, 2:3], in0=c_t[:, 0:1],
                                           scalar1=1.0 / D, scalar2=None,
                                           op0=ALU.mult), waits=[t1], dr=True)
        t5 = b.v(lambda e: e.tensor_tensor(out=c_t[:, 3:4], in0=c_t[:, 2:3],
                                           in1=c_t[:, 2:3], op=ALU.mult), dr=True)
        t6 = b.v(lambda e: e.scalar_tensor_tensor(out=c_t[:, 4:5], in0=c_t[:, 1:2],
                                                  scalar=1.0 / D, in1=c_t[:, 3:4],
                                                  op0=ALU.mult, op1=ALU.subtract),
                 waits=[t3])
        t7a = b.a(lambda e: e.activation(out=c_t[:, 31:32], in_=c_t[:, 4:5],
                                         func=AF.Sqrt, bias=cs_af[:, 0:1]),
                  waits=[t6])
        t7 = b.v(lambda e: e.reciprocal(out=c_t[:, 5:6], in_=c_t[:, 31:32]),
                 waits=[t7a], dr=True)
        t8 = b.v(lambda e: e.tensor_scalar(out=a_tmp[:, :], in0=a_xt[:, :],
                                           scalar1=c_t[:, 2:3], scalar2=c_t[:, 5:6],
                                           op0=ALU.subtract, op1=ALU.mult),
                 waits=[t7])
        tv = t8
        for k in range(16):
            q = k % 4
            tp = b.pe_tr(pt[:, q * 128:(q + 1) * 128],
                         a_tmp[:, k * 128:(k + 1) * 128], cs_id[:, :],
                         waits=[tv])
            col = k * colw + dst_col0
            tv = b.split_av(pt[:, q * 128:(q + 1) * 128], xh[:, col:col + 128],
                            xl[:, col:col + 128], waits=[tp])
        return tv

    def mm3pass(out_ap, out_tail, wh_s, wl_s, xcol0, ncols, waits,
                xh=None, xl=None, colw=640):
        xh = xh if xh is not None else xlt_h
        xl = xl if xl is not None else xlt_l
        parts = []
        n0 = min(ncols, 512)
        for k in range(16):
            xhi = xh[:, k * colw + xcol0: k * colw + xcol0 + n0]
            xlo = xl[:, k * colw + xcol0: k * colw + xcol0 + n0]
            first = (k == 0)
            for (wop, xop) in ((wh_s(k), xhi), (wh_s(k), xlo), (wl_s(k), xhi)):
                parts.append((out_ap, wop, xop, first, False))
                first = False
        parts[-1] = parts[-1][:4] + (True,)
        if ncols > 512:
            for k in range(16):
                xhi2 = xh[:, k * colw + xcol0 + 512: k * colw + xcol0 + ncols]
                xlo2 = xl[:, k * colw + xcol0 + 512: k * colw + xcol0 + ncols]
                f2 = (k == 0)
                for (wop, xop) in ((wh_s(k), xhi2), (wh_s(k), xlo2),
                                   (wl_s(k), xhi2)):
                    parts.append((out_tail, wop, xop, f2, False))
                    f2 = False
            parts[-1] = parts[-1][:4] + (True,)
        return b.mm_group(parts, waits=waits)

    CSTG = int(os.environ.get("K_CHUNK_STAGE", "9"))
    LB = {}

    def chunk_math(h, blk, c_loc, t_first):
        cg = (blk * NBLK + c_loc) + h * NCH
        if CSTG == 0:
            return
        tyl_prev_guard = (None, 0)
        if dbg_ck is not None and "dbgc" in b.Q:
            t_first = list(t_first) + [b.P(b.Q["dbgc"])]
        col = c_loc * C
        gsl = a_ghat[:, c_loc * 128 + h * NG: c_loc * 128 + h * NG + NG]
        bcol = a_beta[:, c_loc * 16 + h: c_loc * 16 + h + 1]
        for n in range(NG):
            b.dma("gpsimd", a_kg[:, n * C:(n + 1) * C],
                  a_kf[16 * n:16 * (n + 1), col:col + C], stream="rg",
                  waits=t_first if n == 0 else ())
        for n in range(NG):
            b.dma("gpsimd", a_qg[:, n * C:(n + 1) * C],
                  a_qf[16 * n:16 * (n + 1), col:col + C], stream="rg")
        t_rg = b.P(b.Q["rg"])
        t1 = b.pe_mm(px[:, 0:8], cs_ui[:, :], gsl, True, True,
                     waits=t_first, inc=True)
        t2 = b.v(lambda e: e.tensor_copy(out=c_gt[:, :], in_=px[:, 0:8]), waits=[t1])
        LB["px_t2"] = t2
        t3 = b.pe_tr(pt[0:8, 0:128], c_gt[:, :], cs_id[:, :], waits=[t2])
        t4 = b.v(lambda e: e.tensor_copy(out=c_gf[:, :], in_=pt[0:8, 0:128]),
                 waits=[t3])
        if CSTG == 1:
            return
        t5 = b.dma("gpsimd", c_gfl[:, :], c_gf[:, :], stream="fl", waits=[t4])
        b.pe_mm(pr[:, 0:512], cs_or[0:1, :], c_gfl[0:1, 0:512], True, True,
                waits=[t5])
        t7 = b.pe_mm(pr[:, 512:1024], cs_or[0:1, :], c_gfl[0:1, 512:1024],
                     True, True, inc=True)
        t8 = b.v(lambda e: e.tensor_tensor(out=c_ra[:, :], in0=pr[:, :],
                                           in1=csbA[:, :], op=ALU.add), waits=[t7])
        t9 = b.v(lambda e: e.tensor_tensor(out=c_rm[:, :], in0=pr[:, :],
                                           in1=csbM[:, :], op=ALU.add))
        for n in range(NG):
            s2 = slice(n * C, (n + 1) * C)
            gcol = c_gt[:, n:n + 1]
            b.v(lambda e, s2=s2, g=gcol: e.tensor_scalar(
                out=c_ra[:, s2], in0=c_ra[:, s2], scalar1=g, scalar2=-1.0,
                op0=ALU.subtract, op1=ALU.mult))
            b.v(lambda e, s2=s2, g=gcol: e.tensor_scalar(
                out=c_rm[:, s2], in0=c_rm[:, s2], scalar1=g, scalar2=None,
                op0=ALU.subtract))
        tv = b.P(b.ve)
        b.a(lambda e: e.activation(out=c_ra[:, 0:512], in_=c_ra[:, 0:512],
                                   func=AF.Exp), waits=[tv])
        ea2 = b.a(lambda e: e.activation(out=c_ra[:, 512:1024], in_=c_ra[:, 512:1024],
                                         func=AF.Exp))
        b.a(lambda e: e.activation(out=c_rm[:, 0:512], in_=c_rm[:, 0:512],
                                   func=AF.Exp))
        em2 = b.a(lambda e: e.activation(out=c_rm[:, 512:1024], in_=c_rm[:, 512:1024],
                                         func=AF.Exp))
        if CSTG == 2:
            return
        tA = tM = None
        for wv in range(4):
            mwaits = [ea2, em2, b.P(b.ve)]
            toks = []
            for i, n in enumerate((wv * 2, wv * 2 + 1)):
                ksl = a_kg[:, n * C:(n + 1) * C]
                qsl = a_qg[:, n * C:(n + 1) * C]
                toks.append(b.pe_mm(pg[:, i * 256:i * 256 + 128], ksl, ksl,
                                    True, True,
                                    waits=(mwaits + [t_rg]) if i == 0 else (),
                                    inc=True))
                toks.append(b.pe_mm(pg[:, i * 256 + 128:i * 256 + 256], ksl, qsl,
                                    True, True, inc=True))
            for i, n in enumerate((wv * 2, wv * 2 + 1)):
                s2 = slice(n * C, (n + 1) * C)
                pK = pg[:, i * 256:i * 256 + 128]
                pQ = pg[:, i * 256 + 128:i * 256 + 256]
                wv_done = [toks[-1]]
                if n == 0:
                    tA = b.v(lambda e, s2=s2, p=pK: e.tensor_tensor(
                        out=c_A[:, :], in0=p, in1=c_ra[:, s2], op=ALU.mult),
                        waits=wv_done)
                    tM = b.v(lambda e, s2=s2, p=pQ: e.tensor_tensor(
                        out=c_Mt[:, :], in0=p, in1=c_rm[:, s2], op=ALU.mult))
                else:
                    b.v(lambda e, s2=s2, p=pK: e.tensor_tensor(
                        out=c_tt[:, :], in0=p, in1=c_ra[:, s2], op=ALU.mult),
                        waits=wv_done)
                    b.v(lambda e, s2=s2, p=pQ: e.tensor_tensor(
                        out=c_tt2[:, :], in0=p, in1=c_rm[:, s2], op=ALU.mult))
                    tA = b.v(lambda e: e.tensor_tensor(
                        out=c_A[:, :], in0=c_A[:, :], in1=c_tt[:, :], op=ALU.add))
                    tM = b.v(lambda e: e.tensor_tensor(
                        out=c_Mt[:, :], in0=c_Mt[:, :], in1=c_tt2[:, :],
                        op=ALU.add))
        tA = b.v(lambda e, g=bcol: e.tensor_scalar(out=c_A[:, :], in0=c_A[:, :],
                                                   scalar1=g, scalar2=None,
                                                   op0=ALU.mult), waits=[tA])
        if CSTG == 3:
            return
        t10 = b.pe_mm(px[:, 128:256], cs_e1[:, :], c_gf[:, :], True, True,
                      waits=[t4, tA], inc=True)
        t11 = b.a(lambda e: e.activation(out=c_eg[:, :], in_=px[:, 128:256],
                                         func=AF.Exp), waits=[t10])
        t12 = b.v(lambda e: e.tensor_scalar(out=c_r2[:, :], in0=c_gf[:, :],
                                            scalar1=c_gf[:, 127:128], scalar2=-1.0,
                                            op0=ALU.subtract, op1=ALU.mult),
                  waits=[t4])
        t13 = b.pe_mm(px[:, 256:384], cs_e1[:, :], c_r2[:, :], True, True,
                      waits=[t12, t11], inc=True)
        t14 = b.a(lambda e: e.activation(out=c_kh[:, :], in_=px[:, 256:384],
                                         func=AF.Exp), waits=[t13])
        LB["px_act"] = t14
        kslf = a_kf[:, col:col + C]
        qslf = a_qf[:, col:col + C]
        vslf = a_vf[:, col:col + C]
        t15 = b.v(lambda e: e.tensor_tensor(out=c_kt[:, :], in0=kslf, in1=c_eg[:, :],
                                            op=ALU.mult), waits=[t11])
        t16 = b.v(lambda e: e.tensor_tensor(out=c_qt[:, :], in0=qslf, in1=c_eg[:, :],
                                            op=ALU.mult))
        t17 = b.v(lambda e: e.tensor_tensor(out=c_kh[:, :], in0=kslf, in1=c_kh[:, :],
                                            op=ALU.mult), waits=[t14])
        if CSTG == 4:
            return
        t18 = b.pe_tr(pt[:, 0:128], c_kt[:, :], cs_id[:, :], waits=[t15])
        t19 = b.v(lambda e, g=bcol: e.tensor_scalar(out=c_X[:, 0:128],
                                                    in0=pt[:, 0:128], scalar1=g,
                                                    scalar2=None, op0=ALU.mult),
                  waits=[t18])
        t20 = b.pe_tr(pt[:, 128:256], vslf, cs_id[:, :], waits=[t19])
        t21 = b.v(lambda e, g=bcol: e.tensor_scalar(out=c_X[:, 128:256],
                                                    in0=pt[:, 128:256], scalar1=g,
                                                    scalar2=None, op0=ALU.mult),
                  waits=[t20])
        t22 = b.pe_tr(pt[:, 256:384], c_kh[:, :], cs_id[:, :], waits=[t17, t21])
        t23 = b.v(lambda e: e.tensor_copy(out=c_khT[:, :], in_=pt[:, 256:384]),
                  waits=[t22])
        t24 = b.pe_tr(pt[:, 384:512], c_A[:, :], cs_id[:, :], waits=[tA, t23])
        t25 = b.v(lambda e: e.tensor_copy(out=c_At[:, :], in_=pt[:, 384:512]),
                  waits=[t24])
        if CSTG == 5:
            return
        t26 = b.pe_mm(ps[:, 0:256], c_At[:, :], c_X[:, :], True, True,
                      waits=[t25], inc=True)
        tX = b.v(lambda e: e.tensor_sub(out=c_X[:, :], in0=c_X[:, :],
                                        in1=ps[:, 0:256]), waits=[t26])
        cur, curT = c_A, c_At
        nxt, nxtT = c_Ap, c_ApT
        lvl = 2
        while lvl < C:
            tq = b.pe_mm(ps[:, 256:384], curT[:, :], cur[:, :], True, True,
                         waits=[tX], inc=True)
            te = b.v(lambda e, nx=nxt: e.tensor_copy(out=nx[:, :], in_=ps[:, 256:384]),
                     waits=[tq])
            tt_ = b.pe_tr(pt[:, 384:512], nxt[:, :], cs_id[:, :], waits=[te])
            te2 = b.v(lambda e, nx=nxtT: e.tensor_copy(out=nx[:, :],
                                                       in_=pt[:, 384:512]),
                      waits=[tt_])
            ta = b.pe_mm(ps[:, 0:256], nxtT[:, :], c_X[:, :], True, True,
                         waits=[te2], inc=True)
            tX = b.v(lambda e: e.tensor_add(out=c_X[:, :], in0=c_X[:, :],
                                            in1=ps[:, 0:256]), waits=[ta])
            cur, curT, nxt, nxtT = nxt, nxtT, cur, curT
            lvl *= 2
        LB["ps_solve"] = tX
        if CSTG == 6:
            return
        lamc = c_eg[:, 127:128]
        m1 = b.pe_mm(pg[:, 0:128], c_X[:, 0:128], c_khT[:, :], True, True,
                     waits=[tX], inc=True)
        tms0 = b.v(lambda e, l=lamc: e.tensor_scalar(out=c_tt[:, :],
                                                     in0=cs_id[:, :],
                                                     scalar1=l, scalar2=None,
                                                     op0=ALU.mult), waits=[t11],
                   dr=True)
        tms = b.v(lambda e: e.tensor_sub(out=c_ms[:, :], in0=c_tt[:, :],
                                         in1=pg[:, 0:128]),
                  waits=[m1] + ([b.P(b.Q["sp"])] if "sp" in b.Q else []))
        m2 = b.pe_mm(pg[:, 128:256], c_khT[:, :], c_X[:, 128:256], True, True,
                     waits=[tms], inc=True)
        tns = b.v(lambda e: e.tensor_copy(out=c_ns[:, :], in_=pg[:, 128:256]),
                  waits=[m2])
        m3 = b.pe_mm(pg[:, 256:384], c_X[:, 0:128], c_Mt[:, :], True, True,
                     waits=[tns], inc=True)
        tq2 = b.v(lambda e: e.tensor_sub(out=c_q2[:, :], in0=c_qt[:, :],
                                         in1=pg[:, 256:384]), waits=[m3, t16])
        m4 = b.pe_mm(pg[:, 384:512], c_Mt[:, :], c_X[:, 128:256], True, True,
                     waits=[tq2], inc=True)
        tyl = b.v(lambda e: e.tensor_copy(out=c_yl[:, :], in_=pg[:, 384:512]),
                  waits=[m4])
        LB["pg_chunk"] = tyl
        r0 = cg * 128
        if dbg_ck is not None:
            for col, (src, tok, w_) in enumerate((
                    (c_gt[:, 0:8], t2, 8), (c_eg[:, :], t11, 128),
                    (c_A[:, :], tA, 128), (c_Mt[:, :], tM, 128),
                    (c_X[:, :], tX, 256), (c_kt[:, :], t15, 128),
                    (c_kh[:, :], t17, 128), (c_khT[:, :], t23, 128),
                    (c_qt[:, :], t16, 128), (c_ra[:, :], ea2, 1024),
                    (c_rm[:, :], em2, 1024), (a_kg[:, :], t_rg, 1024))):
                off = [0, 8, 136, 264, 392, 648, 776, 904, 1032, 1160,
                       2184, 3208][col]
                b.dma("gpsimd", dbg_ck[r0:r0 + (128 if col != 11 else 16),
                                       off:off + w_],
                      src, stream="dbgc", waits=[tX, tyl_prev_guard])
        b.dma("gpsimd", sp_mst[r0:r0 + 128, :], c_ms[:, :], stream="sp", waits=[tms])
        b.dma("gpsimd", sp_ns[r0:r0 + 128, :], c_ns[:, :], stream="sp", waits=[tns])
        b.dma("gpsimd", sp_q2[r0:r0 + 128, :], c_q2[:, :], stream="sp", waits=[tq2])
        b.dma("gpsimd", sp_yl[r0:r0 + 128, :], c_yl[:, :], stream="sp", waits=[tyl])

    HSEL = list(range(H)) if heads is None else heads
    BSEL = list(range(NBLK)) if blocks is None else blocks
    if run["A"]:
        for blk in BSEL:
            deps = [t_consts, t_splits, t_gws, t_mask, b.P(b.pe), b.P(b.ve),
                    b.P(b.ac)]
            if "sp" in b.Q:
                deps.append(b.P(b.Q["sp"]))
            t_x = None
            for i in range(5):
                row0 = blk * BT + i * 128
                t_x = ln_transpose(xs[row0:row0 + 128, :], xlt_h, xlt_l, 640,
                                   i * 128, deps if i == 0 else [t_x])
            tgh = mm3pass(pj[:, 0:512], None,
                          lambda k: wf1h[:, k * 128:(k + 1) * 128],
                          lambda k: wf1l[:, k * 128:(k + 1) * 128], 128, 512,
                          waits=[t_x, b.P(b.ac)])
            tev = b.v(lambda e: e.tensor_copy(out=a_t1[:, :], in_=pj[:, 0:512]),
                      waits=[tgh])
            tv1 = None
            for cl in range(4):
                tm = b.pe_mm(pg[:, cl * 128:(cl + 1) * 128],
                             a_t1[:, cl * 128:(cl + 1) * 128], cs_wf2[:, :],
                             True, True, waits=[tev] if cl == 0 else [tv1],
                             inc=True)
                tv1 = b.v(lambda e, cl=cl: e.tensor_tensor(
                    out=a_ghat[:, cl * 128:(cl + 1) * 128],
                    in0=pg[:, cl * 128:(cl + 1) * 128], in1=cs_dtb[:, :],
                    op=ALU.add), waits=[tm])
                tv2a = b.a(lambda e, cl=cl: e.activation(
                    out=a_ghat[:, cl * 128:(cl + 1) * 128],
                    in_=a_ghat[:, cl * 128:(cl + 1) * 128], func=AF.Exp),
                    waits=[tv1])
                tv2 = b.a(lambda e, cl=cl: e.activation(
                    out=a_ghat[:, cl * 128:(cl + 1) * 128],
                    in_=a_ghat[:, cl * 128:(cl + 1) * 128], func=AF.Ln,
                    bias=cs_af[:, 4:5]), waits=[tv2a])
                b.v(lambda e, cl=cl: e.tensor_tensor(
                    out=a_ghat[:, cl * 128:(cl + 1) * 128],
                    in0=a_ghat[:, cl * 128:(cl + 1) * 128], in1=cs_ascl[:, :],
                    op=ALU.mult), waits=[tv2])
            tb_prev = None
            for cl in range(4):
                xc = 128 + cl * 128
                parts = []
                for k in range(16):
                    xhi = xlt_h[:, k * 640 + xc: k * 640 + xc + 128]
                    xlo = xlt_l[:, k * 640 + xc: k * 640 + xc + 128]
                    whi = wbh[:, k * 16:(k + 1) * 16]
                    wlo = wbl[:, k * 16:(k + 1) * 16]
                    for (xo, wop) in ((xhi, whi), (xlo, whi), (xhi, wlo)):
                        parts.append((px[:, 384 + cl * 16:400 + cl * 16],
                                      xo, wop, False, False))
                parts[0] = parts[0][:3] + (True, False)
                parts[-1] = parts[-1][:4] + (True,)
                tb = b.mm_group(parts, waits=[t_x, b.P(b.ac)] if cl == 0
                                else [tb_prev])
                tb_prev = b.a(lambda e, cl=cl: e.activation(
                    out=a_beta[:, cl * 16:(cl + 1) * 16],
                    in_=px[:, 384 + cl * 16:400 + cl * 16], func=AF.Sigmoid),
                    waits=[tb])
            wpar = 0
            land_free = [None, None]
            hilo_free = [None, None]
            lastA = {}

            def load_w(srcfn):
                nonlocal wpar
                par = wpar
                wpar = 1 - wpar
                w0 = [land_free[par]] if land_free[par] else []
                for k in range(16):
                    b.dma("sync", a_wl[par][:, k * 128:(k + 1) * 128], srcfn(k),
                          stream=f"wl{par}", waits=w0 if k == 0 else ())
                t_w = b.P(b.Q[f"wl{par}"])
                ws = [t_w] + ([hilo_free[par]] if hilo_free[par] else [])
                t_ws = b.split_sg(a_wl[par][:, :], a_wh[par][:, :],
                                  a_wlo[par][:, :], waits=ws)
                land_free[par] = t_ws
                return par, t_ws

            for h in HSEL:
                conv_done = []
                for ti, (wt, cw_off) in enumerate(((wq, 0), (wk, 64), (wv, 128))):
                    par, t_ws = load_w(lambda k, wt=wt, hh=h: wt[
                        k * 128:(k + 1) * 128, hh * 128:(hh + 1) * 128])
                    tp = mm3pass(pj[:, 0:512], px[:, 0:3],
                                 lambda k, p=par: a_wh[p][:, k * 128:(k + 1) * 128],
                                 lambda k, p=par: a_wlo[p][:, k * 128:(k + 1) * 128],
                                 125, 515,
                                 waits=[t_ws, t_x, lastA.get("pj", (None, 0)),
                                        lastA.get("px3", (None, 0)),
                                        LB.get("px_t2", (None, 0)),
                                        LB.get("px_act", (None, 0))])
                    hilo_free[par] = tp
                    te1 = b.v(lambda e: e.tensor_copy(out=a_pre[:, 0:512],
                                                      in_=pj[:, 0:512]),
                              waits=[tp])
                    te2 = b.v(lambda e: e.tensor_copy(out=a_pre[:, 512:515],
                                                      in_=px[:, 0:3]))
                    lastA["pj"] = te1
                    lastA["px3"] = te2
                    dst = (a_qf, a_kf, a_vf)[ti]
                    t_cv = b.v(lambda e, d=dst, co=cw_off, hh=h: e.tensor_scalar(
                        out=d[:, :], in0=a_pre[:, 0:512],
                        scalar1=cs_cw[:, co + hh * 4: co + hh * 4 + 1],
                        scalar2=None, op0=ALU.mult),
                        waits=[te1, te2,
                               b.P(b.Q["rg"]) if "rg" in b.Q else (None, 0)],
                        dr=True)
                    for j in range(1, 4):
                        t_cv = b.v(lambda e, d=dst, co=cw_off, j=j, hh=h:
                                   e.scalar_tensor_tensor(
                                       out=d[:, :], in0=a_pre[:, j:j + 512],
                                       scalar=cs_cw[:, co + hh * 4 + j:
                                                    co + hh * 4 + j + 1],
                                       in1=d[:, :], op0=ALU.mult, op1=ALU.add),
                                   dr=(j < 3))
                    t_si = b.a(lambda e, d=dst: e.activation(out=d[:, :],
                                                             in_=d[:, :],
                                                             func=AF.Silu),
                               waits=[t_cv])
                    conv_done.append(t_si)
                for dst, t_pr, bcol_, scol_ in ((a_qf, conv_done[0], 1, 3),
                                                (a_kf, conv_done[1], 2, 4)):
                    tsq = b.a(lambda e, d=dst: e.square(out=a_sq[:, :],
                                                        in_=d[:, :]),
                              waits=[t_pr])
                    tmm = b.pe_mm(ps[0:1, 0:512], cs_oc[:, 0:1], a_sq[:, :],
                                  True, True,
                                  waits=[tsq, lastA.get("ps_row", (None, 0)),
                                         LB.get("ps_solve", (None, 0))],
                                  inc=True)
                    trsa = b.a(lambda e, bc=bcol_, sc=scol_: e.activation(
                        out=a_srow[0:1, :], in_=ps[0:1, 0:512], func=AF.Sqrt,
                        bias=cs_af[0:1, bc:bc + 1],
                        scale=cs_af[0:1, sc:sc + 1]), waits=[tmm])
                    lastA["ps_row"] = trsa
                    trs = b.v(lambda e: e.reciprocal(out=a_srow[0:1, :],
                                                     in_=a_srow[0:1, :]),
                              waits=[trsa])
                    tbc = b.pe_mm(pg[:, 0:512], cs_or[0:1, :], a_srow[0:1, :],
                                  True, True,
                                  waits=[trs, lastA.get("pg_l2", (None, 0)),
                                         LB.get("pg_chunk", (None, 0))],
                                  inc=True)
                    tml = b.v(lambda e, d=dst: e.tensor_tensor(
                        out=d[:, :], in0=d[:, :], in1=pg[:, 0:512],
                        op=ALU.mult), waits=[tbc])
                    lastA["pg_l2"] = tml
                t_qk = b.P(b.ve)
                if dbg_qkv is not None:
                    for ti2, src2 in enumerate((a_qf, a_kf, a_vf)):
                        b.dma("gpsimd",
                              dbg_qkv[(ti2 * H + h) * 128:(ti2 * H + h + 1) * 128,
                                      blk * BT:(blk + 1) * BT],
                              src2[:, :], stream="dbgq",
                              waits=[t_qk, conv_done[2]])
                par, t_ws = load_w(lambda k, hh=h: wg[k * 128:(k + 1) * 128,
                                                      hh * 128:(hh + 1) * 128])
                tg = mm3pass(pj2[:, 0:512], None,
                             lambda k, p=par: a_wh[p][:, k * 128:(k + 1) * 128],
                             lambda k, p=par: a_wlo[p][:, k * 128:(k + 1) * 128],
                             128, 512,
                             waits=[t_ws, lastA.get("pj2", (None, 0))])
                hilo_free[par] = tg
                tsg = b.a(lambda e, hh=h: e.activation(out=a_sg[:, :],
                                                       in_=pj2[:, 0:512],
                                                       func=AF.Silu,
                                                       bias=cs_bg[:, hh:hh + 1]),
                          waits=[tg, b.P(b.Q["sg"]) if "sg" in b.Q else (None, 0)])
                lastA["pj2"] = tsg
                b.dma("gpsimd", sp_sg[h * 128:(h + 1) * 128,
                                      blk * BT:(blk + 1) * BT],
                      a_sg[:, :], stream="sg", waits=[tsg])
                t_in = [t_qk, conv_done[2], b.P(b.ac)]
                for cl in range(4):
                    chunk_math(h, blk, cl, t_in if cl == 0 else
                               ([b.P(b.Q["sp"])] if "sp" in b.Q else t_in))

    stkA.close()

    b_s = [E(nc.sbuf_tensor(f"b_s{i}", [128, 128], F32)) for i in range(2)]
    b_m = [E(nc.sbuf_tensor(f"b_m{i}", [128, 128], F32)) for i in range(2)]
    b_n = [E(nc.sbuf_tensor(f"b_n{i}", [128, 128], F32)) for i in range(2)]
    if run["B"]:
        for h in range(H):
            tz = b.v(lambda e: e.memset(b_s[0][:, :], 0.0),
                     waits=[b.P(b.Q["sp"]), b.P(b.Q["sg"]), b.P(b.pe)]
                     if h == 0 else
                     ([b.P(b.Q["ccin"])] if "ccin" in b.Q else []))
            t_pv = tz
            for ci in range(NCH):
                cg = h * NCH + ci
                par = ci % 2
                r0 = cg * 128
                tm1 = b.dma("sync", b_m[par][:, :], sp_mst[r0:r0 + 128, :],
                            stream=f"pb{par}", waits=[t_pv])
                tm2 = b.dma("sync", b_n[par][:, :], sp_ns[r0:r0 + 128, :],
                            stream=f"pb{par}")
                sprev = b_s[ci % 2]
                snext = b_s[1 - ci % 2]
                tmm = b.pe_mm(px[:, 0:128], b_m[par][:, :], sprev[:, :],
                              True, True, waits=[tm1, t_pv], inc=True)
                t_pv = b.v(lambda e, sn=snext, pn=b_n[par]: e.tensor_add(
                    out=sn[:, :], in0=pn[:, :], in1=px[:, 0:128]),
                    waits=[tmm, tm2], dr=(ci == NCH - 1))
            send = b_s[0]
            tsc = b.v(lambda e, s=send: e.tensor_scalar(
                out=s[:, :], in0=s[:, :], scalar1=cs_iev[:, 0:1], scalar2=None,
                op0=ALU.mult), waits=[t_pv])
            b.dma("gpsimd", cc_in.ap()[h * 128:(h + 1) * 128, :], send[:, :],
                  stream="ccin", waits=[tsc])
            if dbg:
                b.dma("gpsimd", dbg_send[h * 128:(h + 1) * 128, :], send[:, :],
                      stream="dbg")
        b.emit("gpsimd", lambda e: e.collective_compute(
            "AllReduce", ALU.add,
            replica_groups=[[0, 1], [2, 3], [4, 5], [6, 7]],
            ins=[cc_in.ap().opt()], outs=[cc_out.ap().opt()]),
            waits=[b.P(b.Q["ccin"])], inc=b.cc, d=1)

    yslab_h = EC(nc.sbuf_tensor("yslab_h", [128, H * TLOC], BF16))
    yslab_l = EC(nc.sbuf_tensor("yslab_l", [128, H * TLOC], BF16))
    c_rl = [[EC(nc.sbuf_tensor(f"c_rl{i}_{j}", [128, 128], F32)) for j in range(4)]
            for i in range(2)]
    c_sgC = [EC(nc.sbuf_tensor(f"c_sgC{i}", [128, BT], F32)) for i in range(2)]
    c_O = EC(nc.sbuf_tensor("c_O", [128, 128], F32))
    c_y = EC(nc.sbuf_tensor("c_y", [128, 128], F32))
    c_y2 = EC(nc.sbuf_tensor("c_y2", [128, 128], F32))
    if run["C"]:
        for h in range(H):
            tl = b.dma("sync", b_s[0][:, :], cc_out.ap()[h * 128:(h + 1) * 128, :],
                       stream="si", waits=[(b.cc, 1), b.P(b.ve)])
            t_pv = b.v(lambda e: e.tensor_scalar(out=b_s[0][:, :], in0=b_s[0][:, :],
                                                 scalar1=cs_iod[:, 0:1],
                                                 scalar2=None, op0=ALU.mult),
                       waits=[tl])
            tsg = None
            for ci in range(NCH):
                cg = h * NCH + ci
                par = ci % 2
                r0 = cg * 128
                rl = c_rl[par]
                tr1 = b.dma("sync", rl[0][:, :], sp_mst[r0:r0 + 128, :],
                            stream=f"pc{par}", waits=[t_pv])
                tr2 = b.dma("sync", rl[1][:, :], sp_ns[r0:r0 + 128, :],
                            stream=f"pc{par}")
                tr3 = b.dma("sync", rl[2][:, :], sp_q2[r0:r0 + 128, :],
                            stream=f"pc{par}")
                tr4 = b.dma("sync", rl[3][:, :], sp_yl[r0:r0 + 128, :],
                            stream=f"pc{par}")
                if ci % 4 == 0:
                    tsg = b.dma("sync", c_sgC[(ci // 4) % 2][:, :],
                                sp_sg[h * 128:(h + 1) * 128,
                                      (ci // 4) * BT:(ci // 4 + 1) * BT],
                                stream="sgc", waits=[b.P(b.pe)])
                sgb = c_sgC[(ci // 4) % 2]
                sprev = b_s[ci % 2]
                snext = b_s[1 - ci % 2]
                to1 = b.pe_mm(pg[:, 0:128], rl[2][:, :], sprev[:, :], True, True,
                              waits=[tr3, t_pv], inc=True)
                to2 = b.v(lambda e, r=rl[3]: e.tensor_add(out=c_O[:, :],
                                                          in0=pg[:, 0:128],
                                                          in1=r[:, :]),
                          waits=[to1, tr4])
                tc1 = b.pe_mm(pg[:, 128:256], rl[0][:, :], sprev[:, :], True, True,
                              waits=[tr1, to2], inc=True)
                t_pv = b.v(lambda e, sn=snext, r=rl[1]: e.tensor_add(
                    out=sn[:, :], in0=r[:, :], in1=pg[:, 128:256]),
                    waits=[tc1, tr2])
                tt1 = b.pe_tr(pt[:, 0:128],
                              sgb[:, (ci % 4) * 128:(ci % 4 + 1) * 128],
                              cs_id[:, :],
                              waits=[tsg, to2] if ci % 4 == 0 else [to2])
                ty1 = b.v(lambda e: e.tensor_tensor(out=c_y[:, :], in0=c_O[:, :],
                                                    in1=pt[:, 0:128], op=ALU.mult),
                          waits=[tt1], dr=True)
                ty2 = b.v(lambda e: e.tensor_tensor(out=c_y2[:, :], in0=c_y[:, :],
                                                    in1=c_y[:, :], op=ALU.mult),
                          dr=True)
                ty3 = b.v(lambda e: e.tensor_reduce(out=c_t[:, 6:7], in_=c_y2[:, :],
                                                    axis=AX.X, op=ALU.add),
                          waits=[ty2])
                ty4a = b.a(lambda e: e.activation(out=c_t[:, 30:31],
                                                  in_=c_t[:, 6:7],
                                                  func=AF.Sqrt,
                                                  bias=cs_af[:, 0:1],
                                                  scale=cs_af[:, 5:6]),
                           waits=[ty3])
                ty4 = b.v(lambda e: e.reciprocal(out=c_t[:, 7:8],
                                                 in_=c_t[:, 30:31]),
                          waits=[ty4a], dr=True)
                ty5 = b.v(lambda e: e.scalar_tensor_tensor(
                    out=c_y2[:, :], in0=c_y[:, :], scalar=c_t[:, 7:8],
                    in1=cs_nw[:, :], op0=ALU.mult, op1=ALU.mult), waits=[ty4])
                tt2 = b.pe_tr(pt[:, 128:256], c_y2[:, :], cs_id[:, :], waits=[ty5])
                colh = h * TLOC + ci * 128
                b.split_av(pt[:, 128:256], yslab_h[:, colh:colh + 128],
                           yslab_l[:, colh:colh + 128], waits=[tt2])
        if dbg:
            tA = b.P(b.ve)
            for h in range(H):
                b.dma("gpsimd", dbg_y[h * 128:(h + 1) * 128, :],
                      yslab_h[:, h * TLOC:(h + 1) * TLOC], stream="dbg",
                      waits=[tA] if h == 0 else ())

    d_wl = EC(nc.sbuf_tensor("d_wl", [128, 16 * 256], F32))
    d_wh = EC(nc.sbuf_tensor("d_wh", [128, 16 * 256], BF16))
    d_wlo = EC(nc.sbuf_tensor("d_wlo", [128, 16 * 256], BF16))
    d_xr = [EC(nc.sbuf_tensor(f"d_xr{i}", [128, 256], F32)) for i in range(2)]
    d_x1 = [EC(nc.sbuf_tensor(f"d_x1{i}", [128, 256], F32)) for i in range(2)]
    if run["D"]:
        t_y = b.P(b.ve)
        for dc in range(8):
            for k in range(16):
                b.dma("sync", d_wl[:, k * 256:(k + 1) * 256],
                      wo[k * 128:(k + 1) * 128, dc * 256:(dc + 1) * 256],
                      stream="wo", waits=[b.P(b.ve)] if k == 0 else ())
            t_w = b.P(b.Q["wo"])
            t_ws = b.split_sg(d_wl[:, :], d_wh[:, :], d_wlo[:, :],
                              waits=[t_w, b.P(b.pe)])
            for tt in range(16):
                par = tt % 2
                txr = b.dma("sync", d_xr[par][:, :],
                            xs[HALO + tt * 128:HALO + (tt + 1) * 128,
                               dc * 256:(dc + 1) * 256],
                            stream=f"xr{par}", waits=[b.P(b.ve)])
                parts = []
                for k in range(16):
                    yh = yslab_h[:, k * TLOC + tt * 128: k * TLOC + (tt + 1) * 128]
                    yl2 = yslab_l[:, k * TLOC + tt * 128: k * TLOC + (tt + 1) * 128]
                    wh2 = d_wh[:, k * 256:(k + 1) * 256]
                    wl2 = d_wlo[:, k * 256:(k + 1) * 256]
                    for (yo, wop) in ((yh, wh2), (yh, wl2), (yl2, wh2)):
                        parts.append((pj[:, 0:256], yo, wop, False, False))
                parts[0] = parts[0][:3] + (True, False)
                parts[-1] = parts[-1][:4] + (True,)
                tm = b.mm_group(parts, waits=[t_ws, t_y, b.P(b.ve)])
                tev = b.v(lambda e, p=par: e.tensor_add(out=d_x1[p][:, :],
                                                        in0=pj[:, 0:256],
                                                        in1=d_xr[p][:, :]),
                          waits=[tm, txr,
                                 b.P(b.Q["x1"]) if "x1" in b.Q else (None, 0)])
                b.dma("gpsimd", sp_x1[tt * 128:(tt + 1) * 128,
                                      dc * 256:(dc + 1) * 256],
                      d_x1[par][:, :], stream="x1", waits=[tev])

    stkCD.close()

    e_acc = EE(nc.sbuf_tensor("e_acc", [128, (EHALF // 128) * D], F32))
    e_x2h = EE(nc.sbuf_tensor("e_x2h", [128, 16 * EHALF], BF16))
    e_x2l = EE(nc.sbuf_tensor("e_x2l", [128, 16 * EHALF], BF16))
    e_t = EE(nc.sbuf_tensor("e_t", [128, D], F32))
    e_h2h = EE(nc.sbuf_tensor("e_h2h", [128, 2 * EHALF], BF16))
    e_h2l = EE(nc.sbuf_tensor("e_h2l", [128, 2 * EHALF], BF16))
    e_w1l = [EE(nc.sbuf_tensor(f"e_w1l{i}", [128, D], F32)) for i in range(2)]
    e_w1h = [EE(nc.sbuf_tensor(f"e_w1h{i}", [128, D], BF16)) for i in range(2)]
    e_w1o = [EE(nc.sbuf_tensor(f"e_w1o{i}", [128, D], BF16)) for i in range(2)]
    e_w3l = [EE(nc.sbuf_tensor(f"e_w3l{i}", [128, D], F32)) for i in range(2)]
    e_w3h = [EE(nc.sbuf_tensor(f"e_w3h{i}", [128, D], BF16)) for i in range(2)]
    e_w3o = [EE(nc.sbuf_tensor(f"e_w3o{i}", [128, D], BF16)) for i in range(2)]
    e_w2l = EE(nc.sbuf_tensor("e_w2l", [128, 2 * D], F32))
    e_w2h = EE(nc.sbuf_tensor("e_w2h", [128, 2 * D], BF16))
    e_w2o = EE(nc.sbuf_tensor("e_w2o", [128, 2 * D], BF16))
    e_cw = EE(nc.sbuf_tensor("e_cw", [128, 64], F32))
    e_sc = EE(nc.sbuf_tensor("e_sc", [128, 8], F32))
    if run["E"]:
        for hf in range(TLOC // EHALF):
            t_pd = [b.P(b.Q["x1"]) if "x1" in b.Q else (None, 0),
                    b.P(b.pe), b.P(b.ve)]
            if "out" in b.Q:
                t_pd.append(b.P(b.Q["out"]))
            for tt in range(EHALF // 128):
                row0 = hf * EHALF + tt * 128
                tx = b.dma("sync", e_acc[:, tt * D:(tt + 1) * D],
                           sp_x1[row0:row0 + 128, :], stream="ex",
                           waits=t_pd if tt == 0 else [b.P(b.ve)])
                accs = e_acc[:, tt * D:(tt + 1) * D]
                t1 = b.v(lambda e, a2=accs: e.tensor_reduce(
                    out=c_t[:, 0:1], in_=a2, axis=AX.X, op=ALU.add), waits=[tx])
                t2 = b.a(lambda e, a2=accs: e.square(out=e_t[:, :], in_=a2),
                         waits=[tx, b.P(b.ve)])
                t3 = b.v(lambda e: e.tensor_reduce(out=c_t[:, 1:2], in_=e_t[:, :],
                                                   axis=AX.X, op=ALU.add),
                         waits=[t2])
                t4 = b.v(lambda e: e.tensor_scalar(out=c_t[:, 2:3],
                                                   in0=c_t[:, 0:1],
                                                   scalar1=1.0 / D, scalar2=None,
                                                   op0=ALU.mult), waits=[t1],
                         dr=True)
                t5 = b.v(lambda e: e.tensor_tensor(out=c_t[:, 3:4],
                                                   in0=c_t[:, 2:3],
                                                   in1=c_t[:, 2:3], op=ALU.mult),
                         dr=True)
                t6 = b.v(lambda e: e.scalar_tensor_tensor(
                    out=c_t[:, 4:5], in0=c_t[:, 1:2], scalar=1.0 / D,
                    in1=c_t[:, 3:4], op0=ALU.mult, op1=ALU.subtract), waits=[t3])
                t7a = b.a(lambda e: e.activation(out=c_t[:, 31:32],
                                                 in_=c_t[:, 4:5],
                                                 func=AF.Sqrt,
                                                 bias=cs_af[:, 0:1]), waits=[t6])
                t7 = b.v(lambda e: e.reciprocal(out=c_t[:, 5:6],
                                                in_=c_t[:, 31:32]), waits=[t7a],
                         dr=True)
                t8 = b.v(lambda e, a2=accs: e.tensor_scalar(
                    out=e_t[:, :], in0=a2, scalar1=c_t[:, 2:3],
                    scalar2=c_t[:, 5:6], op0=ALU.subtract, op1=ALU.mult),
                    waits=[t7])
                tv = t8
                for k in range(16):
                    q = k % 4
                    tp = b.pe_tr(pt[:, q * 128:(q + 1) * 128],
                                 e_t[:, k * 128:(k + 1) * 128], cs_id[:, :],
                                 waits=[tv])
                    col = k * EHALF + tt * 128
                    tv = b.split_av(pt[:, q * 128:(q + 1) * 128],
                                    e_x2h[:, col:col + 128],
                                    e_x2l[:, col:col + 128], waits=[tp])
                parts = []
                for k in range(16):
                    xh2 = e_x2h[:, k * EHALF + tt * 128: k * EHALF + (tt + 1) * 128]
                    xl2 = e_x2l[:, k * EHALF + tt * 128: k * EHALF + (tt + 1) * 128]
                    gh2 = gwh[:, k * 8:(k + 1) * 8]
                    gl2 = gwl[:, k * 8:(k + 1) * 8]
                    for (xo, go) in ((xh2, gh2), (xl2, gh2), (xh2, gl2)):
                        parts.append((px[:, 0:8], xo, go, False, False))
                parts[0] = parts[0][:3] + (True, False)
                parts[-1] = parts[-1][:4] + (True,)
                tsc_ = b.mm_group(parts, waits=[tv, t_gws, b.P(b.ve), b.P(b.ac)])
                r1 = b.v(lambda e: e.tensor_reduce(out=c_t[:, 8:9], in_=px[:, 0:8],
                                                   axis=AX.X, op=ALU.max),
                         waits=[tsc_], dr=True)
                r1b = b.v(lambda e: e.tensor_scalar(out=c_t[:, 8:9],
                                                    in0=c_t[:, 8:9], scalar1=-1.0,
                                                    scalar2=None, op0=ALU.mult))
                r2 = b.a(lambda e: e.activation(out=e_sc[:, :], in_=px[:, 0:8],
                                                func=AF.Exp, bias=c_t[:, 8:9]),
                         waits=[r1b])
                r3 = b.v(lambda e: e.tensor_reduce(out=c_t[:, 9:10], in_=e_sc[:, :],
                                                   axis=AX.X, op=ALU.add),
                         waits=[r2], dr=True)
                r4 = b.v(lambda e: e.reciprocal(out=c_t[:, 10:11],
                                                in_=c_t[:, 9:10]), waits=[r3],
                         dr=True)
                r5 = b.v(lambda e: e.tensor_scalar(out=e_sc[:, :], in0=e_sc[:, :],
                                                   scalar1=c_t[:, 10:11],
                                                   scalar2=None, op0=ALU.mult),
                         waits=[r4], dr=True)
                r6 = b.v(lambda e: e.tensor_reduce(out=c_t[:, 11:12],
                                                   in_=e_sc[:, :], axis=AX.X,
                                                   op=ALU.max), waits=[r5],
                         dr=True)
                r7 = b.v(lambda e: e.tensor_scalar(out=c_t[:, 16:24], in0=e_sc[:, :],
                                                   scalar1=c_t[:, 11:12],
                                                   scalar2=None, op0=ALU.is_ge),
                         waits=[r6], dr=True)
                r8 = b.v(lambda e: e.scalar_tensor_tensor(
                    out=c_t[:, 16:24], in0=c_t[:, 16:24], scalar=-BIG,
                    in1=e_sc[:, :], op0=ALU.mult, op1=ALU.add), waits=[r7],
                         dr=True)
                r9 = b.v(lambda e: e.tensor_reduce(out=c_t[:, 12:13],
                                                   in_=c_t[:, 16:24], axis=AX.X,
                                                   op=ALU.max), waits=[r8],
                         dr=True)
                r10 = b.v(lambda e: e.tensor_scalar(out=c_t[:, 16:24],
                                                    in0=e_sc[:, :],
                                                    scalar1=c_t[:, 12:13],
                                                    scalar2=None, op0=ALU.is_ge),
                          waits=[r9], dr=True)
                b.v(lambda e, tt=tt: e.tensor_tensor(
                    out=e_cw[:, tt * 8:(tt + 1) * 8], in0=e_sc[:, :],
                    in1=c_t[:, 16:24], op=ALU.mult), waits=[r10])
            t_router = b.P(b.ve)
            lastE = {"px": t_router, "ps": t_router}
            ew_used = [None, None]
            ewl_free = [None, None]
            w2_used = None
            w2l_free = None
            h2_used = None
            for ex in range(10):
                if ex < 2:
                    w1src = lambda k, ft, ex=ex: sw1[
                        k * 128:(k + 1) * 128,
                        ex * FR + ft * 128: ex * FR + (ft + 1) * 128]
                    w3src = lambda k, ft, ex=ex: sw3[
                        k * 128:(k + 1) * 128,
                        ex * FR + ft * 128: ex * FR + (ft + 1) * 128]
                    w2src = lambda ft, ex=ex: sw2[
                        ex * FR + ft * 128: ex * FR + (ft + 1) * 128, :]
                    cwcol = None
                else:
                    ei = ex - 2
                    w1src = lambda k, ft, ei=ei: rw1[
                        ei * D + k * 128: ei * D + (k + 1) * 128,
                        ft * 128:(ft + 1) * 128]
                    w3src = lambda k, ft, ei=ei: rw3[
                        ei * D + k * 128: ei * D + (k + 1) * 128,
                        ft * 128:(ft + 1) * 128]
                    w2src = lambda ft, ei=ei: rw2[
                        ei * FR + ft * 128: ei * FR + (ft + 1) * 128, :]
                    cwcol = ex - 2
                nft = FR // 128
                for fg0 in range(0, nft, 2):
                    gsz = min(2, nft - fg0)
                    for sl_ in range(gsz):
                        ft = fg0 + sl_
                        par = ft % 2
                        wfree = [ewl_free[par]] if ewl_free[par] else []
                        for k in range(16):
                            b.dma("sync", e_w1l[par][:, k * 128:(k + 1) * 128],
                                  w1src(k, ft), stream=f"ew{par}",
                                  waits=wfree if k == 0 else ())
                        for k in range(16):
                            b.dma("sync", e_w3l[par][:, k * 128:(k + 1) * 128],
                                  w3src(k, ft), stream=f"ew{par}")
                        t_w = b.P(b.Q[f"ew{par}"])
                        wu = [ew_used[par]] if ew_used[par] else []
                        ts1 = b.split_sg(e_w1l[par][:, :], e_w1h[par][:, :],
                                         e_w1o[par][:, :], waits=[t_w] + wu)
                        ts3 = b.split_sg(e_w3l[par][:, :], e_w3h[par][:, :],
                                         e_w3o[par][:, :], waits=[t_w])
                        ewl_free[par] = ts3
                        for s_ in range(EHALF // 512):
                            xc = s_ * 512
                            p1b, p1n = (pj, "pj") if s_ == 0 else (pj2, "pj2")
                            p3b, p3n = (pg, "pg") if s_ == 0 else (pr, "pr")
                            parts = []
                            for k in range(16):
                                xh2 = e_x2h[:, k * EHALF + xc: k * EHALF + xc + 512]
                                xl2 = e_x2l[:, k * EHALF + xc: k * EHALF + xc + 512]
                                wch = e_w1h[par][:, k * 128:(k + 1) * 128]
                                wcl = e_w1o[par][:, k * 128:(k + 1) * 128]
                                for (wop, xop) in ((wch, xh2), (wch, xl2),
                                                   (wcl, xh2)):
                                    parts.append((p1b[:, 0:512], wop, xop,
                                                  False, False))
                            parts[0] = parts[0][:3] + (True, False)
                            parts[-1] = parts[-1][:4] + (True,)
                            tm1 = b.mm_group(parts, waits=[
                                ts1, t_router, lastE.get(p1n, (None, 0))])
                            parts = []
                            for k in range(16):
                                xh2 = e_x2h[:, k * EHALF + xc: k * EHALF + xc + 512]
                                xl2 = e_x2l[:, k * EHALF + xc: k * EHALF + xc + 512]
                                wch = e_w3h[par][:, k * 128:(k + 1) * 128]
                                wcl = e_w3o[par][:, k * 128:(k + 1) * 128]
                                for (wop, xop) in ((wch, xh2), (wch, xl2),
                                                   (wcl, xh2)):
                                    parts.append((p3b[:, 0:512], wop, xop,
                                                  False, False))
                            parts[0] = parts[0][:3] + (True, False)
                            parts[-1] = parts[-1][:4] + (True,)
                            tm3 = b.mm_group(parts, waits=[
                                ts3, lastE.get(p3n, (None, 0))])
                            ew_used[par] = tm3
                            tsl = b.a(lambda e, p=p1b: e.activation(
                                out=e_t[:, 0:512], in_=p[:, 0:512],
                                func=AF.Silu),
                                waits=[tm1, lastE.get("e_t1", (None, 0))])
                            lastE[p1n] = tsl
                            th2 = b.v(lambda e, p=p3b: e.tensor_tensor(
                                out=e_t[:, 512:1024], in0=e_t[:, 0:512],
                                in1=p[:, 0:512], op=ALU.mult),
                                waits=[tsl, tm3])
                            lastE[p3n] = th2
                            lastE["e_t1"] = th2
                            col = sl_ * EHALF + xc
                            th2s = b.split_sg(e_t[:, 512:1024],
                                              e_h2h[:, col:col + 512],
                                              e_h2l[:, col:col + 512],
                                              waits=[th2] +
                                              ([h2_used] if h2_used else []))
                    t_h2 = th2s
                    w2f = [w2l_free] if w2l_free else []
                    for sl_ in range(gsz):
                        b.dma("sync", e_w2l[:, sl_ * D:(sl_ + 1) * D],
                              w2src(fg0 + sl_), stream="ew2",
                              waits=w2f if sl_ == 0 else ())
                    t_w2 = b.P(b.Q["ew2"])
                    ts2 = b.split_sg(e_w2l[:, 0:gsz * D], e_w2h[:, 0:gsz * D],
                                     e_w2o[:, 0:gsz * D],
                                     waits=[t_w2] + ([w2_used] if w2_used else []))
                    w2l_free = ts2
                    for tt in range(EHALF // 128):
                        for dc in range(4):
                            pb, pn = (px, "px") if (tt * 4 + dc) % 2 == 0 \
                                else (ps, "ps")
                            parts = []
                            for sl_ in range(gsz):
                                hh = e_h2h[:, sl_ * EHALF + tt * 128:
                                           sl_ * EHALF + (tt + 1) * 128]
                                hl = e_h2l[:, sl_ * EHALF + tt * 128:
                                           sl_ * EHALF + (tt + 1) * 128]
                                wh2 = e_w2h[:, sl_ * D + dc * 512:
                                            sl_ * D + (dc + 1) * 512]
                                wl2 = e_w2o[:, sl_ * D + dc * 512:
                                            sl_ * D + (dc + 1) * 512]
                                for (ho, wop) in ((hh, wh2), (hh, wl2),
                                                  (hl, wh2)):
                                    parts.append((pb[:, 0:512], ho, wop,
                                                  False, False))
                            parts[0] = parts[0][:3] + (True, False)
                            parts[-1] = parts[-1][:4] + (True,)
                            tm2 = b.mm_group(parts, waits=[
                                ts2, t_h2, lastE.get(pn, (None, 0))])
                            w2_used = tm2
                            h2_used = tm2
                            acc2 = e_acc[:, tt * D + dc * 512:
                                         tt * D + (dc + 1) * 512]
                            if cwcol is None:
                                tev2 = b.v(lambda e, p=pb, a2=acc2:
                                           e.tensor_tensor(
                                    out=a2, in0=p[:, 0:512], in1=a2,
                                    op=ALU.add), waits=[tm2])
                            else:
                                cwc = e_cw[:, tt * 8 + cwcol: tt * 8 + cwcol + 1]
                                tev2 = b.v(lambda e, p=pb, a2=acc2, cw=cwc:
                                           e.scalar_tensor_tensor(
                                        out=a2, in0=p[:, 0:512], scalar=cw,
                                        in1=a2, op0=ALU.mult, op1=ALU.add),
                                    waits=[tm2])
                            lastE[pn] = tev2
            tfin = b.P(b.ve)
            for tt in range(EHALF // 128):
                b.dma("gpsimd", out[hf * EHALF + tt * 128:
                                    hf * EHALF + (tt + 1) * 128, :],
                      e_acc[:, tt * D:(tt + 1) * D], stream="out",
                      waits=[tfin] if tt == 0 else ())

    stkE.close()

    endw = [b.P(b.pe), b.P(b.ve), b.P(b.ac)] + [b.P(q) for q in b.Q.values()]

    def fin(e, ws=tuple(endw)):
        for s, v in ws:
            if v > 0:
                e.wait_ge(s.h, v)
    b.q["gpsimd"].append(fin)

    with nc.Block() as block:
        @block.sync
        def _(e):
            for f in b.q["sync"]:
                f(e)

        @block.gpsimd
        def _(e):
            for f in b.q["gpsimd"]:
                f(e)

        @block.tensor
        def _(e):
            for f in b.q["tensor"]:
                f(e)

        @block.vector
        def _(e):
            for f in b.q["vector"]:
                f(e)

        @block.scalar
        def _(e):
            for f in b.q["scalar"]:
                f(e)
    stack.close()
    return nc


def make_in_maps(inputs):
    cs = host_constants()
    x = np.asarray(inputs["L0_attn_hidden_states"])
    g = lambda n: np.ascontiguousarray(np.asarray(inputs[n], np.float32))
    conv = np.zeros((128, 192), np.float32)
    for ti, n in enumerate(("conv_q", "conv_k", "conv_v")):
        w = g(f"L0_attn_{n}").reshape(16, 128, 4)
        conv[:, ti * 64:(ti + 1) * 64] = w.transpose(1, 0, 2).reshape(128, 64)
    common = {
        "wq": g("L0_attn_Wq"), "wk": g("L0_attn_Wk"), "wv": g("L0_attn_Wv"),
        "wg": g("L0_attn_Wg"), "wo": g("L0_attn_Wo"), "wf1": g("L0_attn_Wf1"),
        "wf2": g("L0_attn_Wf2"), "wb": g("L0_attn_Wb"), "convw": conv,
        "afm": np.ascontiguousarray(np.broadcast_to(
            g("L0_attn_A_log")[:, None], (16, 128))),
        "dtb": np.ascontiguousarray(np.broadcast_to(
            g("L0_attn_dt_bias")[None, :], (128, 128))),
        "nwbc": np.ascontiguousarray(np.broadcast_to(
            g("L0_attn_norm_w")[None, :], (128, 128))),
        "bgc": np.ascontiguousarray(g("L0_attn_bg").reshape(16, 128).T),
        "gatew": g("L0_ffn_gate_w"),
        "sw1": g("L0_ffn_sw1"), "sw2": g("L0_ffn_sw2"), "sw3": g("L0_ffn_sw3"),
        "rw1": g("L0_ffn_rw1").reshape(NE * D, FR),
        "rw2": g("L0_ffn_rw2").reshape(NE * FR, D),
        "rw3": g("L0_ffn_rw3").reshape(NE * D, FR),
        "cident": cs["cident"], "cuincl": cs["cuincl"], "ce1": cs["ce1"],
        "ce2": cs["ce2"], "conesr": cs["conesr"], "conesc": cs["conesc"],
        "cmaskA": cs["cmaskA"], "cmaskM": cs["cmaskM"],
        "cactf": np.ascontiguousarray(np.broadcast_to(np.array(
            [1e-5, 1e-6 * 128.0, 1e-6, 128.0, 1.0, 1.0 / 128.0, 0.0, 0.0],
            np.float32)[None, :], (128, 8))),
    }
    in_maps = []
    for cid in range(8):
        bi, half = cid // 2, cid % 2
        shard = np.zeros((HALO + TLOC, D), np.float32)
        lo = half * TLOC
        if lo >= HALO:
            shard[:HALO] = x[bi, lo - HALO:lo]
        shard[HALO:] = x[bi, lo:lo + TLOC]
        m = dict(common)
        m["xs"] = shard
        m["cisev"] = np.full((128, 1), 1.0 if half == 0 else 0.0, np.float32)
        m["cisod"] = np.full((128, 1), 1.0 if half == 1 else 0.0, np.float32)
        in_maps.append(m)
    return in_maps


def kernel(**inputs):
    nc = build_program()
    in_maps = make_in_maps(inputs)
    res = run_bass_kernel_spmd(nc, in_maps, core_ids=list(range(8)))
    out = np.empty((4, 4096, D), np.float32)
    for cid in range(8):
        bi, half = cid // 2, cid % 2
        out[bi, half * TLOC:(half + 1) * TLOC] = res.results[cid]["out"]
    return out


# version 04 (final)
# speedup vs baseline: 1.6468x; 1.1958x over previous
import os
import numpy as np
import concourse.bass as bass
import concourse.mybir as mybir
from concourse.bass_utils import run_bass_kernel_spmd

F32 = mybir.dt.float32
BF16 = mybir.dt.bfloat16
ALU = mybir.AluOpType
AF = mybir.ActivationFunctionType
AX = mybir.AxisListType

H, DK, DV, GG, NG = 16, 128, 128, 16, 8
D = 2048
SCALE = DK ** -0.5
C = 128
TLOC = 2048
HALO = 128
NCH = TLOC // C
NBLK = 4
BT = 512
FS, FR, NE = 2816, 1408, 8
BIG = 1e30
EHALF = 512

ENG = ("sync", "gpsimd", "tensor", "vector", "scalar")


class S:
    __slots__ = ("h", "cnt", "name")

    def __init__(self, h, name):
        self.h, self.cnt, self.name = h, 0, name


class Bld:
    def __init__(self, nc, stack):
        self.nc, self.stack = nc, stack
        self.q = {e: [] for e in ENG}
        self.pe = self.sem("pe")
        self.ve = self.sem("ve")
        self.ac = self.sem("ac")
        self.cc = self.sem("cc")
        self.gp = self.sem("gp")
        self.Q = {}

    def sem(self, name):
        return S(self.stack.enter_context(self.nc.semaphore(name)), name)

    def P(self, s):
        return (s, s.cnt)

    def emit(self, eng, make, waits=(), inc=None, d=1):
        waits = tuple((s, v) for (s, v) in waits if s is not None and v > 0)
        for s, v in waits:
            assert v <= s.cnt, f"wait on future {s.name} {v}>{s.cnt}"
        tok = None
        if inc is not None:
            inc.cnt += d
            tok = (inc, inc.cnt)

        def f(e, waits=waits, inc=inc, d=d, make=make):
            for s, v in waits:
                e.wait_ge(s.h, v)
            r = make(e)
            if inc is not None:
                r.then_inc(inc.h, d)
        self.q[eng].append(f)
        return tok

    def dma(self, eng, out, in_, stream="ld", waits=()):
        if stream not in self.Q:
            self.Q[stream] = self.sem("q_" + stream)
        s = self.Q[stream]
        return self.emit(eng, lambda e, o=out, i=in_: e.dma_start(out=o, in_=i),
                         waits=waits, inc=s, d=16)

    def pe_mm(self, out, lhsT, rhs, start, stop, waits=(), inc=False):
        return self.emit(
            "tensor",
            lambda e, o=out, l=lhsT, r=rhs, st=start, sp=stop: e.matmul(
                out=o, lhsT=l, rhs=r, start=st, stop=sp),
            waits=waits, inc=self.pe if inc else None)

    def pe_tr(self, out, in_, ident, waits=()):
        return self.emit(
            "tensor",
            lambda e, o=out, i=in_, d=ident: e.transpose(out=o, in_=i, identity=d),
            waits=waits, inc=self.pe)

    def mm_group(self, parts, waits=()):
        n = len(parts)
        tok = None
        for i, (o, l, r, st, sp) in enumerate(parts):
            w = waits if i == 0 else ()
            tok = self.pe_mm(o, l, r, st, sp, waits=w, inc=(i == n - 1))
        return tok

    drains = False
    vdrains = False
    adrains = False

    def v(self, make, waits=(), dr=False):
        tok = self.emit("vector", make, waits=waits, inc=self.ve)
        if dr or self.drains or self.vdrains:
            self.q["vector"].append(lambda e: e.drain())
        return tok

    def a(self, make, waits=()):
        tok = self.emit("scalar", make, waits=waits, inc=self.ac)
        if self.drains or self.adrains:
            self.q["scalar"].append(lambda e: e.drain())
        return tok

    def g(self, make, waits=()):
        tok = self.emit("gpsimd", make, waits=waits, inc=self.gp)
        return tok

    def split_sg(self, src, hi, lo, waits=()):
        if os.environ.get("K_VSPLIT"):
            return self.split(src, hi, lo, waits=waits)
        t1 = self.a(lambda e, s=src, h=hi: e.copy(out=h, in_=s), waits=waits)
        t2 = self.g(lambda e, s=src, h=hi, l=lo: e.tensor_sub(out=l, in0=s, in1=h),
                    waits=[t1])
        return t2

    def split(self, src, hi, lo, waits=()):
        self.v(lambda e, s=src, h=hi: e.tensor_copy(out=h, in_=s), waits=waits,
               dr=True)
        return self.v(lambda e, s=src, h=hi, l=lo: e.tensor_sub(out=l, in0=s,
                                                                in1=h))

    def split_av(self, src, hi, lo, waits=()):
        t1 = self.a(lambda e, s=src, h=hi: e.copy(out=h, in_=s), waits=waits)
        return self.v(lambda e, s=src, h=hi, l=lo: e.tensor_sub(out=l, in0=s,
                                                                in1=h),
                      waits=[t1])


def host_constants():
    cs = {}
    cs["cident"] = np.eye(128, dtype=np.float32)
    cs["cuincl"] = np.triu(np.ones((C, C), np.float32))
    e1 = np.zeros((NG, DK), np.float32)
    for n in range(NG):
        e1[n, n * GG:(n + 1) * GG] = 1
    cs["ce1"] = e1
    e2 = np.zeros((H, DK), np.float32)
    for h in range(H):
        e2[h, h * NG:(h + 1) * NG] = 1
    cs["ce2"] = e2
    cs["conesr"] = np.ones((1, 128), np.float32)
    cs["conesc"] = np.ones((128, 1), np.float32)
    e1r = np.zeros((4, 32, NG * C), np.float32)
    e1c = np.zeros((4, 32, DK), np.float32)
    for c in range(4):
        for n in range(NG):
            e1r[c, c * NG + n, n * C:(n + 1) * C] = 1
            e1c[c, c * NG + n, n * GG:(n + 1) * GG] = 1
    cs["ce1c"] = e1c.reshape(4 * 32, DK)
    cs["cones32"] = np.ones((32, 128), np.float32)
    ii = np.arange(C)[:, None]
    jj = np.arange(C)[None, :]
    mA = np.where(ii > jj, 0.0, -BIG).astype(np.float32)
    mM = np.where(jj >= ii, 0.0, -BIG).astype(np.float32)
    cs["cmaskA"] = np.tile(mA, (1, NG))
    cs["cmaskM"] = np.tile(mM, (1, NG))
    return cs


def build_program(dbg=False, upto="E", heads=None, blocks=None):
    import contextlib
    check = bool(os.environ.get("K_CHECK"))
    nc = bass.Bass("TRN2", target_bir_lowering=False,
                   detect_race_conditions=check)
    stack = contextlib.ExitStack()
    b = Bld(nc, stack)
    b.drains = check
    b.vdrains = bool(os.environ.get('K_VDRAIN'))
    b.adrains = bool(os.environ.get('K_ADRAIN'))
    PH = ("A", "B", "C", "D", "E")
    run = {p: PH.index(p) <= PH.index(upto) for p in PH}

    def dt(name, shape, kind="ExternalInput", dtype=F32):
        return nc.dram_tensor(name, shape, dtype, kind=kind).ap()

    xs = dt("xs", [HALO + TLOC, D])
    wq, wk, wv, wg, wo = (dt(n, [D, D]) for n in ("wq", "wk", "wv", "wg", "wo"))
    wf1 = dt("wf1", [D, 128])
    wf2 = dt("wf2", [128, 128])
    wb = dt("wb", [D, 16])
    convw = dt("convw", [128, 192])
    afm = dt("afm", [16, 128])
    dtb = dt("dtb", [128, 128])
    nwbc = dt("nwbc", [128, 128])
    bgc = dt("bgc", [128, 16])
    gatew = dt("gatew", [D, 8])
    sw1 = dt("sw1", [D, FS])
    sw2 = dt("sw2", [FS, D])
    sw3 = dt("sw3", [D, FS])
    rw1 = dt("rw1", [NE * D, FR])
    rw2 = dt("rw2", [NE * FR, D])
    rw3 = dt("rw3", [NE * D, FR])
    cident = dt("cident", [128, 128])
    cuincl = dt("cuincl", [128, 128])
    ce1 = dt("ce1", [8, 128])
    ce2 = dt("ce2", [16, 128])
    conesr = dt("conesr", [1, 128])
    conesc = dt("conesc", [128, 1])
    cmaskA = dt("cmaskA", [128, NG * C])
    cmaskM = dt("cmaskM", [128, NG * C])
    cisev = dt("cisev", [128, 1])
    cisod = dt("cisod", [128, 1])
    cactf = dt("cactf", [128, 8])
    ce1c = dt("ce1c", [4 * 32, 128])
    cones32 = dt("cones32", [32, 128])
    spk = "ExternalOutput" if dbg else "Internal"
    sp_mst = dt("sp_mst", [H * NCH * 128, 128], spk)
    sp_ns = dt("sp_ns", [H * NCH * 128, 128], spk)
    sp_q2 = dt("sp_q2", [H * NCH * 128, 128], spk)
    sp_yl = dt("sp_yl", [H * NCH * 128, 128], spk)
    sp_sg = dt("sp_sg", [H * 128, TLOC], spk)
    sp_x1 = dt("sp_x1", [TLOC, D], spk)
    cc_in = nc.dram_tensor("cc_in", [H * 128, 128], F32)
    cc_out = nc.dram_tensor("cc_out", [H * 128, 128], F32)
    dbg_y = dt("dbg_y", [H * 128, TLOC], "ExternalOutput", BF16) if dbg else None
    dbg_send = dt("dbg_send", [H * 128, 128], "ExternalOutput") if dbg else None
    dbg_qkv = (dt("dbg_qkv", [3 * H * 128, TLOC], "ExternalOutput")
               if dbg and os.environ.get("K_DBG_QKV") else None)
    dbg_ck = (dt("dbg_ck", [H * NCH * 128, 4352], "ExternalOutput")
              if dbg and os.environ.get("K_DBG_CHUNK") else None)
    dbg_pre = (dt("dbg_pre", [3 * H * 128, 515 * NBLK], "ExternalOutput")
               if dbg and os.environ.get("K_DBG_PRE") else None)
    out = dt("out", [TLOC, D], kind="ExternalOutput")

    E = stack.enter_context
    import contextlib as _ctl
    stk0 = _ctl.ExitStack()
    stkA = _ctl.ExitStack()
    stkCD = _ctl.ExitStack()
    stkE = _ctl.ExitStack()
    E0 = stk0.enter_context
    EA = stkA.enter_context
    EC = stkCD.enter_context
    EE = stkE.enter_context

    pj = E(nc.psum_tensor("pj", [128, 512], F32))
    pj2 = E(nc.psum_tensor("pj2", [128, 512], F32))
    pg = E(nc.psum_tensor("pg", [128, 512], F32))
    pr = E(nc.psum_tensor("pr", [128, 1024], F32))
    pt = E(nc.psum_tensor("pt", [128, 512], F32))
    px = E(nc.psum_tensor("px", [128, 512], F32))
    ps = E(nc.psum_tensor("ps", [128, 512], F32))

    cs_id = E(nc.sbuf_tensor("cs_id", [128, 128], F32))
    cs_ui = E(nc.sbuf_tensor("cs_ui", [128, 128], F32))
    cs_e1 = E(nc.sbuf_tensor("cs_e1", [8, 128], F32))
    cs_e2 = E(nc.sbuf_tensor("cs_e2", [16, 128], F32))
    cs_or = E(nc.sbuf_tensor("cs_or", [1, 128], F32))
    cs_oc = E(nc.sbuf_tensor("cs_oc", [128, 1], F32))
    cs_nw = E(nc.sbuf_tensor("cs_nw", [128, 128], F32))
    cs_dtb = E(nc.sbuf_tensor("cs_dtb", [128, 128], F32))
    cs_bg = E(nc.sbuf_tensor("cs_bg", [128, 16], F32))
    cs_cw = E(nc.sbuf_tensor("cs_cw", [128, 192], F32))
    cs_afm = E(nc.sbuf_tensor("cs_afm", [16, 128], F32))
    cs_ascl = E(nc.sbuf_tensor("cs_ascl", [128, 128], F32))
    cs_iev = E(nc.sbuf_tensor("cs_iev", [128, 1], F32))
    cs_iod = E(nc.sbuf_tensor("cs_iod", [128, 1], F32))
    cs_af = E(nc.sbuf_tensor("cs_af", [128, 8], F32))
    wbh = E(nc.sbuf_tensor("wbh", [128, 256], BF16))
    wbl = E(nc.sbuf_tensor("wbl", [128, 256], BF16))
    wf1h = E(nc.sbuf_tensor("wf1h", [128, D], BF16))
    wf1l = E(nc.sbuf_tensor("wf1l", [128, D], BF16))
    cs_wf2 = E(nc.sbuf_tensor("cs_wf2", [128, 128], F32))
    gwh = E(nc.sbuf_tensor("gwh", [128, 128], BF16))
    gwl = E(nc.sbuf_tensor("gwl", [128, 128], BF16))
    c_t = E(nc.sbuf_tensor("c_t", [128, 32], F32))

    tmpw = E0(nc.sbuf_tensor("tmpw", [128, D + 256], F32))
    for dst, src in ((cs_id, cident), (cs_ui, cuincl), (cs_e1, ce1),
                     (cs_e2, ce2), (cs_or, conesr), (cs_oc, conesc),
                     (cs_nw, nwbc), (cs_dtb, dtb), (cs_bg, bgc),
                     (cs_cw, convw), (cs_afm, afm), (cs_iev, cisev),
                     (cs_iod, cisod), (cs_wf2, wf2), (cs_af, cactf)):
        b.dma("sync", dst[:, :], src[:, :])
    t_ld0 = b.P(b.Q["ld"])
    for k in range(16):
        b.dma("sync", tmpw[:, k * 16:(k + 1) * 16], wb[k * 128:(k + 1) * 128, :],
              stream="ldb")
    t_wb = b.P(b.Q["ldb"])
    b.split(tmpw[:, 0:256], wbh[:, :], wbl[:, :], waits=[t_wb])
    for k in range(16):
        b.dma("sync", tmpw[:, 256 + k * 128:256 + (k + 1) * 128],
              wf1[k * 128:(k + 1) * 128, :], stream="ldf")
    t_wf1 = b.P(b.Q["ldf"])
    t_splits = b.split(tmpw[:, 256:256 + D], wf1h[:, :], wf1l[:, :], waits=[t_wf1])
    gwtmp = E0(nc.sbuf_tensor("gwtmp", [128, 128], F32))
    for k in range(16):
        b.dma("sync", gwtmp[:, k * 8:(k + 1) * 8],
              gatew[k * 128:(k + 1) * 128, :], stream="ldg")
    t_gw = b.P(b.Q["ldg"])
    t_gws = b.split(gwtmp[:, :], gwh[:, :], gwl[:, :], waits=[t_gw])
    t_e = b.a(lambda e: e.activation(out=cs_afm[:, :], in_=cs_afm[:, :],
                                     func=AF.Exp), waits=[t_ld0])
    t_n = b.v(lambda e: e.tensor_scalar(out=cs_afm[:, :], in0=cs_afm[:, :],
                                        scalar1=-1.0, scalar2=None, op0=ALU.mult),
              waits=[t_e])
    t_m = b.pe_mm(ps[:, 0:128], cs_afm[:, :], cs_e2[:, :], True, True,
                  waits=[t_n, t_ld0], inc=True)
    t_consts = b.v(lambda e: e.tensor_copy(out=cs_ascl[:, :], in_=ps[:, 0:128]),
                   waits=[t_m])

    stk0.close()

    csbA = EA(nc.sbuf_tensor("csbA", [128, NG * C], F32))
    csbM = EA(nc.sbuf_tensor("csbM", [128, NG * C], F32))
    b.dma("sync", csbA[:, :], cmaskA[:, :], stream="ldm",
          waits=[t_splits, t_gws, t_consts])
    t_mask = b.dma("sync", csbM[:, :], cmaskM[:, :], stream="ldm")

    xlt_h = EA(nc.sbuf_tensor("xlt_h", [128, 16 * 640], BF16))
    xlt_l = EA(nc.sbuf_tensor("xlt_l", [128, 16 * 640], BF16))
    a_xt = EA(nc.sbuf_tensor("a_xt", [128, D], F32))
    a_tmp = EA(nc.sbuf_tensor("a_tmp", [128, D], F32))
    a_pre = EA(nc.sbuf_tensor("a_pre", [128, 515], F32))
    a_qf = EA(nc.sbuf_tensor("a_qf", [128, BT], F32))
    a_kf = EA(nc.sbuf_tensor("a_kf", [128, BT], F32))
    a_vf = EA(nc.sbuf_tensor("a_vf", [128, BT], F32))
    a_sq = EA(nc.sbuf_tensor("a_sq", [128, BT], F32))
    a_srow = EA(nc.sbuf_tensor("a_srow", [1, BT], F32))
    a_sg = EA(nc.sbuf_tensor("a_sg", [128, BT], F32))
    a_ghat = EA(nc.sbuf_tensor("a_ghat", [128, NBLK * 128], F32))
    a_beta = EA(nc.sbuf_tensor("a_beta", [128, NBLK * 16], F32))
    a_t1 = EA(nc.sbuf_tensor("a_t1", [128, BT], F32))
    a_wl = [EA(nc.sbuf_tensor(f"a_wl{i}", [128, D], F32)) for i in range(2)]
    a_wh = [EA(nc.sbuf_tensor(f"a_wh{i}", [128, D], BF16)) for i in range(2)]
    a_wlo = [EA(nc.sbuf_tensor(f"a_wlo{i}", [128, D], BF16)) for i in range(2)]
    a_kg = EA(nc.sbuf_tensor("a_kg", [16, NG * 256], F32))
    a_qg = EA(nc.sbuf_tensor("a_qg", [16, NG * 256], F32))
    cE1c = EA(nc.sbuf_tensor("cE1c", [32, 4 * 128], F32))
    cOn32 = EA(nc.sbuf_tensor("cOn32", [32, 128], F32))
    c4_gt = EA(nc.sbuf_tensor("c4_gt", [128, 32], F32))
    c4_gf = EA(nc.sbuf_tensor("c4_gf", [32, 128], F32))
    c4_r2 = EA(nc.sbuf_tensor("c4_r2", [32, 128], F32))
    c4_rhs2 = EA(nc.sbuf_tensor("c4_rhs2", [32, NG * C], F32))
    c_ra = [EA(nc.sbuf_tensor(f"c_ra{i}", [128, NG * C], F32)) for i in range(2)]
    c_rm = [EA(nc.sbuf_tensor(f"c_rm{i}", [128, NG * C], F32)) for i in range(2)]
    c4_A = EA(nc.sbuf_tensor("c4_A", [128, 512], F32))
    c4_At = EA(nc.sbuf_tensor("c4_At", [128, 512], F32))
    c4_Ap = EA(nc.sbuf_tensor("c4_Ap", [128, 512], F32))
    c4_ApT = EA(nc.sbuf_tensor("c4_ApT", [128, 512], F32))
    c4_Mt = EA(nc.sbuf_tensor("c4_Mt", [128, 512], F32))
    c4_eg = EA(nc.sbuf_tensor("c4_eg", [128, 512], F32))
    c4_kt = EA(nc.sbuf_tensor("c4_kt", [128, 512], F32))
    c4_qt = EA(nc.sbuf_tensor("c4_qt", [128, 512], F32))
    c4_kh = EA(nc.sbuf_tensor("c4_kh", [128, 512], F32))
    c4_khT = EA(nc.sbuf_tensor("c4_khT", [128, 512], F32))
    c4_X = EA(nc.sbuf_tensor("c4_X", [128, 1024], F32))
    c4_ms = EA(nc.sbuf_tensor("c4_ms", [128, 512], F32))
    c4_ns = EA(nc.sbuf_tensor("c4_ns", [128, 512], F32))
    c4_q2 = EA(nc.sbuf_tensor("c4_q2", [128, 512], F32))
    c4_yl = EA(nc.sbuf_tensor("c4_yl", [128, 512], F32))
    c_tt = EA(nc.sbuf_tensor("c_tt", [128, 128], F32))
    for c4i in range(4):
        b.dma("sync", cE1c[:, c4i * 128:(c4i + 1) * 128],
              ce1c[c4i * 32:(c4i + 1) * 32, :], stream="ldc")
    t_ldc = b.dma("sync", cOn32[:, :], cones32[:, :], stream="ldc")

    c_tt2 = EA(nc.sbuf_tensor("c_tt2", [128, 128], F32))

    def ln_transpose(src_rows, xh, xl, colw, dst_col0, deps):
        tok_x = b.dma("sync", a_xt[:, :], src_rows, stream="xt", waits=deps)
        t1 = b.v(lambda e: e.tensor_reduce(out=c_t[:, 0:1], in_=a_xt[:, :],
                                           axis=AX.X, op=ALU.add), waits=[tok_x])
        t2 = b.a(lambda e: e.square(out=a_tmp[:, :], in_=a_xt[:, :]),
                 waits=[tok_x, b.P(b.ve)])
        t3 = b.v(lambda e: e.tensor_reduce(out=c_t[:, 1:2], in_=a_tmp[:, :],
                                           axis=AX.X, op=ALU.add), waits=[t2])
        t4 = b.v(lambda e: e.tensor_scalar(out=c_t[:, 2:3], in0=c_t[:, 0:1],
                                           scalar1=1.0 / D, scalar2=None,
                                           op0=ALU.mult), waits=[t1], dr=True)
        t5 = b.v(lambda e: e.tensor_tensor(out=c_t[:, 3:4], in0=c_t[:, 2:3],
                                           in1=c_t[:, 2:3], op=ALU.mult), dr=True)
        t6 = b.v(lambda e: e.scalar_tensor_tensor(out=c_t[:, 4:5], in0=c_t[:, 1:2],
                                                  scalar=1.0 / D, in1=c_t[:, 3:4],
                                                  op0=ALU.mult, op1=ALU.subtract),
                 waits=[t3])
        t7a = b.a(lambda e: e.activation(out=c_t[:, 31:32], in_=c_t[:, 4:5],
                                         func=AF.Sqrt, bias=cs_af[:, 0:1]),
                  waits=[t6])
        t7 = b.v(lambda e: e.reciprocal(out=c_t[:, 5:6], in_=c_t[:, 31:32]),
                 waits=[t7a], dr=True)
        t8 = b.v(lambda e: e.tensor_scalar(out=a_tmp[:, :], in0=a_xt[:, :],
                                           scalar1=c_t[:, 2:3], scalar2=c_t[:, 5:6],
                                           op0=ALU.subtract, op1=ALU.mult),
                 waits=[t7])
        tv = t8
        for k in range(16):
            q = k % 4
            tp = b.pe_tr(pt[:, q * 128:(q + 1) * 128],
                         a_tmp[:, k * 128:(k + 1) * 128], cs_id[:, :],
                         waits=[tv])
            col = k * colw + dst_col0
            tv = b.split_av(pt[:, q * 128:(q + 1) * 128], xh[:, col:col + 128],
                            xl[:, col:col + 128], waits=[tp])
        return tv

    def mm3pass(out_ap, out_tail, wh_s, wl_s, xcol0, ncols, waits,
                xh=None, xl=None, colw=640):
        xh = xh if xh is not None else xlt_h
        xl = xl if xl is not None else xlt_l
        parts = []
        n0 = min(ncols, 512)
        for k in range(16):
            xhi = xh[:, k * colw + xcol0: k * colw + xcol0 + n0]
            xlo = xl[:, k * colw + xcol0: k * colw + xcol0 + n0]
            first = (k == 0)
            for (wop, xop) in ((wh_s(k), xhi), (wh_s(k), xlo), (wl_s(k), xhi)):
                parts.append((out_ap, wop, xop, first, False))
                first = False
        parts[-1] = parts[-1][:4] + (True,)
        if ncols > 512:
            for k in range(16):
                xhi2 = xh[:, k * colw + xcol0 + 512: k * colw + xcol0 + ncols]
                xlo2 = xl[:, k * colw + xcol0 + 512: k * colw + xcol0 + ncols]
                f2 = (k == 0)
                for (wop, xop) in ((wh_s(k), xhi2), (wh_s(k), xlo2),
                                   (wl_s(k), xhi2)):
                    parts.append((out_tail, wop, xop, f2, False))
                    f2 = False
            parts[-1] = parts[-1][:4] + (True,)
        return b.mm_group(parts, waits=waits)

    LB = {}
    CSTG = int(os.environ.get("K_CHUNK_STAGE", "9"))

    def chunk_math4(h, blk, t_first):
        APC = type(px[:, :])

        def bc_gf():
            ap = c4_gf[:, :]
            return APC(tensor=ap.tensor, offset=0,
                       ap=[[128, 32], [0, 8], [1, 128]])

        bank = [pg, pj]
        bcols = [a_beta[:, cl * 16 + h: cl * 16 + h + 1] for cl in range(4)]
        def regroup(half, waits):
            st = f"rg{half}"
            for n in range(NG):
                b.dma("gpsimd", a_kg[:, n * 256:(n + 1) * 256],
                      a_kf[16 * n:16 * (n + 1), half * 256:(half + 1) * 256],
                      stream=st, waits=waits if n == 0 else ())
            for n in range(NG):
                b.dma("gpsimd", a_qg[:, n * 256:(n + 1) * 256],
                      a_qf[16 * n:16 * (n + 1), half * 256:(half + 1) * 256],
                      stream=st)
            return b.P(b.Q[st])

        t_rg = regroup(0, t_first)
        gh_rhs = a_ghat[:, :].rearrange("p (c x) -> p c x", c=4)[
            :, :, h * NG:(h + 1) * NG]
        t1 = b.pe_mm(px[:, 0:32], cs_ui[:, :], gh_rhs, True, True,
                     waits=list(t_first) + [LB.get("px_t2", (None, 0))],
                     inc=True)
        t2 = b.v(lambda e: e.tensor_copy(out=c4_gt[:, :], in_=px[:, 0:32]),
                 waits=[t1])
        LB["px_t2"] = t2
        t3 = b.pe_tr(pt[0:32, 0:128], c4_gt[:, :], cs_id[:, :],
                     waits=[t2, LB.get("ptq0", (None, 0))])
        t4 = b.v(lambda e: e.tensor_copy(out=c4_gf[:, :], in_=pt[0:32, 0:128]),
                 waits=[t3])
        LB["ptq0"] = t4
        if CSTG == 1:
            return
        exps = [None] * 4
        evs = [None] * 4
        tA = [None] * 4
        tM = [None] * 4

        def stage2(c):
            par = c % 2
            idap = cs_id[:, :]
            e1r_c = APC(tensor=idap.tensor, offset=c * NG,
                        ap=[[128, 32], [1, 8], [0, 128]])
            tr2 = b.v(lambda e, c=c, rr=e1r_c: e.scalar_tensor_tensor(
                out=c4_rhs2[:, :], in0=bc_gf(), scalar=-1.0,
                in1=rr, op0=ALU.mult, op1=ALU.mult),
                waits=[t4, LB.get("rhs2", (None, 0))])
            e1r_h1 = APC(tensor=idap.tensor, offset=c * NG,
                         ap=[[128, 32], [1, 4], [0, 128]])
            e1r_h2 = APC(tensor=idap.tensor, offset=c * NG + 4,
                         ap=[[128, 32], [1, 4], [0, 128]])
            m_a = b.pe_mm(pr[:, 0:512], c4_gf[:, :], e1r_h1, True, False,
                          waits=[tr2, LB.get("pr_sl", (None, 0))])
            b.pe_mm(pr[:, 512:1024], c4_gf[:, :], e1r_h2, True, False)
            b.pe_mm(pr[:, 0:512], cOn32[:, :], c4_rhs2[:, 0:512], False, True)
            m_d = b.pe_mm(pr[:, 512:1024], cOn32[:, :], c4_rhs2[:, 512:1024],
                          False, True, inc=True)
            LB["rhs2"] = m_d
            e1 = b.v(lambda e, p=par: e.tensor_tensor(
                out=c_ra[p][:, :], in0=pr[:, :], in1=csbA[:, :], op=ALU.add),
                waits=[m_d, LB.get(f"ra{par}", (None, 0))])
            e2 = b.v(lambda e, p=par: e.scalar_tensor_tensor(
                out=c_rm[p][:, :], in0=pr[:, :], scalar=-1.0, in1=csbM[:, :],
                op0=ALU.mult, op1=ALU.add))
            evs[c] = e2
            LB["pr_sl"] = e2
            x1 = b.a(lambda e, p=par: e.activation(
                out=c_ra[p][:, 0:512], in_=c_ra[p][:, 0:512], func=AF.Exp),
                waits=[e2])
            b.a(lambda e, p=par: e.activation(
                out=c_ra[p][:, 512:1024], in_=c_ra[p][:, 512:1024],
                func=AF.Exp))
            b.a(lambda e, p=par: e.activation(
                out=c_rm[p][:, 0:512], in_=c_rm[p][:, 0:512], func=AF.Exp))
            x4 = b.a(lambda e, p=par: e.activation(
                out=c_rm[p][:, 512:1024], in_=c_rm[p][:, 512:1024],
                func=AF.Exp))
            exps[c] = x4

        def stage3(c):
            par = c % 2
            bk = bank[par]
            bn = "pg" if par == 0 else "pj"
            col = c * C
            for wv in range(4):
                toks = []
                mwaits = [exps[c], LB.get(bn, (None, 0)),
                          t_rg if wv == 0 else (None, 0)]
                for i, n in enumerate((wv * 2, wv * 2 + 1)):
                    hcol = (c % 2) * 128
                    ksl = a_kg[:, n * 256 + hcol: n * 256 + hcol + C]
                    qsl = a_qg[:, n * 256 + hcol: n * 256 + hcol + C]
                    toks.append(b.pe_mm(bk[:, i * 256:i * 256 + 128], ksl, ksl,
                                        True, True,
                                        waits=mwaits if i == 0 else (),
                                        inc=True))
                    toks.append(b.pe_mm(bk[:, i * 256 + 128:i * 256 + 256],
                                        ksl, qsl, True, True, inc=True))
                last = None
                for i, n in enumerate((wv * 2, wv * 2 + 1)):
                    s2 = slice(n * C, (n + 1) * C)
                    pK = bk[:, i * 256:i * 256 + 128]
                    pQ = bk[:, i * 256 + 128:i * 256 + 256]
                    if n == 0:
                        tA[c] = b.v(lambda e, s2=s2, p=pK, pr_=par:
                                    e.tensor_tensor(
                            out=c4_A[:, col:col + 128], in0=p,
                            in1=c_ra[pr_][:, s2], op=ALU.mult),
                            waits=[toks[-1]])
                        last = tM[c] = b.v(lambda e, s2=s2, p=pQ, pr_=par:
                                           e.tensor_tensor(
                            out=c4_Mt[:, col:col + 128], in0=p,
                            in1=c_rm[pr_][:, s2], op=ALU.mult))
                    else:
                        b.v(lambda e, s2=s2, p=pK, pr_=par: e.tensor_tensor(
                            out=c_tt[:, :], in0=p, in1=c_ra[pr_][:, s2],
                            op=ALU.mult), waits=[toks[-1]])
                        b.v(lambda e, s2=s2, p=pQ, pr_=par: e.tensor_tensor(
                            out=c_tt2[:, :], in0=p, in1=c_rm[pr_][:, s2],
                            op=ALU.mult))
                        tA[c] = b.v(lambda e: e.tensor_tensor(
                            out=c4_A[:, col:col + 128],
                            in0=c4_A[:, col:col + 128], in1=c_tt[:, :],
                            op=ALU.add))
                        last = tM[c] = b.v(lambda e: e.tensor_tensor(
                            out=c4_Mt[:, col:col + 128],
                            in0=c4_Mt[:, col:col + 128], in1=c_tt2[:, :],
                            op=ALU.add))
                LB[bn] = last
            tA[c] = b.v(lambda e, g=bcols[c]: e.tensor_scalar(
                out=c4_A[:, col:col + 128], in0=c4_A[:, col:col + 128],
                scalar1=g, scalar2=None, op0=ALU.mult), waits=[tA[c]])
            LB[f"ra{par}"] = tA[c]

        for c in range(4):
            stage2(c)
            if c > 0:
                stage3(c - 1)
            if c == 2:
                t_rg = regroup(1, [tA[1]])
        stage3(3)
        if CSTG == 2:
            return
        t10 = None
        for c in range(4):
            t10 = b.pe_mm(px[:, 128 * c + 0:128 * (c + 1)] if False else
                          px[:, c * 128:(c + 1) * 128],
                          cE1c[:, c * 128:(c + 1) * 128], c4_gf[:, :],
                          True, True,
                          waits=[t4, LB.get("px_t2", (None, 0)),
                                 LB.get("px_act", (None, 0))]
                          if c == 0 else (), inc=(c == 3))
        t11 = b.a(lambda e: e.activation(out=c4_eg[:, :], in_=px[:, 0:512],
                                         func=AF.Exp), waits=[t10])
        LB["px_act"] = t11
        t12 = b.v(lambda e: e.tensor_scalar(out=c4_r2[:, :], in0=c4_gf[:, :],
                                            scalar1=c4_gf[:, 127:128],
                                            scalar2=-1.0, op0=ALU.subtract,
                                            op1=ALU.mult), waits=[t4])
        t13 = None
        for c in range(4):
            t13 = b.pe_mm(ps[:, c * 128:(c + 1) * 128],
                          cE1c[:, c * 128:(c + 1) * 128], c4_r2[:, :],
                          True, True,
                          waits=[t12, t11, LB.get("ps_sl", (None, 0))]
                          if c == 0 else (), inc=(c == 3))
        t14 = b.a(lambda e: e.activation(out=c4_kh[:, :], in_=ps[:, 0:512],
                                         func=AF.Exp), waits=[t13])
        LB["ps_act"] = t14
        t15 = b.v(lambda e: e.tensor_tensor(out=c4_kt[:, :], in0=a_kf[:, :],
                                            in1=c4_eg[:, :], op=ALU.mult),
                  waits=[t11])
        t16 = b.v(lambda e: e.tensor_tensor(out=c4_qt[:, :], in0=a_qf[:, :],
                                            in1=c4_eg[:, :], op=ALU.mult))
        t17 = b.v(lambda e: e.tensor_tensor(out=c4_kh[:, :], in0=a_kf[:, :],
                                            in1=c4_kh[:, :], op=ALU.mult),
                  waits=[t14])
        if CSTG == 3:
            return
        ptq2 = [LB.get("ptq0", (None, 0)), LB.get("ptq1", (None, 0))]

        def trb(c):
            bk = pt if c % 2 == 0 else pj2
            return bk[:, 0:128]

        def tr_ev(src_sl, dst_fn, waits_in, scale_beta):
            toks = [None] * 4
            for p_ in range(2):
                tp_ = [None] * 2
                for i_ in range(2):
                    c = 2 * p_ + i_
                    w = [waits_in[c] if isinstance(waits_in, list) else waits_in,
                         ptq2[i_]]
                    if isinstance(waits_in, dict):
                        w = [waits_in.get(c, (None, 0)), ptq2[i_]]
                    tp_[i_] = b.pe_tr(trb(c), src_sl(c), cs_id[:, :], waits=w)
                for i_ in range(2):
                    c = 2 * p_ + i_
                    if scale_beta:
                        tv_ = b.v(lambda e, c=c, d=dst_fn(c): e.tensor_scalar(
                            out=d, in0=trb(c), scalar1=bcols[c], scalar2=None,
                            op0=ALU.mult), waits=[tp_[i_]])
                    else:
                        tv_ = b.v(lambda e, c=c, d=dst_fn(c): e.tensor_copy(
                            out=d, in_=trb(c)), waits=[tp_[i_]])
                    ptq2[i_] = tv_
                    toks[c] = tv_
            return toks

        tKt = tr_ev(lambda c: c4_kt[:, c * 128:(c + 1) * 128],
                    lambda c: c4_X[:, c * 256: c * 256 + 128], t15, True)
        tV = tr_ev(lambda c: a_vf[:, c * 128:(c + 1) * 128],
                   lambda c: c4_X[:, c * 256 + 128: c * 256 + 256],
                   (None, 0), True)
        tKh = tr_ev(lambda c: c4_kh[:, c * 128:(c + 1) * 128],
                    lambda c: c4_khT[:, c * 128:(c + 1) * 128], t17, False)
        tAt = tr_ev(lambda c: c4_A[:, c * 128:(c + 1) * 128],
                    lambda c: c4_At[:, c * 128:(c + 1) * 128], tA, False)

        if CSTG == 4:
            LB["ptq0"] = ptq2[0]
            LB["ptq1"] = ptq2[1]
            return

        def prq(c):
            return pr[:, (c % 2) * 512:(c % 2) * 512 + 256]

        def sqb(c):
            bk = ps if c % 2 == 0 else px
            return bk[:, 256:384]

        tX = [None] * 4
        for p_ in range(2):
            cs2 = (2 * p_, 2 * p_ + 1)
            m_ap = {}
            for c in cs2:
                m_ap[c] = b.pe_mm(prq(c), c4_At[:, c * 128:(c + 1) * 128],
                                  c4_X[:, c * 256:(c + 1) * 256], True, True,
                                  waits=[tAt[c], tV[c]] +
                                  ([LB.get("pr_sl", (None, 0))]
                                   if c == 0 else []) +
                                  ([tX[c - 2]] if c >= 2 else []), inc=True)
            for c in cs2:
                tX[c] = b.v(lambda e, c=c: e.tensor_sub(
                    out=c4_X[:, c * 256:(c + 1) * 256],
                    in0=c4_X[:, c * 256:(c + 1) * 256], in1=prq(c)),
                    waits=[m_ap[c]])
            cur, curT = c4_A, c4_At
            nxt, nxtT = c4_Ap, c4_ApT
            lvl = 2
            sq_ev = {c: None for c in cs2}
            while lvl < C:
                m_sq = {}
                for c in cs2:
                    m_sq[c] = b.pe_mm(sqb(c), curT[:, c * 128:(c + 1) * 128],
                                      cur[:, c * 128:(c + 1) * 128], True,
                                      True,
                                      waits=[tX[c]] +
                                      ([sq_ev[c]] if sq_ev[c] else
                                       ([LB.get("ps_act", (None, 0)), t16]
                                        if p_ == 0 else [])), inc=True)
                for c in cs2:
                    sq_ev[c] = b.v(lambda e, c=c, nx=nxt: e.tensor_copy(
                        out=nx[:, c * 128:(c + 1) * 128], in_=sqb(c)),
                        waits=[m_sq[c]])
                trt = {}
                for c in cs2:
                    trt[c] = b.pe_tr(trb(c), nxt[:, c * 128:(c + 1) * 128],
                                     cs_id[:, :], waits=[sq_ev[c],
                                                         ptq2[c % 2]])
                for c in cs2:
                    ptq2[c % 2] = b.v(lambda e, c=c, nx=nxtT: e.tensor_copy(
                        out=nx[:, c * 128:(c + 1) * 128], in_=trb(c)),
                        waits=[trt[c]])
                for c in cs2:
                    m_ap[c] = b.pe_mm(prq(c), nxtT[:, c * 128:(c + 1) * 128],
                                      c4_X[:, c * 256:(c + 1) * 256], True,
                                      True, waits=[ptq2[c % 2], tX[c]],
                                      inc=True)
                for c in cs2:
                    tX[c] = b.v(lambda e, c=c: e.tensor_add(
                        out=c4_X[:, c * 256:(c + 1) * 256],
                        in0=c4_X[:, c * 256:(c + 1) * 256], in1=prq(c)),
                        waits=[m_ap[c]])
                cur, curT, nxt, nxtT = nxt, nxtT, cur, curT
                lvl *= 2
        LB["ptq0"] = ptq2[0]
        LB["ptq1"] = ptq2[1]
        LB["ps_sl"] = sq_ev[2]
        LB["px_act"] = sq_ev[3]
        LB["pr_sl"] = tX[3]
        if CSTG == 5:
            return
        qof = lambda c: (c // 2) * 256
        m1 = [None] * 4
        for c in range(4):
            m1[c] = b.pe_mm(bank[c % 2][:, qof(c):qof(c) + 128],
                            c4_X[:, c * 256: c * 256 + 128],
                            c4_khT[:, c * 128:(c + 1) * 128], True, True,
                            waits=[tX[c], tKh[c]] +
                            ([LB.get("pg", (None, 0)), LB.get("pj", (None, 0))]
                             if c == 0 else []), inc=True)
        tms = [None] * 4
        for c in range(4):
            b.v(lambda e, c=c: e.tensor_scalar(
                out=c4_ms[:, c * 128:(c + 1) * 128], in0=cs_id[:, :],
                scalar1=c4_eg[:, c * 128 + 127: c * 128 + 128], scalar2=None,
                op0=ALU.mult), waits=[m1[c], m1[c + 2] if c < 2 else (None, 0)])
            tms[c] = b.v(lambda e, c=c: e.tensor_sub(
                out=c4_ms[:, c * 128:(c + 1) * 128],
                in0=c4_ms[:, c * 128:(c + 1) * 128],
                in1=bank[c % 2][:, qof(c):qof(c) + 128]))
        m2 = [None] * 4
        for c in range(4):
            m2[c] = b.pe_mm(bank[c % 2][:, qof(c) + 128:qof(c) + 256],
                            c4_khT[:, c * 128:(c + 1) * 128],
                            c4_X[:, c * 256 + 128: c * 256 + 256], True, True,
                            waits=[tms[2 + (c % 2)]] if c < 2 else (),
                            inc=True)
        tns = [None] * 4
        for c in range(4):
            tns[c] = b.v(lambda e, c=c: e.tensor_copy(
                out=c4_ns[:, c * 128:(c + 1) * 128],
                in_=bank[c % 2][:, qof(c) + 128:qof(c) + 256]),
                waits=[m2[c], m2[c + 2] if c < 2 else (None, 0)])
        m3 = [None] * 4
        for c in range(4):
            m3[c] = b.pe_mm(bank[c % 2][:, qof(c):qof(c) + 128],
                            c4_X[:, c * 256: c * 256 + 128],
                            c4_Mt[:, c * 128:(c + 1) * 128], True, True,
                            waits=[tms[c], tns[2 + (c % 2)] if c < 2 else
                                   (None, 0)], inc=True)
        tq2 = [None] * 4
        for c in range(4):
            tq2[c] = b.v(lambda e, c=c: e.tensor_sub(
                out=c4_q2[:, c * 128:(c + 1) * 128],
                in0=c4_qt[:, c * 128:(c + 1) * 128],
                in1=bank[c % 2][:, qof(c):qof(c) + 128]),
                waits=[m3[c], m3[c + 2] if c < 2 else (None, 0),
                       t16 if c == 0 else (None, 0)])
        m4 = [None] * 4
        for c in range(4):
            m4[c] = b.pe_mm(bank[c % 2][:, qof(c) + 128:qof(c) + 256],
                            c4_Mt[:, c * 128:(c + 1) * 128],
                            c4_X[:, c * 256 + 128: c * 256 + 256], True, True,
                            waits=[tns[c], tq2[2 + (c % 2)] if c < 2 else
                                   (None, 0)], inc=True)
        tyl = [None] * 4
        for c in range(4):
            tyl[c] = b.v(lambda e, c=c: e.tensor_copy(
                out=c4_yl[:, c * 128:(c + 1) * 128],
                in_=bank[c % 2][:, qof(c) + 128:qof(c) + 256]),
                waits=[m4[c], m4[c + 2] if c < 2 else (None, 0)])
        LB["pg"] = tyl[2]
        LB["pj"] = tyl[3]
        LB["pj2"] = ptq2[1]
        for c in range(4):
            cg = h * NCH + blk * 4 + c
            r0 = cg * 128
            b.dma("gpsimd", sp_mst[r0:r0 + 128, :],
                  c4_ms[:, c * 128:(c + 1) * 128], stream="sp",
                  waits=[tms[c]] + ([b.P(b.Q["sp"])] if "sp" in b.Q and c == 0
                                    else []))
            b.dma("gpsimd", sp_ns[r0:r0 + 128, :],
                  c4_ns[:, c * 128:(c + 1) * 128], stream="sp",
                  waits=[tns[c]])
            b.dma("gpsimd", sp_q2[r0:r0 + 128, :],
                  c4_q2[:, c * 128:(c + 1) * 128], stream="sp",
                  waits=[tq2[c]])
            b.dma("gpsimd", sp_yl[r0:r0 + 128, :],
                  c4_yl[:, c * 128:(c + 1) * 128], stream="sp",
                  waits=[tyl[c]])

    HSEL = list(range(H)) if heads is None else heads
    BSEL = list(range(NBLK)) if blocks is None else blocks
    if run["A"]:
        for blk in BSEL:
            deps = [t_consts, t_splits, t_gws, t_mask, t_ldc, b.P(b.pe),
                    b.P(b.ve), b.P(b.ac)]
            if "sp" in b.Q:
                deps.append(b.P(b.Q["sp"]))
            t_x = None
            for i in range(5):
                row0 = blk * BT + i * 128
                t_x = ln_transpose(xs[row0:row0 + 128, :], xlt_h, xlt_l, 640,
                                   i * 128, deps if i == 0 else [t_x])
            tgh = mm3pass(pj[:, 0:512], None,
                          lambda k: wf1h[:, k * 128:(k + 1) * 128],
                          lambda k: wf1l[:, k * 128:(k + 1) * 128], 128, 512,
                          waits=[t_x, b.P(b.ac)])
            tev = b.v(lambda e: e.tensor_copy(out=a_t1[:, :], in_=pj[:, 0:512]),
                      waits=[tgh])
            tv1 = None
            for cl in range(4):
                tm = b.pe_mm(pg[:, cl * 128:(cl + 1) * 128],
                             a_t1[:, cl * 128:(cl + 1) * 128], cs_wf2[:, :],
                             True, True, waits=[tev] if cl == 0 else [tv1],
                             inc=True)
                tv1 = b.v(lambda e, cl=cl: e.tensor_tensor(
                    out=a_ghat[:, cl * 128:(cl + 1) * 128],
                    in0=pg[:, cl * 128:(cl + 1) * 128], in1=cs_dtb[:, :],
                    op=ALU.add), waits=[tm])
                tv2a = b.a(lambda e, cl=cl: e.activation(
                    out=a_ghat[:, cl * 128:(cl + 1) * 128],
                    in_=a_ghat[:, cl * 128:(cl + 1) * 128], func=AF.Exp),
                    waits=[tv1])
                tv2 = b.a(lambda e, cl=cl: e.activation(
                    out=a_ghat[:, cl * 128:(cl + 1) * 128],
                    in_=a_ghat[:, cl * 128:(cl + 1) * 128], func=AF.Ln,
                    bias=cs_af[:, 4:5]), waits=[tv2a])
                b.v(lambda e, cl=cl: e.tensor_tensor(
                    out=a_ghat[:, cl * 128:(cl + 1) * 128],
                    in0=a_ghat[:, cl * 128:(cl + 1) * 128], in1=cs_ascl[:, :],
                    op=ALU.mult), waits=[tv2])
            tb_prev = None
            for cl in range(4):
                xc = 128 + cl * 128
                parts = []
                for k in range(16):
                    xhi = xlt_h[:, k * 640 + xc: k * 640 + xc + 128]
                    xlo = xlt_l[:, k * 640 + xc: k * 640 + xc + 128]
                    whi = wbh[:, k * 16:(k + 1) * 16]
                    wlo = wbl[:, k * 16:(k + 1) * 16]
                    for (xo, wop) in ((xhi, whi), (xlo, whi), (xhi, wlo)):
                        parts.append((px[:, 384 + cl * 16:400 + cl * 16],
                                      xo, wop, False, False))
                parts[0] = parts[0][:3] + (True, False)
                parts[-1] = parts[-1][:4] + (True,)
                tb = b.mm_group(parts, waits=[t_x, b.P(b.ac)] if cl == 0
                                else [tb_prev])
                tb_prev = b.a(lambda e, cl=cl: e.activation(
                    out=a_beta[:, cl * 16:(cl + 1) * 16],
                    in_=px[:, 384 + cl * 16:400 + cl * 16], func=AF.Sigmoid),
                    waits=[tb])
            wpar = 0
            land_free = [None, None]
            hilo_free = [None, None]


            def load_w(srcfn):
                nonlocal wpar
                par = wpar
                wpar = 1 - wpar
                w0 = [land_free[par]] if land_free[par] else []
                for k in range(16):
                    b.dma("sync", a_wl[par][:, k * 128:(k + 1) * 128], srcfn(k),
                          stream=f"wl{par}", waits=w0 if k == 0 else ())
                t_w = b.P(b.Q[f"wl{par}"])
                ws = [t_w] + ([hilo_free[par]] if hilo_free[par] else [])
                t_ws = b.split_sg(a_wl[par][:, :], a_wh[par][:, :],
                                  a_wlo[par][:, :], waits=ws)
                land_free[par] = t_ws
                return par, t_ws

            for h in HSEL:
                conv_done = []
                for ti, (wt, cw_off) in enumerate(((wq, 0), (wk, 64), (wv, 128))):
                    par, t_ws = load_w(lambda k, wt=wt, hh=h: wt[
                        k * 128:(k + 1) * 128, hh * 128:(hh + 1) * 128])
                    tp = mm3pass(pj[:, 0:512], px[:, 0:3],
                                 lambda k, p=par: a_wh[p][:, k * 128:(k + 1) * 128],
                                 lambda k, p=par: a_wlo[p][:, k * 128:(k + 1) * 128],
                                 125, 515,
                                 waits=[t_ws, t_x, LB.get("pj", (None, 0)),
                                        LB.get("px3", (None, 0)),
                                        LB.get("px_t2", (None, 0)),
                                        LB.get("px_act", (None, 0))])
                    hilo_free[par] = tp
                    te1 = b.v(lambda e: e.tensor_copy(out=a_pre[:, 0:512],
                                                      in_=pj[:, 0:512]),
                              waits=[tp])
                    te2 = b.v(lambda e: e.tensor_copy(out=a_pre[:, 512:515],
                                                      in_=px[:, 0:3]))
                    LB["pj"] = te1
                    LB["px3"] = te2
                    dst = (a_qf, a_kf, a_vf)[ti]
                    t_cv = b.v(lambda e, d=dst, co=cw_off, hh=h: e.tensor_scalar(
                        out=d[:, :], in0=a_pre[:, 0:512],
                        scalar1=cs_cw[:, co + hh * 4: co + hh * 4 + 1],
                        scalar2=None, op0=ALU.mult),
                        waits=[te1, te2,
                               b.P(b.Q["rg0"]) if "rg0" in b.Q else (None, 0),
                               b.P(b.Q["rg1"]) if "rg1" in b.Q else (None, 0)],
                        dr=True)
                    for j in range(1, 4):
                        t_cv = b.v(lambda e, d=dst, co=cw_off, j=j, hh=h:
                                   e.scalar_tensor_tensor(
                                       out=d[:, :], in0=a_pre[:, j:j + 512],
                                       scalar=cs_cw[:, co + hh * 4 + j:
                                                    co + hh * 4 + j + 1],
                                       in1=d[:, :], op0=ALU.mult, op1=ALU.add),
                                   dr=(j < 3))
                    t_si = b.a(lambda e, d=dst: e.activation(out=d[:, :],
                                                             in_=d[:, :],
                                                             func=AF.Silu),
                               waits=[t_cv])
                    conv_done.append(t_si)
                for dst, t_pr, bcol_, scol_ in ((a_qf, conv_done[0], 1, 3),
                                                (a_kf, conv_done[1], 2, 4)):
                    tsq = b.a(lambda e, d=dst: e.square(out=a_sq[:, :],
                                                        in_=d[:, :]),
                              waits=[t_pr])
                    tmm = b.pe_mm(ps[0:1, 0:512], cs_oc[:, 0:1], a_sq[:, :],
                                  True, True,
                                  waits=[tsq, LB.get("ps_row", (None, 0)),
                                         LB.get("ps_sl", (None, 0)),
                                         LB.get("px_act", (None, 0))],
                                  inc=True)
                    trsa = b.a(lambda e, bc=bcol_, sc=scol_: e.activation(
                        out=a_srow[0:1, :], in_=ps[0:1, 0:512], func=AF.Sqrt,
                        bias=cs_af[0:1, bc:bc + 1],
                        scale=cs_af[0:1, sc:sc + 1]), waits=[tmm])
                    LB["ps_row"] = trsa
                    trs = b.v(lambda e: e.reciprocal(out=a_srow[0:1, :],
                                                     in_=a_srow[0:1, :]),
                              waits=[trsa])
                    tbc = b.pe_mm(pg[:, 0:512], cs_or[0:1, :], a_srow[0:1, :],
                                  True, True,
                                  waits=[trs, LB.get("pg_l2", (None, 0)),
                                         LB.get("pg", (None, 0))],
                                  inc=True)
                    tml = b.v(lambda e, d=dst: e.tensor_tensor(
                        out=d[:, :], in0=d[:, :], in1=pg[:, 0:512],
                        op=ALU.mult), waits=[tbc])
                    LB["pg_l2"] = tml
                t_qk = b.P(b.ve)
                if dbg_qkv is not None:
                    for ti2, src2 in enumerate((a_qf, a_kf, a_vf)):
                        b.dma("gpsimd",
                              dbg_qkv[(ti2 * H + h) * 128:(ti2 * H + h + 1) * 128,
                                      blk * BT:(blk + 1) * BT],
                              src2[:, :], stream="dbgq",
                              waits=[t_qk, conv_done[2]])
                par, t_ws = load_w(lambda k, hh=h: wg[k * 128:(k + 1) * 128,
                                                      hh * 128:(hh + 1) * 128])
                tg = mm3pass(pj2[:, 0:512], None,
                             lambda k, p=par: a_wh[p][:, k * 128:(k + 1) * 128],
                             lambda k, p=par: a_wlo[p][:, k * 128:(k + 1) * 128],
                             128, 512,
                             waits=[t_ws, LB.get("pj2", (None, 0))])
                hilo_free[par] = tg
                tsg = b.a(lambda e, hh=h: e.activation(out=a_sg[:, :],
                                                       in_=pj2[:, 0:512],
                                                       func=AF.Silu,
                                                       bias=cs_bg[:, hh:hh + 1]),
                          waits=[tg, b.P(b.Q["sg"]) if "sg" in b.Q else (None, 0)])
                LB["pj2"] = tsg
                b.dma("gpsimd", sp_sg[h * 128:(h + 1) * 128,
                                      blk * BT:(blk + 1) * BT],
                      a_sg[:, :], stream="sg", waits=[tsg])
                t_in = [t_qk, conv_done[2], b.P(b.ac)]
                if "sp" in b.Q:
                    t_in.append(b.P(b.Q["sp"]))
                chunk_math4(h, blk, t_in)

    stkA.close()

    b_s = [E(nc.sbuf_tensor(f"b_s{i}", [128, 128], F32)) for i in range(2)]
    b_m = [E(nc.sbuf_tensor(f"b_m{i}", [128, 128], F32)) for i in range(2)]
    b_n = [E(nc.sbuf_tensor(f"b_n{i}", [128, 128], F32)) for i in range(2)]
    if run["B"]:
        for h in range(H):
            tz = b.v(lambda e: e.memset(b_s[0][:, :], 0.0),
                     waits=[b.P(b.Q["sp"]), b.P(b.Q["sg"]), b.P(b.pe)]
                     if h == 0 else
                     ([b.P(b.Q["ccin"])] if "ccin" in b.Q else []))
            t_pv = tz
            for ci in range(NCH):
                cg = h * NCH + ci
                par = ci % 2
                r0 = cg * 128
                tm1 = b.dma("sync", b_m[par][:, :], sp_mst[r0:r0 + 128, :],
                            stream=f"pb{par}", waits=[t_pv])
                tm2 = b.dma("sync", b_n[par][:, :], sp_ns[r0:r0 + 128, :],
                            stream=f"pb{par}")
                sprev = b_s[ci % 2]
                snext = b_s[1 - ci % 2]
                tmm = b.pe_mm(px[:, 0:128], b_m[par][:, :], sprev[:, :],
                              True, True, waits=[tm1, t_pv], inc=True)
                t_pv = b.v(lambda e, sn=snext, pn=b_n[par]: e.tensor_add(
                    out=sn[:, :], in0=pn[:, :], in1=px[:, 0:128]),
                    waits=[tmm, tm2], dr=(ci == NCH - 1))
            send = b_s[0]
            tsc = b.v(lambda e, s=send: e.tensor_scalar(
                out=s[:, :], in0=s[:, :], scalar1=cs_iev[:, 0:1], scalar2=None,
                op0=ALU.mult), waits=[t_pv])
            b.dma("gpsimd", cc_in.ap()[h * 128:(h + 1) * 128, :], send[:, :],
                  stream="ccin", waits=[tsc])
            if dbg:
                b.dma("gpsimd", dbg_send[h * 128:(h + 1) * 128, :], send[:, :],
                      stream="dbg")
        b.emit("gpsimd", lambda e: e.collective_compute(
            "AllReduce", ALU.add,
            replica_groups=[[0, 1], [2, 3], [4, 5], [6, 7]],
            ins=[cc_in.ap().opt()], outs=[cc_out.ap().opt()]),
            waits=[b.P(b.Q["ccin"])], inc=b.cc, d=1)

    yslab_h = EC(nc.sbuf_tensor("yslab_h", [128, H * TLOC], BF16))
    yslab_l = EC(nc.sbuf_tensor("yslab_l", [128, H * TLOC], BF16))
    c_rl = [[EC(nc.sbuf_tensor(f"c_rl{i}_{j}", [128, 128], F32)) for j in range(4)]
            for i in range(2)]
    c_sgC = [EC(nc.sbuf_tensor(f"c_sgC{i}", [128, BT], F32)) for i in range(2)]
    c_O = EC(nc.sbuf_tensor("c_O", [128, 128], F32))
    c_y = EC(nc.sbuf_tensor("c_y", [128, 128], F32))
    c_y2 = EC(nc.sbuf_tensor("c_y2", [128, 128], F32))
    if run["C"]:
        for h in range(H):
            tl = b.dma("sync", b_s[0][:, :], cc_out.ap()[h * 128:(h + 1) * 128, :],
                       stream="si", waits=[(b.cc, 1), b.P(b.ve)])
            t_pv = b.v(lambda e: e.tensor_scalar(out=b_s[0][:, :], in0=b_s[0][:, :],
                                                 scalar1=cs_iod[:, 0:1],
                                                 scalar2=None, op0=ALU.mult),
                       waits=[tl])
            tsg = None
            for ci in range(NCH):
                cg = h * NCH + ci
                par = ci % 2
                r0 = cg * 128
                rl = c_rl[par]
                tr1 = b.dma("sync", rl[0][:, :], sp_mst[r0:r0 + 128, :],
                            stream=f"pc{par}", waits=[t_pv])
                tr2 = b.dma("sync", rl[1][:, :], sp_ns[r0:r0 + 128, :],
                            stream=f"pc{par}")
                tr3 = b.dma("sync", rl[2][:, :], sp_q2[r0:r0 + 128, :],
                            stream=f"pc{par}")
                tr4 = b.dma("sync", rl[3][:, :], sp_yl[r0:r0 + 128, :],
                            stream=f"pc{par}")
                if ci % 4 == 0:
                    tsg = b.dma("sync", c_sgC[(ci // 4) % 2][:, :],
                                sp_sg[h * 128:(h + 1) * 128,
                                      (ci // 4) * BT:(ci // 4 + 1) * BT],
                                stream="sgc", waits=[b.P(b.pe)])
                sgb = c_sgC[(ci // 4) % 2]
                sprev = b_s[ci % 2]
                snext = b_s[1 - ci % 2]
                to1 = b.pe_mm(pg[:, 0:128], rl[2][:, :], sprev[:, :], True, True,
                              waits=[tr3, t_pv], inc=True)
                to2 = b.v(lambda e, r=rl[3]: e.tensor_add(out=c_O[:, :],
                                                          in0=pg[:, 0:128],
                                                          in1=r[:, :]),
                          waits=[to1, tr4])
                tc1 = b.pe_mm(pg[:, 128:256], rl[0][:, :], sprev[:, :], True, True,
                              waits=[tr1, to2], inc=True)
                t_pv = b.v(lambda e, sn=snext, r=rl[1]: e.tensor_add(
                    out=sn[:, :], in0=r[:, :], in1=pg[:, 128:256]),
                    waits=[tc1, tr2])
                tt1 = b.pe_tr(pt[:, 0:128],
                              sgb[:, (ci % 4) * 128:(ci % 4 + 1) * 128],
                              cs_id[:, :],
                              waits=[tsg, to2] if ci % 4 == 0 else [to2])
                ty1 = b.v(lambda e: e.tensor_tensor(out=c_y[:, :], in0=c_O[:, :],
                                                    in1=pt[:, 0:128], op=ALU.mult),
                          waits=[tt1], dr=True)
                ty2 = b.v(lambda e: e.tensor_tensor(out=c_y2[:, :], in0=c_y[:, :],
                                                    in1=c_y[:, :], op=ALU.mult),
                          dr=True)
                ty3 = b.v(lambda e: e.tensor_reduce(out=c_t[:, 6:7], in_=c_y2[:, :],
                                                    axis=AX.X, op=ALU.add),
                          waits=[ty2])
                ty4a = b.a(lambda e: e.activation(out=c_t[:, 30:31],
                                                  in_=c_t[:, 6:7],
                                                  func=AF.Sqrt,
                                                  bias=cs_af[:, 0:1],
                                                  scale=cs_af[:, 5:6]),
                           waits=[ty3])
                ty4 = b.v(lambda e: e.reciprocal(out=c_t[:, 7:8],
                                                 in_=c_t[:, 30:31]),
                          waits=[ty4a], dr=True)
                ty5 = b.v(lambda e: e.scalar_tensor_tensor(
                    out=c_y2[:, :], in0=c_y[:, :], scalar=c_t[:, 7:8],
                    in1=cs_nw[:, :], op0=ALU.mult, op1=ALU.mult), waits=[ty4])
                tt2 = b.pe_tr(pt[:, 128:256], c_y2[:, :], cs_id[:, :], waits=[ty5])
                colh = h * TLOC + ci * 128
                b.split_av(pt[:, 128:256], yslab_h[:, colh:colh + 128],
                           yslab_l[:, colh:colh + 128], waits=[tt2])
        if dbg:
            tA = b.P(b.ve)
            for h in range(H):
                b.dma("gpsimd", dbg_y[h * 128:(h + 1) * 128, :],
                      yslab_h[:, h * TLOC:(h + 1) * TLOC], stream="dbg",
                      waits=[tA] if h == 0 else ())

    d_wl = EC(nc.sbuf_tensor("d_wl", [128, 16 * 256], F32))
    d_wh = EC(nc.sbuf_tensor("d_wh", [128, 16 * 256], BF16))
    d_wlo = EC(nc.sbuf_tensor("d_wlo", [128, 16 * 256], BF16))
    d_xr = [EC(nc.sbuf_tensor(f"d_xr{i}", [128, 256], F32)) for i in range(2)]
    d_x1 = [EC(nc.sbuf_tensor(f"d_x1{i}", [128, 256], F32)) for i in range(2)]
    if run["D"]:
        t_y = b.P(b.ve)
        for dc in range(8):
            for k in range(16):
                b.dma("sync", d_wl[:, k * 256:(k + 1) * 256],
                      wo[k * 128:(k + 1) * 128, dc * 256:(dc + 1) * 256],
                      stream="wo", waits=[b.P(b.ve)] if k == 0 else ())
            t_w = b.P(b.Q["wo"])
            t_ws = b.split_sg(d_wl[:, :], d_wh[:, :], d_wlo[:, :],
                              waits=[t_w, b.P(b.pe)])
            for tt in range(16):
                par = tt % 2
                txr = b.dma("sync", d_xr[par][:, :],
                            xs[HALO + tt * 128:HALO + (tt + 1) * 128,
                               dc * 256:(dc + 1) * 256],
                            stream=f"xr{par}", waits=[b.P(b.ve)])
                parts = []
                for k in range(16):
                    yh = yslab_h[:, k * TLOC + tt * 128: k * TLOC + (tt + 1) * 128]
                    yl2 = yslab_l[:, k * TLOC + tt * 128: k * TLOC + (tt + 1) * 128]
                    wh2 = d_wh[:, k * 256:(k + 1) * 256]
                    wl2 = d_wlo[:, k * 256:(k + 1) * 256]
                    for (yo, wop) in ((yh, wh2), (yh, wl2), (yl2, wh2)):
                        parts.append((pj[:, 0:256], yo, wop, False, False))
                parts[0] = parts[0][:3] + (True, False)
                parts[-1] = parts[-1][:4] + (True,)
                tm = b.mm_group(parts, waits=[t_ws, t_y, b.P(b.ve)])
                tev = b.v(lambda e, p=par: e.tensor_add(out=d_x1[p][:, :],
                                                        in0=pj[:, 0:256],
                                                        in1=d_xr[p][:, :]),
                          waits=[tm, txr,
                                 b.P(b.Q["x1"]) if "x1" in b.Q else (None, 0)])
                b.dma("gpsimd", sp_x1[tt * 128:(tt + 1) * 128,
                                      dc * 256:(dc + 1) * 256],
                      d_x1[par][:, :], stream="x1", waits=[tev])

    stkCD.close()

    e_acc = EE(nc.sbuf_tensor("e_acc", [128, (EHALF // 128) * D], F32))
    e_x2h = EE(nc.sbuf_tensor("e_x2h", [128, 16 * EHALF], BF16))
    e_x2l = EE(nc.sbuf_tensor("e_x2l", [128, 16 * EHALF], BF16))
    e_t = EE(nc.sbuf_tensor("e_t", [128, D], F32))
    e_h2h = EE(nc.sbuf_tensor("e_h2h", [128, 2 * EHALF], BF16))
    e_h2l = EE(nc.sbuf_tensor("e_h2l", [128, 2 * EHALF], BF16))
    e_w1l = [EE(nc.sbuf_tensor(f"e_w1l{i}", [128, D], F32)) for i in range(2)]
    e_w1h = [EE(nc.sbuf_tensor(f"e_w1h{i}", [128, D], BF16)) for i in range(2)]
    e_w1o = [EE(nc.sbuf_tensor(f"e_w1o{i}", [128, D], BF16)) for i in range(2)]
    e_w3l = [EE(nc.sbuf_tensor(f"e_w3l{i}", [128, D], F32)) for i in range(2)]
    e_w3h = [EE(nc.sbuf_tensor(f"e_w3h{i}", [128, D], BF16)) for i in range(2)]
    e_w3o = [EE(nc.sbuf_tensor(f"e_w3o{i}", [128, D], BF16)) for i in range(2)]
    e_w2l = EE(nc.sbuf_tensor("e_w2l", [128, 2 * D], F32))
    e_w2h = EE(nc.sbuf_tensor("e_w2h", [128, 2 * D], BF16))
    e_w2o = EE(nc.sbuf_tensor("e_w2o", [128, 2 * D], BF16))
    e_cw = EE(nc.sbuf_tensor("e_cw", [128, 64], F32))
    e_sc = EE(nc.sbuf_tensor("e_sc", [128, 8], F32))
    if run["E"]:
        for hf in range(TLOC // EHALF):
            t_pd = [b.P(b.Q["x1"]) if "x1" in b.Q else (None, 0),
                    b.P(b.pe), b.P(b.ve)]
            if "out" in b.Q:
                t_pd.append(b.P(b.Q["out"]))
            for tt in range(EHALF // 128):
                row0 = hf * EHALF + tt * 128
                tx = b.dma("sync", e_acc[:, tt * D:(tt + 1) * D],
                           sp_x1[row0:row0 + 128, :], stream="ex",
                           waits=t_pd if tt == 0 else [b.P(b.ve)])
                accs = e_acc[:, tt * D:(tt + 1) * D]
                t1 = b.v(lambda e, a2=accs: e.tensor_reduce(
                    out=c_t[:, 0:1], in_=a2, axis=AX.X, op=ALU.add), waits=[tx])
                t2 = b.a(lambda e, a2=accs: e.square(out=e_t[:, :], in_=a2),
                         waits=[tx, b.P(b.ve)])
                t3 = b.v(lambda e: e.tensor_reduce(out=c_t[:, 1:2], in_=e_t[:, :],
                                                   axis=AX.X, op=ALU.add),
                         waits=[t2])
                t4 = b.v(lambda e: e.tensor_scalar(out=c_t[:, 2:3],
                                                   in0=c_t[:, 0:1],
                                                   scalar1=1.0 / D, scalar2=None,
                                                   op0=ALU.mult), waits=[t1],
                         dr=True)
                t5 = b.v(lambda e: e.tensor_tensor(out=c_t[:, 3:4],
                                                   in0=c_t[:, 2:3],
                                                   in1=c_t[:, 2:3], op=ALU.mult),
                         dr=True)
                t6 = b.v(lambda e: e.scalar_tensor_tensor(
                    out=c_t[:, 4:5], in0=c_t[:, 1:2], scalar=1.0 / D,
                    in1=c_t[:, 3:4], op0=ALU.mult, op1=ALU.subtract), waits=[t3])
                t7a = b.a(lambda e: e.activation(out=c_t[:, 31:32],
                                                 in_=c_t[:, 4:5],
                                                 func=AF.Sqrt,
                                                 bias=cs_af[:, 0:1]), waits=[t6])
                t7 = b.v(lambda e: e.reciprocal(out=c_t[:, 5:6],
                                                in_=c_t[:, 31:32]), waits=[t7a],
                         dr=True)
                t8 = b.v(lambda e, a2=accs: e.tensor_scalar(
                    out=e_t[:, :], in0=a2, scalar1=c_t[:, 2:3],
                    scalar2=c_t[:, 5:6], op0=ALU.subtract, op1=ALU.mult),
                    waits=[t7])
                tv = t8
                for k in range(16):
                    q = k % 4
                    tp = b.pe_tr(pt[:, q * 128:(q + 1) * 128],
                                 e_t[:, k * 128:(k + 1) * 128], cs_id[:, :],
                                 waits=[tv])
                    col = k * EHALF + tt * 128
                    tv = b.split_av(pt[:, q * 128:(q + 1) * 128],
                                    e_x2h[:, col:col + 128],
                                    e_x2l[:, col:col + 128], waits=[tp])
                parts = []
                for k in range(16):
                    xh2 = e_x2h[:, k * EHALF + tt * 128: k * EHALF + (tt + 1) * 128]
                    xl2 = e_x2l[:, k * EHALF + tt * 128: k * EHALF + (tt + 1) * 128]
                    gh2 = gwh[:, k * 8:(k + 1) * 8]
                    gl2 = gwl[:, k * 8:(k + 1) * 8]
                    for (xo, go) in ((xh2, gh2), (xl2, gh2), (xh2, gl2)):
                        parts.append((px[:, 0:8], xo, go, False, False))
                parts[0] = parts[0][:3] + (True, False)
                parts[-1] = parts[-1][:4] + (True,)
                tsc_ = b.mm_group(parts, waits=[tv, t_gws, b.P(b.ve), b.P(b.ac)])
                r1 = b.v(lambda e: e.tensor_reduce(out=c_t[:, 8:9], in_=px[:, 0:8],
                                                   axis=AX.X, op=ALU.max),
                         waits=[tsc_], dr=True)
                r1b = b.v(lambda e: e.tensor_scalar(out=c_t[:, 8:9],
                                                    in0=c_t[:, 8:9], scalar1=-1.0,
                                                    scalar2=None, op0=ALU.mult))
                r2 = b.a(lambda e: e.activation(out=e_sc[:, :], in_=px[:, 0:8],
                                                func=AF.Exp, bias=c_t[:, 8:9]),
                         waits=[r1b])
                r3 = b.v(lambda e: e.tensor_reduce(out=c_t[:, 9:10], in_=e_sc[:, :],
                                                   axis=AX.X, op=ALU.add),
                         waits=[r2], dr=True)
                r4 = b.v(lambda e: e.reciprocal(out=c_t[:, 10:11],
                                                in_=c_t[:, 9:10]), waits=[r3],
                         dr=True)
                r5 = b.v(lambda e: e.tensor_scalar(out=e_sc[:, :], in0=e_sc[:, :],
                                                   scalar1=c_t[:, 10:11],
                                                   scalar2=None, op0=ALU.mult),
                         waits=[r4], dr=True)
                r6 = b.v(lambda e: e.tensor_reduce(out=c_t[:, 11:12],
                                                   in_=e_sc[:, :], axis=AX.X,
                                                   op=ALU.max), waits=[r5],
                         dr=True)
                r7 = b.v(lambda e: e.tensor_scalar(out=c_t[:, 16:24], in0=e_sc[:, :],
                                                   scalar1=c_t[:, 11:12],
                                                   scalar2=None, op0=ALU.is_ge),
                         waits=[r6], dr=True)
                r8 = b.v(lambda e: e.scalar_tensor_tensor(
                    out=c_t[:, 16:24], in0=c_t[:, 16:24], scalar=-BIG,
                    in1=e_sc[:, :], op0=ALU.mult, op1=ALU.add), waits=[r7],
                         dr=True)
                r9 = b.v(lambda e: e.tensor_reduce(out=c_t[:, 12:13],
                                                   in_=c_t[:, 16:24], axis=AX.X,
                                                   op=ALU.max), waits=[r8],
                         dr=True)
                r10 = b.v(lambda e: e.tensor_scalar(out=c_t[:, 16:24],
                                                    in0=e_sc[:, :],
                                                    scalar1=c_t[:, 12:13],
                                                    scalar2=None, op0=ALU.is_ge),
                          waits=[r9], dr=True)
                b.v(lambda e, tt=tt: e.tensor_tensor(
                    out=e_cw[:, tt * 8:(tt + 1) * 8], in0=e_sc[:, :],
                    in1=c_t[:, 16:24], op=ALU.mult), waits=[r10])
            t_router = b.P(b.ve)
            lastE = {"px": t_router, "ps": t_router}
            ew_used = [None, None]
            ewl_free = [None, None]
            w2_used = None
            w2l_free = None
            h2_used = None
            for ex in range(10):
                if ex < 2:
                    w1src = lambda k, ft, ex=ex: sw1[
                        k * 128:(k + 1) * 128,
                        ex * FR + ft * 128: ex * FR + (ft + 1) * 128]
                    w3src = lambda k, ft, ex=ex: sw3[
                        k * 128:(k + 1) * 128,
                        ex * FR + ft * 128: ex * FR + (ft + 1) * 128]
                    w2src = lambda ft, ex=ex: sw2[
                        ex * FR + ft * 128: ex * FR + (ft + 1) * 128, :]
                    cwcol = None
                else:
                    ei = ex - 2
                    w1src = lambda k, ft, ei=ei: rw1[
                        ei * D + k * 128: ei * D + (k + 1) * 128,
                        ft * 128:(ft + 1) * 128]
                    w3src = lambda k, ft, ei=ei: rw3[
                        ei * D + k * 128: ei * D + (k + 1) * 128,
                        ft * 128:(ft + 1) * 128]
                    w2src = lambda ft, ei=ei: rw2[
                        ei * FR + ft * 128: ei * FR + (ft + 1) * 128, :]
                    cwcol = ex - 2
                nft = FR // 128
                for fg0 in range(0, nft, 2):
                    gsz = min(2, nft - fg0)
                    for sl_ in range(gsz):
                        ft = fg0 + sl_
                        par = ft % 2
                        wfree = [ewl_free[par]] if ewl_free[par] else []
                        for k in range(16):
                            b.dma("sync", e_w1l[par][:, k * 128:(k + 1) * 128],
                                  w1src(k, ft), stream=f"ew{par}",
                                  waits=wfree if k == 0 else ())
                        for k in range(16):
                            b.dma("sync", e_w3l[par][:, k * 128:(k + 1) * 128],
                                  w3src(k, ft), stream=f"ew{par}")
                        t_w = b.P(b.Q[f"ew{par}"])
                        wu = [ew_used[par]] if ew_used[par] else []
                        ts1 = b.split_sg(e_w1l[par][:, :], e_w1h[par][:, :],
                                         e_w1o[par][:, :], waits=[t_w] + wu)
                        ts3 = b.split_sg(e_w3l[par][:, :], e_w3h[par][:, :],
                                         e_w3o[par][:, :], waits=[t_w])
                        ewl_free[par] = ts3
                        for s_ in range(EHALF // 512):
                            xc = s_ * 512
                            p1b, p1n = (pj, "pj") if s_ == 0 else (pj2, "pj2")
                            p3b, p3n = (pg, "pg") if s_ == 0 else (pr, "pr")
                            parts = []
                            for k in range(16):
                                xh2 = e_x2h[:, k * EHALF + xc: k * EHALF + xc + 512]
                                xl2 = e_x2l[:, k * EHALF + xc: k * EHALF + xc + 512]
                                wch = e_w1h[par][:, k * 128:(k + 1) * 128]
                                wcl = e_w1o[par][:, k * 128:(k + 1) * 128]
                                for (wop, xop) in ((wch, xh2), (wch, xl2),
                                                   (wcl, xh2)):
                                    parts.append((p1b[:, 0:512], wop, xop,
                                                  False, False))
                            parts[0] = parts[0][:3] + (True, False)
                            parts[-1] = parts[-1][:4] + (True,)
                            tm1 = b.mm_group(parts, waits=[
                                ts1, t_router, lastE.get(p1n, (None, 0))])
                            parts = []
                            for k in range(16):
                                xh2 = e_x2h[:, k * EHALF + xc: k * EHALF + xc + 512]
                                xl2 = e_x2l[:, k * EHALF + xc: k * EHALF + xc + 512]
                                wch = e_w3h[par][:, k * 128:(k + 1) * 128]
                                wcl = e_w3o[par][:, k * 128:(k + 1) * 128]
                                for (wop, xop) in ((wch, xh2), (wch, xl2),
                                                   (wcl, xh2)):
                                    parts.append((p3b[:, 0:512], wop, xop,
                                                  False, False))
                            parts[0] = parts[0][:3] + (True, False)
                            parts[-1] = parts[-1][:4] + (True,)
                            tm3 = b.mm_group(parts, waits=[
                                ts3, lastE.get(p3n, (None, 0))])
                            ew_used[par] = tm3
                            tsl = b.a(lambda e, p=p1b: e.activation(
                                out=e_t[:, 0:512], in_=p[:, 0:512],
                                func=AF.Silu),
                                waits=[tm1, lastE.get("e_t1", (None, 0))])
                            lastE[p1n] = tsl
                            th2 = b.v(lambda e, p=p3b: e.tensor_tensor(
                                out=e_t[:, 512:1024], in0=e_t[:, 0:512],
                                in1=p[:, 0:512], op=ALU.mult),
                                waits=[tsl, tm3])
                            lastE[p3n] = th2
                            lastE["e_t1"] = th2
                            col = sl_ * EHALF + xc
                            th2s = b.split_sg(e_t[:, 512:1024],
                                              e_h2h[:, col:col + 512],
                                              e_h2l[:, col:col + 512],
                                              waits=[th2] +
                                              ([h2_used] if h2_used else []))
                    t_h2 = th2s
                    w2f = [w2l_free] if w2l_free else []
                    for sl_ in range(gsz):
                        b.dma("sync", e_w2l[:, sl_ * D:(sl_ + 1) * D],
                              w2src(fg0 + sl_), stream="ew2",
                              waits=w2f if sl_ == 0 else ())
                    t_w2 = b.P(b.Q["ew2"])
                    ts2 = b.split_sg(e_w2l[:, 0:gsz * D], e_w2h[:, 0:gsz * D],
                                     e_w2o[:, 0:gsz * D],
                                     waits=[t_w2] + ([w2_used] if w2_used else []))
                    w2l_free = ts2
                    for tt in range(EHALF // 128):
                        for dc in range(4):
                            pb, pn = (px, "px") if (tt * 4 + dc) % 2 == 0 \
                                else (ps, "ps")
                            parts = []
                            for sl_ in range(gsz):
                                hh = e_h2h[:, sl_ * EHALF + tt * 128:
                                           sl_ * EHALF + (tt + 1) * 128]
                                hl = e_h2l[:, sl_ * EHALF + tt * 128:
                                           sl_ * EHALF + (tt + 1) * 128]
                                wh2 = e_w2h[:, sl_ * D + dc * 512:
                                            sl_ * D + (dc + 1) * 512]
                                wl2 = e_w2o[:, sl_ * D + dc * 512:
                                            sl_ * D + (dc + 1) * 512]
                                for (ho, wop) in ((hh, wh2), (hh, wl2),
                                                  (hl, wh2)):
                                    parts.append((pb[:, 0:512], ho, wop,
                                                  False, False))
                            parts[0] = parts[0][:3] + (True, False)
                            parts[-1] = parts[-1][:4] + (True,)
                            tm2 = b.mm_group(parts, waits=[
                                ts2, t_h2, lastE.get(pn, (None, 0))])
                            w2_used = tm2
                            h2_used = tm2
                            acc2 = e_acc[:, tt * D + dc * 512:
                                         tt * D + (dc + 1) * 512]
                            if cwcol is None:
                                tev2 = b.v(lambda e, p=pb, a2=acc2:
                                           e.tensor_tensor(
                                    out=a2, in0=p[:, 0:512], in1=a2,
                                    op=ALU.add), waits=[tm2])
                            else:
                                cwc = e_cw[:, tt * 8 + cwcol: tt * 8 + cwcol + 1]
                                tev2 = b.v(lambda e, p=pb, a2=acc2, cw=cwc:
                                           e.scalar_tensor_tensor(
                                        out=a2, in0=p[:, 0:512], scalar=cw,
                                        in1=a2, op0=ALU.mult, op1=ALU.add),
                                    waits=[tm2])
                            lastE[pn] = tev2
            tfin = b.P(b.ve)
            for tt in range(EHALF // 128):
                b.dma("gpsimd", out[hf * EHALF + tt * 128:
                                    hf * EHALF + (tt + 1) * 128, :],
                      e_acc[:, tt * D:(tt + 1) * D], stream="out",
                      waits=[tfin] if tt == 0 else ())

    stkE.close()

    endw = [b.P(b.pe), b.P(b.ve), b.P(b.ac)] + [b.P(q) for q in b.Q.values()]

    def fin(e, ws=tuple(endw)):
        for s, v in ws:
            if v > 0:
                e.wait_ge(s.h, v)
    b.q["gpsimd"].append(fin)

    with nc.Block() as block:
        @block.sync
        def _(e):
            for f in b.q["sync"]:
                f(e)

        @block.gpsimd
        def _(e):
            for f in b.q["gpsimd"]:
                f(e)

        @block.tensor
        def _(e):
            for f in b.q["tensor"]:
                f(e)

        @block.vector
        def _(e):
            for f in b.q["vector"]:
                f(e)

        @block.scalar
        def _(e):
            for f in b.q["scalar"]:
                f(e)
    stack.close()
    return nc


def make_in_maps(inputs):
    cs = host_constants()
    x = np.asarray(inputs["L0_attn_hidden_states"])
    g = lambda n: np.ascontiguousarray(np.asarray(inputs[n], np.float32))
    conv = np.zeros((128, 192), np.float32)
    for ti, n in enumerate(("conv_q", "conv_k", "conv_v")):
        w = g(f"L0_attn_{n}").reshape(16, 128, 4)
        conv[:, ti * 64:(ti + 1) * 64] = w.transpose(1, 0, 2).reshape(128, 64)
    common = {
        "wq": g("L0_attn_Wq"), "wk": g("L0_attn_Wk"), "wv": g("L0_attn_Wv"),
        "wg": g("L0_attn_Wg"), "wo": g("L0_attn_Wo"), "wf1": g("L0_attn_Wf1"),
        "wf2": g("L0_attn_Wf2"), "wb": g("L0_attn_Wb"), "convw": conv,
        "afm": np.ascontiguousarray(np.broadcast_to(
            g("L0_attn_A_log")[:, None], (16, 128))),
        "dtb": np.ascontiguousarray(np.broadcast_to(
            g("L0_attn_dt_bias")[None, :], (128, 128))),
        "nwbc": np.ascontiguousarray(np.broadcast_to(
            g("L0_attn_norm_w")[None, :], (128, 128))),
        "bgc": np.ascontiguousarray(g("L0_attn_bg").reshape(16, 128).T),
        "gatew": g("L0_ffn_gate_w"),
        "sw1": g("L0_ffn_sw1"), "sw2": g("L0_ffn_sw2"), "sw3": g("L0_ffn_sw3"),
        "rw1": g("L0_ffn_rw1").reshape(NE * D, FR),
        "rw2": g("L0_ffn_rw2").reshape(NE * FR, D),
        "rw3": g("L0_ffn_rw3").reshape(NE * D, FR),
        "cident": cs["cident"], "cuincl": cs["cuincl"], "ce1": cs["ce1"],
        "ce2": cs["ce2"], "conesr": cs["conesr"], "conesc": cs["conesc"],
        "cmaskA": cs["cmaskA"], "cmaskM": cs["cmaskM"],
        "cactf": np.ascontiguousarray(np.broadcast_to(np.array(
            [1e-5, 1e-6 * 128.0, 1e-6, 128.0, 1.0, 1.0 / 128.0, 0.0, 0.0],
            np.float32)[None, :], (128, 8))),
        "ce1c": cs["ce1c"], "cones32": cs["cones32"],
    }
    in_maps = []
    for cid in range(8):
        bi, half = cid // 2, cid % 2
        shard = np.zeros((HALO + TLOC, D), np.float32)
        lo = half * TLOC
        if lo >= HALO:
            shard[:HALO] = x[bi, lo - HALO:lo]
        shard[HALO:] = x[bi, lo:lo + TLOC]
        m = dict(common)
        m["xs"] = shard
        m["cisev"] = np.full((128, 1), 1.0 if half == 0 else 0.0, np.float32)
        m["cisod"] = np.full((128, 1), 1.0 if half == 1 else 0.0, np.float32)
        in_maps.append(m)
    return in_maps


def kernel(**inputs):
    nc = build_program()
    in_maps = make_in_maps(inputs)
    res = run_bass_kernel_spmd(nc, in_maps, core_ids=list(range(8)))
    out = np.empty((4, 4096, D), np.float32)
    for cid in range(8):
        bi, half = cid // 2, cid % 2
        out[bi, half * TLOC:(half + 1) * TLOC] = res.results[cid]["out"]
    return out
